# version 01
# baseline (speedup 1.0000x reference)
import numpy as np
from contextlib import ExitStack
import concourse.bass as bass
import concourse.mybir as mybir
from concourse.bass_utils import run_bass_kernel_spmd

F32 = mybir.dt.float32
BF16 = mybir.dt.bfloat16
I16 = mybir.dt.int16
I8 = mybir.dt.int8
U16 = mybir.dt.uint16
ALU = mybir.AluOpType
ACT = mybir.ActivationFunctionType
AX = mybir.AxisListType

D, T, NCH, H, HD = 1024, 2048, 8, 16, 64
C, NCHUNK, NSLAB, SLAB, CPS = 128, 16, 4, 512, 4
HMOE, E, I3, NRB = 2048, 8, 3072, 24
LN_EPS = 1e-5
ATT_EPS = 1e-5

R_LOGF, R_LGAM, R_S, R_W, R_LAM, R_KSC, R_BET, R_BLAM = range(8)
CS, CW, CLAM, CKSC, CBET, CBLAM = 0, 16, 32, 48, 64, 80


class Builder:
    def __init__(self, dbg=None):
        self.dbg = dbg
        self.nc = bass.Bass("TRN2", target_bir_lowering=False)
        self.es = ExitStack()
        self.dbg_done = False

    def din(self, name, shape, dt=F32):
        return self.nc.dram_tensor(name, list(shape), dt, kind="ExternalInput").ap()

    def dtmp(self, name, shape, dt=F32):
        return self.nc.dram_tensor(name, list(shape), dt).ap()

    def S(self, name, shape, dt=F32):
        return self.es.enter_context(self.nc.sbuf_tensor(name, list(shape), dt))

    def declare(self):
        nc, din, dtmp, S = self.nc, self.din, self.dtmp, self.S
        self.d_x = din("x", [D, T])
        self.d_wq = din("wq", [D, D]); self.d_wk = din("wk", [D, D])
        self.d_wv = din("wv", [D, D]); self.d_wo = din("wo", [D, D])
        self.d_wgam = din("wgam", [D, H]); self.d_wf = din("wf", [D, H])
        self.d_wg1 = din("wg1", [D, HD]); self.d_wg2 = din("wg2", [HD, D])
        self.d_normw = din("normw", [128, NCH])
        self.d_mwg = din("mwg", [D, E])
        self.d_mw1 = din("mw1", [E, 16, D, 128])
        self.d_mb1 = din("mb1", [128, E * 16])
        self.d_mw2 = din("mw2", [E, 8, HMOE, 128])
        self.d_mb2 = din("mb2", [E, D])
        self.d_acol = din("acol", [64, 1])
        self.d_bmat = din("bmat", [64, D])
        self.d_cmat = din("cmat", [D, 64])
        self.d_dvec = din("dvec", [128, NCH])
        self.d_ldt = din("ldt", [1, D])
        self.d_wup = din("wup", [I3, D]); self.d_wgt = din("wgt", [I3, D])
        self.d_wdn = din("wdn", [D, I3])
        self.d_flag2 = din("flag2", [128, 1])
        self.d_y = self.nc.dram_tensor("y", [D, T], F32, kind="ExternalOutput").ap()

        self.d_oloc = dtmp("d_oloc", [NCHUNK, 128, D], BF16)
        self.d_G = dtmp("d_G", [NCHUNK, 64, H * 128], BF16)
        self.d_MT = dtmp("d_MT", [NCHUNK, 64, 1024], BF16)
        self.d_sout = dtmp("d_sout", [1, 64 * 1024])
        self.d_sag = dtmp("d_sag", [2, 64 * 1024])
        self.d_halo = dtmp("d_halo", [1, 128 * 512], BF16)
        self.d_haloag = dtmp("d_haloag", [2, 128 * 512], BF16)
        self.d_gt = dtmp("d_gt", [NRB, 128, T], BF16)
        self.d_g2 = dtmp("d_g2", [NRB, 128, T], BF16)
        self.d_mx = [dtmp(f"d_mx{i}", [1, 8]) for i in range(4)]
        self.d_mxo = [dtmp(f"d_mxo{i}", [1, 8]) for i in range(4)]
        self.d_up = dtmp("d_up", [NRB, 128, T], BF16)
        self.d_opre = dtmp("d_opre", [NRB, 128, T], BF16)
        if self.dbg:
            self.d_dbg = self.nc.dram_tensor("dbgout", [D, T], F32,
                                             kind="ExternalOutput").ap()

        TE_ = T + 64
        self.TE_ = TE_
        self.xs = S("xs", [128, NCH * T], BF16)
        self.xn = S("xn", [128, NCH * TE_], BF16)
        self.wpool = S("wpool", [128, 3 * NCH * 1024], BF16)
        self.qkv = S("qkv", [128, 3 * NCH * SLAB], BF16)
        self.bigb = S("bigb", [128, 8192], BF16)
        self.otm = S("otm", [128, 2 * D], BF16)
        self.rep = S("rep", [128, 4 * SLAB])
        self.cols = S("cols", [128, 96])
        self.pairc64 = S("pairc64", [64, 16])
        self.onesf_t = S("onesf_t", [1, 128])
        self.identf = S("identf", [128, 128])
        self.mrow_t = S("mrow_t", [1, 32])
        self.nrm_t = S("nrm_t", [16, SLAB])
        self.aux2 = S("aux2", [128, 2 * T])
        self.wsmall = S("wsmall", [128, 1024], BF16)
        self.wg2b = S("wg2b", [64, D], BF16)
        self.mb2s = S("mb2s", [E, D], BF16)
        self.mb1s = S("mb1s", [128, E * 16])
        self.consts = S("consts", [128, 704], BF16)
        self.onesb = S("onesb", [128, 1], BF16)
        self.dws = S("dws", [128, 2048], BF16)
        self.state = S("state", [64, 1024])
        self.stateb = S("stateb", [128, 512], BF16)
        self.gcur = S("gcur", [64, H * 128], BF16)
        self.mtc = S("mtc", [64, 1024], BF16)
        self.maxc = S("maxc", [128, 8])
        self.iota8 = S("iota8", [128, 8])
        self.scol = S("scol", [128, 16])
        self.wsc = S("wsc", [128, 32])
        self.flag2 = S("flag2s", [128, 1])
        qf = self.qkv[:].bitcast(F32)
        self.stateR = qf[0:64, 0:1024]
        self.stateRb = self.qkv[0:64, 2048:3072]
        self.wnat = self.qkv[:, 1536:2560]
        self.wi8 = self.qkv[:, 2560:3072].bitcast(I8)
        self.q16st = self.qkv[:, 3328:3840].bitcast(I16)[:, 0:512]
        self.halo = self.rep[:].bitcast(F32)[:, 0:512] if False else None
        self.tsc = None
        self.dwf = None
        self.statrow = self.aux2[0:1, :]
        self.onesf = self.onesf_t[:]
        self.mrow = self.mrow_t[:]
        self.nrm = self.nrm_t[:]
        self.scors = self.otm[0:8, :]
        self.segA = self.aux2[0:16, 0:512]
        self.segB = self.aux2[0:16, 512:1024]
        self.seglogf = self.aux2[0:16, 1024:1536]
        self.seglgam = self.aux2[0:16, 1536:2048]
        self.strip = self.aux2[0:16, 2304:3072]
        self.kern = self.aux2[0:64, 0:1024]
        self.abuf = self.aux2[64:128, 0:1024]
        self.ebuf = self.aux2[0:64, 1024:2048]
        self.srow0 = self.aux2[0:1, 2048:2176]
        self.nsrow0 = self.aux2[0:1, 2176:2304]
        self.wrow0 = self.aux2[0:1, 2304:2432]
        self.tsc = self.aux2[:, 3072:3120]
        self.halo = self.aux2[:, 2560:3072]

        self.pm = [self.es.enter_context(self.nc.psum_tensor(f"pm{i}", [128, 512], F32))
                   for i in range(6)]
        self.pt = [self.es.enter_context(self.nc.psum_tensor(f"pt{i}", [128, 1024], BF16))
                   for i in range(2)]

        class Sem:
            def __init__(s, name):
                s.h = self.es.enter_context(self.nc.semaphore(name))
                s.n = 0
            def inc(s, inst, k=1):
                inst.then_inc(s.h, k); s.n += k; return inst
            def wait(s, eng, n=None):
                eng.wait_ge(s.h, s.n if n is None else n)

        for nm in ("s_ld", "s_lw", "s_lb", "s_w1", "s_w2a", "s_w2b", "s_v", "s_s", "s_t", "s_g", "s_d", "s_cc", "s_st", "s_rlA", "s_rlB", "s_ch", "s_cm0", "s_cm1", "s_cm2", "s_cm3"):
            setattr(self, nm, Sem(nm))

        self.block = self.es.enter_context(self.nc.Block())
        b = self.block
        self.TE, self.VE, self.SE, self.GE, self.SY = b.tensor, b.vector, b.scalar, b.gpsimd, b.sync

    def dump_fm(self, tag, src, stride=T, off=0):
        if self.dbg != tag or self.dbg_done:
            return False
        self.dbg_done = True
        z = self
        def g(gg):
            for ss in (z.s_v, z.s_s, z.s_t, z.s_g, z.s_ld, z.s_lw, z.s_lb, z.s_d, z.s_cc,
                       z.s_w1, z.s_w2a, z.s_w2b, z.s_rlA, z.s_rlB, z.s_ch,
                       z.s_cm0, z.s_cm1, z.s_cm2, z.s_cm3):
                ss.wait(gg)
            for c in range(NCH):
                z.s_st.inc(gg.dma_start(out=z.d_dbg[c * 128:(c + 1) * 128, :],
                                        in_=src[:, c * stride + off:c * stride + off + T]), 16)
            for c in range(NCH):
                z.s_st.inc(gg.dma_start(out=z.d_y[c * 128:(c + 1) * 128, :],
                                        in_=z.xs[:, c * T:(c + 1) * T]), 16)
            z.s_st.wait(gg)
        self.GE(g)
        return True

    def dump_raw(self, tag, src_ap_list):
        if self.dbg != tag or self.dbg_done:
            return False
        self.dbg_done = True
        z = self
        def g(gg):
            for ss in (z.s_v, z.s_s, z.s_t, z.s_g, z.s_ld, z.s_lw, z.s_lb, z.s_d, z.s_cc,
                       z.s_w1, z.s_w2a, z.s_w2b, z.s_rlA, z.s_rlB, z.s_ch,
                       z.s_cm0, z.s_cm1, z.s_cm2, z.s_cm3):
                ss.wait(gg)
            for c, ap in enumerate(src_ap_list[:NCH]):
                z.s_st.inc(gg.dma_start(out=z.d_dbg[c * 128:(c + 1) * 128, 0:ap.shape[-1]],
                                        in_=ap), 16)
            for c in range(NCH):
                z.s_st.inc(gg.dma_start(out=z.d_y[c * 128:(c + 1) * 128, :],
                                        in_=z.xs[:, c * T:(c + 1) * T]), 16)
            z.s_st.wait(gg)
        self.GE(g)
        return True

    def ph_consts_loads(self):
        z = self
        def g0(g):
            g.iota(z.iota8[:], pattern=[[1, 8]], base=0, channel_multiplier=0,
                   allow_small_or_imprecise_dtypes=True)
            z.s_g.inc(g.memset(z.consts[:], 0.0))
            z.s_g.wait(g)
            g.affine_select(out=z.consts[:, 0:128], in_=z.consts[:, 0:128],
                            compare_op=ALU.not_equal, fill=1.0, base=0,
                            pattern=[[-1, 128]], channel_multiplier=1)
            g.memset(z.onesb[:], 1.0)
            g.memset(z.onesf[:], 1.0)
            z.s_g.inc(g.memset(z.consts[:, 128:256], 1.0))
            z.s_g.wait(g)
            g.affine_select(out=z.consts[:, 128:256], in_=z.consts[:, 128:256],
                            compare_op=ALU.is_ge, fill=0.0, base=-1,
                            pattern=[[-1, 128]], channel_multiplier=1)
            z.s_g.inc(g.memset(z.consts[:, 256:384], -1.0))
            z.s_g.wait(g)
            g.affine_select(out=z.consts[:, 256:384], in_=z.consts[:, 256:384],
                            compare_op=ALU.is_ge, fill=0.0, base=-1,
                            pattern=[[1, 128]], channel_multiplier=-1)
            z.s_g.inc(g.memset(z.consts[:, 384:512], 1.0))
            z.s_g.wait(g)
            g.affine_select(out=z.consts[:, 384:512], in_=z.consts[:, 384:512],
                            compare_op=ALU.is_ge, fill=0.0, base=0,
                            pattern=[[-1, 128]], channel_multiplier=1)
            z.s_g.inc(g.memset(z.consts[:, 512:576], 0.0))
            z.s_g.wait(g)
            g.affine_select(out=z.consts[0:64, 512:576], in_=z.consts[0:64, 512:576],
                            compare_op=ALU.not_equal, fill=1.0, base=0,
                            pattern=[[-1, 64]], channel_multiplier=1)
            g.affine_select(out=z.consts[64:128, 512:576],
                            in_=z.consts[64:128, 512:576],
                            compare_op=ALU.not_equal, fill=1.0, base=-64,
                            pattern=[[-1, 64]], channel_multiplier=1)
            z.s_g.inc(g.memset(z.identf[:], 0.0))
            z.s_g.wait(g)
            g.affine_select(out=z.identf[:], in_=z.identf[:], compare_op=ALU.not_equal,
                            fill=1.0, base=0, pattern=[[-1, 128]], channel_multiplier=1)
            for mc in range(NCH):
                off = 576 + mc * 16
                g.affine_select(out=z.consts[0:64, off:off + 16],
                                in_=z.consts[0:64, off:off + 16],
                                compare_op=ALU.not_equal, fill=1.0, base=-(2 * mc),
                                pattern=[[1, 16]], channel_multiplier=0)
                g.affine_select(out=z.consts[64:128, off:off + 16],
                                in_=z.consts[64:128, off:off + 16],
                                compare_op=ALU.not_equal, fill=1.0, base=-(2 * mc + 1),
                                pattern=[[1, 16]], channel_multiplier=0)
            z.s_g.inc(g.drain())
            for c in range(NCH):
                z.s_ld.inc(g.dma_start(out=z.xs[:, c * T:(c + 1) * T],
                                       in_=z.d_x[c * 128:(c + 1) * 128, :]), 16)
            z.x_loaded = z.s_ld.n
            for c in range(NCH):
                z.s_lw.inc(g.dma_start(out=z.wsmall[:, c * 16:(c + 1) * 16],
                                       in_=z.d_wgam[c * 128:(c + 1) * 128, :]), 16)
                z.s_lw.inc(g.dma_start(out=z.wsmall[:, 128 + c * 16:128 + (c + 1) * 16],
                                       in_=z.d_wf[c * 128:(c + 1) * 128, :]), 16)
                z.s_lw.inc(g.dma_start(out=z.wsmall[:, 256 + c * 64:256 + (c + 1) * 64],
                                       in_=z.d_wg1[c * 128:(c + 1) * 128, :]), 16)
            z.s_lw.inc(g.dma_start(out=z.wg2b[:], in_=z.d_wg2), 16)
            z.s_lw.inc(g.dma_start(out=z.flag2[:], in_=z.d_flag2), 16)
            for c in range(NCH):
                z.s_lw.inc(g.dma_start(out=z.wsmall[:, 768 + c * 8:768 + (c + 1) * 8],
                                       in_=z.d_mwg[c * 128:(c + 1) * 128, :]), 16)
            z.s_lw.inc(g.dma_start(out=z.mb2s[:], in_=z.d_mb2), 16)
            z.s_lw.inc(g.dma_start(out=z.scol[:, 0:8], in_=z.d_normw), 16)
            z.s_lw.inc(g.dma_start(out=z.scol[:, 8:16], in_=z.d_dvec), 16)
            z.s_lw.inc(g.dma_start(out=z.mb1s[:], in_=z.d_mb1), 16)
            z.small_loaded = z.s_lw.n
            for wi_, dw in enumerate([z.d_wq, z.d_wk, z.d_wv]):
                for kc in range(NCH):
                    z.s_lb.inc(g.dma_start(
                        out=z.wpool[:, (wi_ * NCH + kc) * 1024:(wi_ * NCH + kc + 1) * 1024],
                        in_=dw[kc * 128:(kc + 1) * 128, :]), 16)
            z.qkvw_loaded = z.s_lb.n
        self.GE(g0)

    def emit_ln(self, src, dst, need_ld, extra_waits=()):
        z = self
        s_v, s_s, s_t, s_d = z.s_v, z.s_s, z.s_t, z.s_d
        sst, sof, dstr, dof = z.src_stride, z.src_off, z.dst_stride, z.dst_off

        def ssl(c, sl):
            return src[:, c * sst + sof + sl * SLAB:c * sst + sof + (sl + 1) * SLAB]

        def dsl(c, sl):
            return dst[:, c * dstr + dof + sl * SLAB:c * dstr + dof + (sl + 1) * SLAB]

        sq_done = []
        s0 = s_s.n

        def se(sc):
            z.s_ld.wait(sc, need_ld)
            for (ss, cnt) in extra_waits:
                ss.wait(sc, cnt)
            for sl in range(NSLAB):
                if sl >= 2:
                    s_t.wait(sc, t_msq_target[sl - 2])
                last = None
                for c in range(NCH):
                    last = sc.activation(z.bigb[:, ((sl % 2) * NCH + c) * SLAB:
                                                 ((sl % 2) * NCH + c + 1) * SLAB],
                                         ssl(c, sl), ACT.Square)
                s_s.inc(last)
                sq_done.append(s_s.n)
        t_msq_target = [None] * NSLAB

        t0 = s_t.n
        for sl in range(NSLAB):
            t_msq_target[sl] = t0 + sl + 1
        self.SE(se)

        v_pair_done = [s_v.n + 1, s_v.n + 2]

        def te(t):
            z.s_ld.wait(t, need_ld)
            for (ss, cnt) in extra_waits:
                ss.wait(t, cnt)
            for sl in range(NSLAB):
                if sl >= 2:
                    s_v.wait(t, v_pair_done[0])
                s_s.wait(t, sq_done[sl])
                for c in range(NCH):
                    t.matmul(z.pm[2 + sl % 2][0:1, :], z.onesb[:], ssl(c, sl),
                             start=(c == 0), stop=(c == NCH - 1))
                last = None
                for c in range(NCH):
                    last = t.matmul(z.pm[4 + sl % 2][0:1, :], z.onesb[:],
                                    z.bigb[:, ((sl % 2) * NCH + c) * SLAB:
                                           ((sl % 2) * NCH + c + 1) * SLAB],
                                    start=(c == 0), stop=(c == NCH - 1))
                s_t.inc(last)
        self.TE(te)
        assert s_t.n == t0 + NSLAB

        def ve(v):
            for pair in range(2):
                sls = [2 * pair, 2 * pair + 1]
                for sl in sls:
                    s_t.wait(v, t_msq_target[sl])
                for sl in sls:
                    mu = z.statrow[:, sl * 1024:sl * 1024 + SLAB]
                    v.tensor_scalar(mu, z.pm[2 + sl % 2][0:1, :], 1.0 / D, None, ALU.mult)
                v.drain()
                bxf = z.rep
                for si, sl in enumerate(sls):
                    x = bxf[0:1, si * 512:(si + 1) * 512]
                    mu = z.statrow[:, sl * 1024:sl * 1024 + SLAB]
                    v.tensor_mul(x, mu, mu)
                for si, sl in enumerate(sls):
                    q = z.statrow[:, sl * 1024 + 512:sl * 1024 + 512 + SLAB]
                    v.tensor_scalar(q, z.pm[4 + sl % 2][0:1, :], 1.0 / D, float(LN_EPS),
                                    ALU.mult, ALU.add)
                v.drain()
                last = None
                for si, sl in enumerate(sls):
                    q = z.statrow[:, sl * 1024 + 512:sl * 1024 + 512 + SLAB]
                    x = bxf[0:1, si * 512:(si + 1) * 512]
                    last = v.tensor_sub(q, q, x)
                s_v.inc(last)
        self.VE(ve)
        assert s_v.n == v_pair_done[1]
        v_var = s_v.n

        def se2(sc):
            s_v.wait(sc, v_var)
            last = None
            for sl in range(NSLAB):
                q = z.statrow[:, sl * 1024 + 512:sl * 1024 + 512 + SLAB]
                last = sc.activation(q, q, ACT.Sqrt)
            s_s.inc(last)
        self.SE(se2)
        s_sqrt = s_s.n

        def ve2(v):
            s_s.wait(v, s_sqrt)
            last = None
            for sl in range(NSLAB):
                q = z.statrow[:, sl * 1024 + 512:sl * 1024 + 512 + SLAB]
                last = v.reciprocal(q, q)
            s_v.inc(last)
        self.VE(ve2)
        v_rstd = s_v.n

        rep_done = []

        def ge(g):
            s_v.wait(g, v_rstd)
            for sl in range(NSLAB):
                if sl >= 1:
                    s_v.wait(g, v_rstd + sl)
                s_d.inc(g.dma_start(
                    out=z.rep[:, 0:SLAB],
                    in_=z.statrow[:, sl * 1024:sl * 1024 + SLAB]
                        .unsqueeze(1).broadcast_to([1, 128, SLAB])), 16)
                s_d.inc(g.dma_start(
                    out=z.rep[:, SLAB:2 * SLAB],
                    in_=z.statrow[:, sl * 1024 + 512:sl * 1024 + 512 + SLAB]
                        .unsqueeze(1).broadcast_to([1, 128, SLAB])), 16)
                rep_done.append(s_d.n)
        self.GE(ge)

        def ve3(v):
            for sl in range(NSLAB):
                s_d.wait(v, rep_done[sl])
                for c in range(NCH):
                    v.tensor_sub(dsl(c, sl), ssl(c, sl), z.rep[:, 0:SLAB])
                v.drain()
                last = None
                for c in range(NCH):
                    last = v.tensor_mul(dsl(c, sl), dsl(c, sl), z.rep[:, SLAB:2 * SLAB])
                s_v.inc(last)
            assert s_v.n == v_rstd + NSLAB
        self.VE(ve3)
        return s_v.n


    def xnsl(self, kc, t0, n):
        return self.xn[:, kc * self.TE_ + 32 + t0:kc * self.TE_ + 32 + t0 + n]

    def wtile(self, w, kc, mc):
        return self.wpool[:, (w * NCH + kc) * 1024 + mc * 128:(w * NCH + kc) * 1024 + (mc + 1) * 128]

    def ph_l0(self, xn_ready):
        z = self
        s_v, s_s, s_t, s_g, s_d = z.s_v, z.s_s, z.s_t, z.s_g, z.s_d
        TEc = z.TE_

        def v0(v):
            v.memset(z.state[:], 0.0)
            s_v.inc(v.memset(z.stateb[:], 0.0))
            v.drain()
        self.VE(v0)
        z.prev_unit_v = s_v.n

        for sl in range(NSLAB):
            self.l0_slab_proj(sl, xn_ready)
            if self.dbg == "khat":
                return
            for ci in range(CPS):
                cc = sl * CPS + ci
                self.l0_chunk(cc, sl)
        dsp = s_d.n

        def gse(g):
            s_v.wait(g)
            s_d.inc(g.dma_start(out=z.d_sout[0:1, :],
                                in_=z.state[:]), 16)
            s_d.wait(g, dsp + 16)
            z.s_cc.inc(g.collective_compute(
                "AllGather", ALU.bypass,
                replica_groups=[[0, 1], [2, 3], [4, 5], [6, 7]],
                ins=[z.d_sout.opt()], outs=[z.d_sag.opt()]))
        self.GE(gse)

    def l0_slab_proj(self, sl, xn_ready):
        z = self
        s_v, s_s, s_t, s_g, s_d = z.s_v, z.s_s, z.s_t, z.s_g, z.s_d

        t0 = s_t.n
        n_tiles = 3 * NCH
        ev_base = s_s.n
        nev_groups = n_tiles // 4

        def te(t):
            z.s_ld.wait(t, xn_ready and z.x_loaded)
            s_v.wait(t, xn_ready)
            z.s_lb.wait(t)
            z.s_lw.wait(t)
            idx = 0
            for w in range(3):
                for mc in range(NCH):
                    bank = z.pm[idx % 4]
                    if idx >= 4:
                        s_s.wait(t, ev_base + (idx - 4) // 4 + 1)
                    for kc in range(NCH):
                        last = t.matmul(bank[:, :], z.wtile(w, kc, mc),
                                        z.xnsl(kc, sl * SLAB, SLAB),
                                        start=(kc == 0), stop=(kc == NCH - 1))
                    if idx % 4 == 3:
                        s_t.inc(last)
                    idx += 1
            for kc in range(NCH):
                t.matmul(z.pm[4][0:16, :], z.wsmall[:, 128 + kc * 16:128 + (kc + 1) * 16],
                         z.xnsl(kc, sl * SLAB, SLAB), start=(kc == 0), stop=(kc == NCH - 1))
            for kc in range(NCH):
                last = t.matmul(z.pm[5][0:16, :], z.wsmall[:, kc * 16:(kc + 1) * 16],
                                z.xnsl(kc, sl * SLAB, SLAB), start=(kc == 0), stop=(kc == NCH - 1))
            s_t.inc(last)
        self.TE(te)
        t_qkv = [t0 + 1 + i for i in range(nev_groups)]
        t_fg = t0 + nev_groups + 1

        def se(sc):
            idx = 0
            for w in range(3):
                for mc in range(NCH):
                    if idx % 4 == 0:
                        s_t.wait(sc, t_qkv[idx // 4])
                    bank = z.pm[idx % 4]
                    last = sc.activation(z.qkv[:, w * 4096 + mc * SLAB:w * 4096 + (mc + 1) * SLAB],
                                         bank[:, :], ACT.Silu)
                    if idx % 4 == 3:
                        s_s.inc(sc.drain())
                    idx += 1
        self.SE(se)
        s_qkv_done = s_s.n

        def se2(sc):
            s_t.wait(sc, t_fg)
            sc.activation(z.segA[:], z.pm[4][0:16, :], ACT.Abs)
            sc.activation(z.segB[:], z.pm[5][0:16, :], ACT.Abs)
            sc.drain()
            sc.activation(z.segA[:], z.segA[:], ACT.Exp, scale=-1.0)
            sc.activation(z.segB[:], z.segB[:], ACT.Exp, scale=-1.0)
            sc.drain()
            sc.activation(z.segA[:], z.segA[:], ACT.Ln, bias=1.0)
            last = sc.activation(z.segB[:], z.segB[:], ACT.Ln, bias=1.0)
            s_s.inc(sc.drain())
        self.SE(se2)
        s_lnchain = s_s.n

        def ve2(v):
            s_t.wait(v, t_fg)
            v.tensor_scalar(z.seglogf[:], z.pm[4][0:16, :], 0.0, None, ALU.min)
            v.tensor_scalar(z.seglgam[:], z.pm[5][0:16, :], 0.0, None, ALU.min)
            s_s.wait(v, s_lnchain)
            v.drain()
            v.tensor_sub(z.seglogf[:], z.seglogf[:], z.segA[:])
            last = v.tensor_sub(z.seglgam[:], z.seglgam[:], z.segB[:])
            s_v.inc(v.drain())
        self.VE(ve2)
        z.rows_done = s_v.n

        def se3(sc):
            s_s.wait(sc, s_qkv_done)
            last = None
            for mc in range(NCH):
                last = sc.activation(z.bigb[:, mc * SLAB:(mc + 1) * SLAB],
                                     z.qkv[:, 4096 + mc * SLAB:4096 + (mc + 1) * SLAB],
                                     ACT.Square)
            s_s.inc(sc.drain())
        self.SE(se3)
        s_ksq = s_s.n

        def te2(t):
            s_s.wait(t, s_ksq)
            s_v.wait(t, z.rows_done)
            for mc in range(NCH):
                last = t.matmul(z.pm[4][0:16, :], z.consts[:, 576 + mc * 16:576 + (mc + 1) * 16],
                                z.bigb[:, mc * SLAB:(mc + 1) * SLAB],
                                start=(mc == 0), stop=(mc == NCH - 1))
            s_t.inc(last)
        self.TE(te2)
        t_n2 = s_t.n

        def se4(sc):
            s_t.wait(sc, t_n2)
            s_s.inc(sc.activation(z.nrm[:], z.pm[4][0:16, :], ACT.Sqrt))
        self.SE(se4)
        s_nrm = s_s.n

        def ve3(v):
            s_s.wait(v, s_nrm)
            v.tensor_scalar(z.nrm[:], z.nrm[:], 1e-12, None, ALU.max)
            v.drain()
            last = v.reciprocal(z.nrm[:], z.nrm[:])
            s_v.inc(v.drain())
        self.VE(ve3)
        v_nrm = s_v.n

        repd = []

        def ge(g):
            s_v.wait(g, v_nrm)
            for mc in range(NCH):
                if mc >= 1:
                    s_v.wait(g, v_nrm + mc)
                s_d.inc(g.dma_start(
                    out=z.rep[:, 0:SLAB],
                    in_=z.nrm[2 * mc:2 * mc + 2, :].unsqueeze(1).broadcast_to([2, 64, SLAB])), 16)
                repd.append(s_d.n)
        self.GE(ge)

        def ve4(v):
            for mc in range(NCH):
                s_d.wait(v, repd[mc])
                kt = z.qkv[:, 4096 + mc * SLAB:4096 + (mc + 1) * SLAB]
                s_v.inc(v.tensor_mul(kt, kt, z.rep[:, 0:SLAB]))
                v.drain()
        self.VE(ve4)
        z.khat_done = s_v.n

    def l0_chunk(self, cc, sl):
        z = self
        s_v, s_s, s_t, s_g, s_d = z.s_v, z.s_s, z.s_t, z.s_g, z.s_d
        ci = cc - sl * CPS
        win = (ci * 128, ci * 128 + 128)

        def seg(ap):
            return ap[:, win[0]:win[1]]

        def sp(i):
            return z.strip[:, i * 128:(i + 1) * 128]

        def ve(v):
            s_v.wait(v, max(z.rows_done, z.khat_done))
            v.tensor_tensor_scan(sp(0), seg(z.seglogf), seg(z.seglogf), 0.0,
                                 ALU.add, ALU.bypass)
            v.tensor_add(sp(1), seg(z.seglogf), seg(z.seglgam))
            v.drain()
            v.tensor_add(sp(2), sp(1), sp(0))
            v.drain()
            v.tensor_scalar(sp(5), sp(0), sp(0)[:, 127:128], None, ALU.subtract)
            s_v.inc(v.drain())
        self.VE(ve)
        v_r2 = s_v.n

        def se(sc):
            s_v.wait(sc, v_r2)
            sc.activation(sp(4), sp(0), ACT.Exp)
            sc.activation(sp(3), sp(1), ACT.Exp)
            sc.activation(sp(5), sp(5), ACT.Exp, scale=-1.0)
            s_s.inc(sc.drain())
        self.SE(se)
        s_r = s_s.n

        def ve2(v):
            s_s.wait(v, s_r)
            v.tensor_scalar(sp(3), sp(3), -1.0, None, ALU.mult)
            v.drain()
            v.tensor_mul(sp(1), sp(3), sp(4))
            s_v.inc(v.drain())
        self.VE(ve2)
        v_r3 = s_v.n

        v_rows = s_v.n

        def tc0(t):
            s_v.wait(t, v_rows)
            t.transpose(z.pm[4][0:1, 480:496], sp(4)[:, 127:128], z.identf[0:16, 0:16])
            s_t.inc(t.drain())
        self.TE(tc0)
        t_lamc = s_t.n

        def vc0(v):
            s_t.wait(v, t_lamc)
            s_v.inc(v.tensor_copy(z.mrow[:, 0:16], z.pm[4][0:1, 480:496]))
            v.drain()
        self.VE(vc0)
        v_lamc = s_v.n

        def gc0(g):
            s_v.wait(g, v_lamc)
            s_d.inc(g.dma_start(out=z.pairc64[:],
                                in_=z.mrow[:, 0:16].unsqueeze(1).broadcast_to([1, 64, 16])), 16)
        self.GE(gc0)
        d_pairc = s_d.n

        def te(t):
            s_v.wait(t, max(v_rows, z.prev_unit_v))
            for slot, piece in enumerate([0, 2, 4, 5, 3, 1]):
                t.transpose(z.pm[5][:, slot * 16:(slot + 1) * 16], sp(piece),
                            z.identf[0:16, 0:16])
            s_t.inc(t.drain())
        self.TE(te)
        t_cols = s_t.n

        def vec(v):
            s_t.wait(v, t_cols)
            s_d.wait(v, d_pairc)
            s_v.inc(v.tensor_copy(z.cols[:], z.pm[5][:, 0:96]))
            v.drain()
        self.VE(vec)
        v_cols = s_v.n

        for h in range(H):
            self.l0_unit(cc, sl, h, v_cols)

        def gsp(g):
            s_v.wait(g)
            s_d.inc(g.dma_start(out=z.d_oloc[cc], in_=z.otm[:, (cc % 2) * D:(cc % 2 + 1) * D]), 16)
            s_d.inc(g.dma_start(out=z.d_G[cc], in_=z.gcur[:]), 16)
            s_d.inc(g.dma_start(out=z.d_MT[cc], in_=z.mtc[:]), 16)
        self.GE(gsp)
        z.l0_spill_d = s_d.n

    def l0_unit(self, cc, sl, h, v_cols):
        z = self
        s_v, s_s, s_t, s_g, s_d = z.s_v, z.s_s, z.s_t, z.s_g, z.s_d
        ci = cc - sl * CPS
        pb = 64 * (h % 2)
        mc = h // 2
        base = mc * SLAB + ci * 128
        q_fm = z.qkv[pb:pb + 64, 0 * 4096 + base:0 * 4096 + base + 128]
        k_fm = z.qkv[pb:pb + 64, 1 * 4096 + base:1 * 4096 + base + 128]
        v_fm = z.qkv[pb:pb + 64, 2 * 4096 + base:2 * 4096 + base + 128]
        c_s = z.cols[:, 0 * 16 + h:0 * 16 + h + 1]
        c_w = z.cols[:, 1 * 16 + h:1 * 16 + h + 1]
        c_lam = z.cols[:, 2 * 16 + h:2 * 16 + h + 1]
        c_ksc = z.cols[:, 3 * 16 + h:3 * 16 + h + 1]
        c_bet = z.cols[:, 4 * 16 + h:4 * 16 + h + 1]
        c_blam = z.cols[:, 5 * 16 + h:5 * 16 + h + 1]
        lamC_col = z.pairc64[:, h:h + 1]
        S0f = z.state[0:64, h * 64:(h + 1) * 64]
        S0b = z.stateb[pb:pb + 64, mc * 64:(mc + 1) * 64]
        A = z.dws[:, 0:128]; AT = z.dws[:, 128:256]; X = z.dws[:, 256:384]
        QKmT = z.dws[:, 384:512]; DS = z.dws[:, 512:640]
        VTM = z.dws[:, 640:704]; KTK = z.dws[:, 704:768]
        LAMQ = z.dws[:, 768:832]
        msl = z.consts[:, 128:256]
        msuN = z.consts[:, 256:384]
        mli = z.consts[:, 384:512]

        DS2 = z.dws[:, 896:1024]
        QS = z.dws[:, 832:896]

        def t1(t):
            s_v.wait(t, max(v_cols, z.prev_unit_v))
            t.transpose(z.pm[5][0:1, 96:224], c_s, z.identf[:])
            t.transpose(z.pm[5][0:1, 224:352], c_w, z.identf[:])
            s_t.inc(t.drain())
        self.TE(t1)
        t_rows = s_t.n

        def v1(v):
            s_t.wait(v, t_rows)
            v.tensor_copy(z.srow0[:], z.pm[5][0:1, 96:224])
            v.tensor_scalar(z.nsrow0[:], z.pm[5][0:1, 96:224], -1.0, None, ALU.mult)
            v.tensor_copy(z.wrow0[:], z.pm[5][0:1, 224:352])
            s_v.inc(v.drain())
        self.VE(v1)
        v_rows0 = s_v.n

        def t2(t):
            s_v.wait(t, v_rows0)
            t.matmul(z.pm[0][:, 0:128], k_fm, k_fm, start=True, stop=True)
            t.matmul(z.pm[1][:, 0:128], q_fm, k_fm, start=True, stop=True)
            t.matmul(z.pm[2][:, 0:128], z.srow0[:], z.onesf[:], start=True, stop=False)
            t.matmul(z.pm[2][:, 0:128], z.onesf[:], z.nsrow0[:], start=False, stop=True)
            t.matmul(z.pm[3][:, 0:128], z.nsrow0[:], z.onesf[:], start=True, stop=False)
            t.matmul(z.pm[3][:, 0:128], z.onesf[:], z.wrow0[:], start=False, stop=True)
            ident64 = z.consts[pb:pb + 64, pb:pb + 64]
            t.transpose(z.pt[0][:, 0:64], k_fm, ident64)
            t.transpose(z.pt[0][:, 64:128], v_fm, ident64)
            t.transpose(z.pt[0][:, 128:192], q_fm, ident64)
            s_t.inc(t.drain())
        self.TE(t2)
        t_mm1 = s_t.n

        def v2(v):
            s_t.wait(v, t_mm1)
            v.tensor_scalar(z.pm[2][:, 0:128], z.pm[2][:, 0:128], 0.0, None, ALU.min)
            v.tensor_scalar(z.pm[3][:, 0:128], z.pm[3][:, 0:128], 0.0, None, ALU.min)
            s_v.inc(v.drain())
        self.VE(v2)
        v_clamp = s_v.n

        def s1(sc):
            s_v.wait(sc, v_clamp)
            sc.activation(DS, z.pm[2][:, 0:128], ACT.Exp)
            s_s.inc(sc.drain())
        self.SE(s1)
        s_expA = s_s.n

        def v3(v):
            s_s.wait(v, s_expA)
            v.scalar_tensor_tensor(A, DS, c_bet, msl, ALU.mult, ALU.mult)
            v.tensor_tensor(DS2, DS, mli, ALU.mult)
            v.drain()
            v.tensor_tensor(A, A, z.pm[0][:, 0:128], ALU.mult)
            v.tensor_tensor(DS2, DS2, z.pm[1][:, 0:128], ALU.mult)
            s_v.inc(v.drain())
        self.VE(v3)
        v_A = s_v.n

        def s2(sc):
            s_v.wait(sc, v_A)
            sc.activation(DS, z.pm[3][:, 0:128], ACT.Exp)
            s_s.inc(sc.drain())
        self.SE(s2)
        s_expB = s_s.n

        def v4(v):
            s_s.wait(v, s_expB)
            v.tensor_tensor(AT, DS, msuN, ALU.mult)
            v.drain()
            s_v.inc(v.tensor_tensor(AT, AT, z.pm[0][:, 0:128], ALU.mult))
            v.drain()
        self.VE(v4)
        v_AT = s_v.n

        def t3(t):
            s_v.wait(t, v_A)
            s_s.wait(t, s_expB)
            t.transpose(z.pt[1][:, 0:128], DS2, z.consts[:, 0:128])
            t.matmul(z.pm[2][:, 256:320], k_fm, S0b, start=True, stop=True)
            t.matmul(z.pm[3][:, 256:320], q_fm, S0b, start=True, stop=True)
            s_t.inc(t.drain())
        self.TE(t3)
        t_mm2 = s_t.n

        def v5(v):
            s_t.wait(v, t_mm2)
            v.tensor_copy(VTM, z.pt[0][:, 64:128])
            v.tensor_scalar(KTK, z.pt[0][:, 0:64], c_ksc, None, ALU.mult)
            v.tensor_scalar(X[:, 64:128], z.pt[0][:, 0:64], c_blam, None, ALU.mult)
            v.tensor_scalar(LAMQ, z.pt[0][:, 128:192], c_lam, None, ALU.mult)
            v.tensor_copy(QS, z.pm[3][:, 256:320])
            v.drain()
            v.scalar_tensor_tensor(X[:, 0:64], z.pm[2][:, 256:320], c_blam, VTM,
                                   ALU.mult, ALU.add)
            s_v.inc(v.drain())
        self.VE(v5)
        v_X = s_v.n

        def s3(sc):
            s_v.wait(sc, v_AT)
            s_t.wait(sc, t_mm2)
            sc.activation(QKmT, z.pt[1][:, 0:128], ACT.Copy)
            s_s.inc(sc.drain())
        self.SE(s3)
        s_qkmt = s_s.n

        tprev = [s_t.n]
        vprev = [max(v_X, v_AT)]
        sprev = [s_qkmt]
        for j in range(7):
            lastlevel = (j == 6)

            def tl(t, last=lastlevel, _v=vprev[0], _s=sprev[0]):
                s_v.wait(t, _v)
                s_s.wait(t, _s)
                t.matmul(z.pm[0][:, 128:256], AT, X, start=True, stop=True)
                if not last:
                    t.matmul(z.pm[1][:, 128:256], AT, A, start=True, stop=True)
                    t.matmul(z.pm[2][:, 128:256], A, AT, start=True, stop=True)
                s_t.inc(t.drain())
            self.TE(tl)
            tprev[0] = s_t.n

            def vl(v, last=lastlevel, _t=tprev[0]):
                s_t.wait(v, _t)
                v.tensor_tensor(X, X, z.pm[0][:, 128:256], ALU.add)
                if not last:
                    v.tensor_copy(A, z.pm[1][:, 128:256])
                s_v.inc(v.drain())
            self.VE(vl)
            vprev[0] = s_v.n

            if not lastlevel:
                def sl_(sc, _t=tprev[0]):
                    s_t.wait(sc, _t)
                    sc.activation(AT, z.pm[2][:, 128:256], ACT.Copy)
                    s_s.inc(sc.drain())
                self.SE(sl_)
                sprev[0] = s_s.n
        v_solve = vprev[0]
        s_solve = sprev[0]

        def t4(t):
            s_v.wait(t, v_solve)
            s_s.wait(t, s_solve)
            t.matmul(z.pm[0][:, 256:320], QKmT, X[:, 0:64], start=True, stop=True)
            t.matmul(z.pm[1][0:64, 256:320], KTK, X[:, 0:64], start=True, stop=True)
            t.matmul(z.pm[4][0:64, 256:320], X[:, 64:128], KTK, start=True, stop=True)
            t.matmul(z.pm[5][0:64, 352:480], X[:, 64:128], QKmT, start=True, stop=False)
            t.matmul(z.pm[5][0:64, 352:480], LAMQ, z.consts[:, 0:128], start=False, stop=True)
            s_t.inc(t.drain())
        self.TE(t4)
        t_out = s_t.n

        def v6(v):
            s_t.wait(v, t_out)
            otile = z.otm[:, (cc % 2) * D + h * 64:(cc % 2) * D + (h + 1) * 64]
            v.scalar_tensor_tensor(otile, QS, c_lam, z.pm[0][:, 256:320],
                                   ALU.mult, ALU.add)
            v.scalar_tensor_tensor(S0f, S0f, lamC_col, z.pm[1][0:64, 256:320],
                                   ALU.mult, ALU.add)
            v.scalar_tensor_tensor(z.mtc[0:64, h * 64:(h + 1) * 64],
                                   z.consts[0:64, 512:576],
                                   lamC_col, z.pm[4][0:64, 256:320],
                                   ALU.mult, ALU.add)
            v.tensor_copy(z.gcur[0:64, h * 128:(h + 1) * 128], z.pm[5][0:64, 352:480])
            v.drain()
            s_v.inc(v.tensor_copy(S0b, S0f))
            v.drain()
        self.VE(v6)
        z.prev_unit_v = s_v.n


    def ph_moe(self):
        z = self
        s_v, s_s, s_t, s_g, s_d = z.s_v, z.s_s, z.s_t, z.s_g, z.s_d
        repb = z.rep[:].bitcast(BF16)
        stg = lambda n: repb[:, n * SLAB:(n + 1) * SLAB]
        srep = lambda sl_: repb[:, 2048 + sl_ * SLAB:2048 + (sl_ + 1) * SLAB]

        v1n = {}
        v3n = {}
        v_start = z.s_v.n
        s_start = z.s_s.n
        spill_gate = z.s_d.n
        for tt in range(16):
            def tr(t, tt=tt, gate=(v1n.get(tt - 2, v_start))):
                s_v.wait(t, gate)
                s_s.wait(t, s_start)
                bank = z.pm[4 + tt % 2]
                for kc in range(NCH):
                    last = t.matmul(bank[:, 0:8],
                                    z.xn[:, kc * z.TE_ + 32 + tt * 128:
                                         kc * z.TE_ + 32 + (tt + 1) * 128],
                                    z.wsmall[:, 768 + kc * 8:768 + (kc + 1) * 8],
                                    start=(kc == 0), stop=(kc == NCH - 1))
                s_t.inc(last)
            self.TE(tr)
            t_log = s_t.n

            def v1(v, tt=tt, t_log=t_log):
                s_t.wait(v, t_log)
                L = z.tsc[:, 0:8]; M1 = z.tsc[:, 8:9]; I1 = z.tsc[:, 9:10]
                T8 = z.tsc[:, 32:40]; OH1 = z.tsc[:, 16:24]
                v.tensor_copy(L, z.pm[4 + tt % 2][:, 0:8])
                v.drain()
                v.tensor_reduce(M1, L, AX.X, ALU.max)
                v.drain()
                v.tensor_scalar(OH1, L, M1, None, ALU.is_ge)
                v.drain()
                v.tensor_scalar(T8, OH1, -1000.0, 1000.0, ALU.mult, ALU.add)
                v.drain()
                v.tensor_add(T8, T8, z.iota8[:])
                v.drain()
                v.tensor_reduce(I1, T8, AX.X, ALU.min)
                v.drain()
                v.tensor_scalar(OH1, z.iota8[:], I1, None, ALU.is_equal)
                v.drain()
                v.scalar_tensor_tensor(T8, OH1, -1e30, L, ALU.mult, ALU.add)
                v.drain()
                v.tensor_reduce(z.tsc[:, 10:11], T8, AX.X, ALU.max)
                v.drain()
                v.tensor_scalar(z.tsc[:, 24:32], T8, z.tsc[:, 10:11], None, ALU.is_ge)
                v.drain()
                v.tensor_scalar(T8, z.tsc[:, 24:32], -1000.0, 1000.0, ALU.mult, ALU.add)
                v.drain()
                v.tensor_add(T8, T8, z.iota8[:])
                v.drain()
                v.tensor_reduce(z.tsc[:, 11:12], T8, AX.X, ALU.min)
                v.drain()
                v.tensor_scalar(z.tsc[:, 24:32], z.iota8[:], z.tsc[:, 11:12], None, ALU.is_equal)
                v.tensor_sub(z.tsc[:, 12:13], z.tsc[:, 10:11], M1)
                s_v.inc(v.drain())
            self.VE(v1)
            v_pre = s_v.n
            v1n[tt] = v_pre

            def s1(sc, v_pre=v_pre):
                s_v.wait(sc, v_pre)
                sc.activation(z.tsc[:, 12:13], z.tsc[:, 12:13], ACT.Exp)
                s_s.inc(sc.drain())
            self.SE(s1)
            s_exp = s_s.n

            def v2(v, s_exp=s_exp):
                s_s.wait(v, s_exp)
                S1 = z.tsc[:, 13:14]; S2 = z.tsc[:, 14:15]; E2 = z.tsc[:, 12:13]
                v.tensor_scalar(S1, E2, 1.0, None, ALU.add)
                v.drain()
                v.reciprocal(S1, S1)
                v.drain()
                v.tensor_mul(S2, S1, E2)
                v.drain()
                v.tensor_scalar(z.tsc[:, 40:48], z.tsc[:, 24:32], S2, None, ALU.mult)
                v.drain()
                v.scalar_tensor_tensor(z.tsc[:, 40:48], z.tsc[:, 16:24], S1,
                                       z.tsc[:, 40:48], ALU.mult, ALU.add)
                s_v.inc(v.drain())
            self.VE(v2)
            v_sct = s_v.n

            def t2(t, v_sct=v_sct, pv3=v3n.get(tt - 1, v_start)):
                s_v.wait(t, max(v_sct, pv3))
                t.transpose(z.pm[0][0:8, 384:512], z.tsc[:, 40:48], z.identf[:])
                s_t.inc(t.drain())
            self.TE(t2)
            t_tr = s_t.n

            def v3(v, tt=tt, t_tr=t_tr):
                s_t.wait(v, t_tr)
                if tt == 0:
                    s_d.wait(v, spill_gate)
                s_v.inc(v.tensor_copy(z.scors[:, tt * 128:(tt + 1) * 128],
                                      z.pm[0][0:8, 384:512]))
                v.drain()
            self.VE(v3)
            v3n[tt] = z.s_v.n
        v_scores = z.s_v.n

        for mc in range(NCH):
            for n in range(NSLAB):
                prev_v = z.s_v.n

                def tb(t, mc=mc, n=n, prev_v=prev_v):
                    s_v.wait(t, prev_v)
                    t.matmul(z.pm[1][:, 0:512], z.mb2s[:, mc * 128:(mc + 1) * 128],
                             z.scors[:, n * SLAB:(n + 1) * SLAB], start=True, stop=True)
                    s_t.inc(t.drain())
                self.TE(tb)
                tb_n = z.s_t.n

                def vb(v, mc=mc, n=n, tb_n=tb_n):
                    s_t.wait(v, tb_n)
                    xt_ = z.xs[:, mc * T + n * SLAB:mc * T + (n + 1) * SLAB]
                    s_v.inc(v.tensor_add(xt_, xt_, z.pm[1][:, 0:512]))
                    v.drain()
                self.VE(vb)
        v_b2 = z.s_v.n

        def Hc(m):
            if m < 12:
                return z.qkv[:, m * 1024:(m + 1) * 1024]
            return z.bigb[:, (m - 12) * 1024:(m - 11) * 1024]

        w1buf = z.wpool[:, 0:16384]
        w2buf = [z.wpool[:, 16384 + i_ * 2048:16384 + (i_ + 1) * 2048] for i_ in range(2)]

        h_done_t = 0
        ty_prev = [0, 0]
        for e in range(E):
            prev_hv = z.s_v.n

            def gsr(g, e=e, prev_hv=prev_hv):
                s_v.wait(g, prev_hv)
                if e == 0:
                    s_d.wait(g, spill_gate)
                for sl_ in range(NSLAB):
                    s_d.inc(g.dma_start(
                        out=srep(sl_),
                        in_=z.scors[e:e + 1, sl_ * SLAB:(sl_ + 1) * SLAB]
                            .unsqueeze(1).broadcast_to([1, 128, SLAB])), 16)
            self.GE(gsr)
            d_srep = z.s_d.n

            def gw1(g, e=e, gate=h_done_t):
                if gate:
                    z.s_t.wait(g, gate)
                for m in range(16):
                    z.s_w1.inc(g.dma_start(
                        out=w1buf[:, m * 1024:(m + 1) * 1024].rearrange(
                            "p (kc f) -> p kc f", kc=NCH),
                        in_=z.d_mw1[e, m].rearrange("(kc p) f -> kc p f", p=128)
                            .transpose([1, 0, 2])), 16)
            self.GE(gw1)
            w1_ld = z.s_w1.n

            for half in range(2):
                ns = [2 * half, 2 * half + 1]
                sh_prev = z.s_s.n
                for m in range(16):
                    def th(t, e=e, m=m, w1_ld=w1_ld, sh_prev=sh_prev, vw=z.s_v.n,
                           ns=ns):
                        z.s_w1.wait(t, w1_ld)
                        s_s.wait(t, sh_prev)
                        if m == 0:
                            s_v.wait(t, vw)
                        for ni, n in enumerate(ns):
                            for kc in range(NCH):
                                last = t.matmul(
                                    z.pm[ni][:, 0:512],
                                    w1buf[:, m * 1024 + kc * 128:m * 1024 + (kc + 1) * 128],
                                    z.xn[:, kc * z.TE_ + 32 + n * SLAB:
                                         kc * z.TE_ + 32 + (n + 1) * SLAB],
                                    start=(kc == 0), stop=(kc == NCH - 1))
                        s_t.inc(last)
                    self.TE(th)
                    th_n = z.s_t.n

                    def sh(sc, e=e, m=m, th_n=th_n, ns=ns):
                        s_t.wait(sc, th_n)
                        for ni, n in enumerate(ns):
                            sc.activation(stg(ni), z.pm[ni][:, 0:512], ACT.Relu,
                                          bias=z.mb1s[:, e * 16 + m:e * 16 + m + 1])
                        s_s.inc(sc.drain())
                    self.SE(sh)
                    sh_prev = z.s_s.n
                    sh_n = sh_prev

                    def vh(v, e=e, m=m, sh_n=sh_n, d_srep=d_srep, ns=ns):
                        s_s.wait(v, sh_n)
                        s_d.wait(v, d_srep)
                        for ni, n in enumerate(ns):
                            v.tensor_mul(Hc(m)[:, ni * SLAB:(ni + 1) * SLAB],
                                         stg(ni), srep(n))
                        s_v.inc(v.drain())
                    self.VE(vh)
                h_done_t = z.s_t.n
                h_v_done = z.s_v.n

                y_mm_n = {}
                yv_prev = [z.s_v.n]
                for ym in range(8):
                    wsem = z.s_w2a if ym % 2 == 0 else z.s_w2b

                    def gw2(g, e=e, ym=ym, wsem=wsem, gate=ty_prev[ym % 2]):
                        if gate:
                            z.s_t.wait(g, gate)
                        wsem.inc(g.dma_start(
                            out=w2buf[ym % 2].rearrange("p (kc f) -> p kc f", kc=16),
                            in_=z.d_mw2[e, ym].rearrange("(kc p) f -> kc p f", p=128)
                                .transpose([1, 0, 2])), 16)
                    self.GE(gw2)
                    w2_ld = wsem.n

                    def ty(t, e=e, ym=ym, wsem=wsem, w2_ld=w2_ld, hv=h_v_done,
                           yv=yv_prev[0], ns=ns):
                        wsem.wait(t, w2_ld)
                        s_v.wait(t, max(hv, yv))
                        for ni, n in enumerate(ns):
                            for k in range(16):
                                last = t.matmul(z.pm[2 + ni][:, 0:512],
                                                w2buf[ym % 2][:, k * 128:(k + 1) * 128],
                                                Hc(k)[:, ni * SLAB:(ni + 1) * SLAB],
                                                start=(k == 0), stop=(k == 15))
                            s_t.inc(last)
                            y_mm_n[(ym, ni)] = s_t.n
                    self.TE(ty)
                    ty_prev[ym % 2] = z.s_t.n

                    for ni, n in enumerate(ns):
                        def vy(v, e=e, ym=ym, ni=ni, n=n, key=(ym, ni)):
                            s_t.wait(v, y_mm_n[key])
                            xt_ = z.xs[:, ym * T + n * SLAB:ym * T + (n + 1) * SLAB]
                            s_v.inc(v.tensor_add(xt_, xt_, z.pm[2 + ni][:, 0:512]))
                            v.drain()
                        self.VE(vy)
                    yv_prev[0] = z.s_v.n
        z.moe_done_v = z.s_v.n
        z.moe_done_t = z.s_t.n

    def ph_attn_fin(self):
        z = self
        s_v, s_s, s_t, s_d = z.s_v, z.s_s, z.s_t, z.s_d
        qb = z.qkv
        qf = z.qkv[:].bitcast(F32)
        g1b = qb[0:64, 0:2048]
        Rb = [qb[0:64, 2048:3072], qb[0:64, 3072:4096]]
        gbuf = [qb[0:64, 4096:6144], qb[0:64, 6144:8192]]
        mtbuf = [qb[0:64, 8192:9216], qb[0:64, 9216:10240]]
        stateR = qf[0:64, 5120:6144]
        olbuf = [z.otm[:, 0:1024], z.otm[:, 1024:2048]]
        gatebuf = [z.bigb[:, 0:1024], z.bigb[:, 1024:2048]]
        on_fm = z.bigb[:, 4096:8192]
        sqb = z.rep[:, 0:1024]
        smu = z.aux2[:, 3200:3201]; ssq = z.aux2[:, 3201:3202]
        mu = z.aux2[:, 3202:3203]; mu2 = z.aux2[:, 3203:3204]
        vr = z.aux2[:, 3204:3205]
        rls = [z.s_rlA, z.s_rlB]

        t_moe = z.moe_done_t
        vmoe = z.moe_done_v

        def ga(g):
            z.s_t.wait(g, t_moe)
            s_v.wait(g, vmoe)
            for kc in range(NCH):
                z.s_lb.inc(g.dma_start(out=z.wpool[:, kc * 1024:(kc + 1) * 1024],
                                       in_=z.d_wo[kc * 128:(kc + 1) * 128, :]), 16)
            z.s_cc.wait(g, 1)
            s_d.inc(g.dma_start(
                out=stateR, in_=z.d_sag[0:1, :].rearrange("a (p f) -> (a p) f", p=64)), 16)
        self.GE(ga)
        wo_ld = z.s_lb.n
        d_sin = s_d.n

        def va(v):
            z.s_lb.wait(v, wo_ld)
            for kc in range(NCH):
                wslc = z.wpool[:, kc * 1024:(kc + 1) * 1024]
                v.tensor_scalar(wslc, wslc, z.scol[:, kc:kc + 1], None, ALU.mult)
            s_d.wait(v, d_sin)
            v.tensor_scalar(stateR, stateR, z.flag2[0:64, 0:1], None, ALU.mult)
            v.drain()
            v.tensor_copy(Rb[0], stateR)
            s_v.inc(v.drain())
        self.VE(va)
        v_wo = s_v.n

        g1_ev = []
        for sl in range(NSLAB):
            def tg(t, sl=sl, prev=(g1_ev[-1] if g1_ev else None)):
                if sl == 0:
                    s_v.wait(t, v_wo)
                else:
                    s_s.wait(t, prev)
                for kc in range(NCH):
                    last = t.matmul(z.pm[4][0:64, :],
                                    z.wsmall[:, 256 + kc * 64:256 + (kc + 1) * 64],
                                    z.xnsl(kc, sl * SLAB, SLAB),
                                    start=(kc == 0), stop=(kc == NCH - 1))
                s_t.inc(last)
            self.TE(tg)
            tg_n = s_t.n

            def sg(sc, sl=sl, tg_n=tg_n):
                s_t.wait(sc, tg_n)
                sc.activation(g1b[:, sl * SLAB:(sl + 1) * SLAB], z.pm[4][0:64, :],
                              ACT.Copy)
                s_s.inc(sc.drain())
            self.SE(sg)
            g1_ev.append(s_s.n)
        s_g1 = s_s.n

        n_rl = {}
        v_ofin = {}
        v_Rrd = {-1: v_wo}
        s_gate = {}
        s_Rev = {-1: 0}
        v_tev = {}
        pm01_free = [v_wo]
        for cc in range(NCHUNK):
            sl, ci = cc // CPS, cc % CPS
            rl = rls[cc % 2]

            def grl(g, cc=cc, rl=rl):
                if cc >= 2:
                    s_v.wait(g, v_ofin[cc - 2])
                    s_t.wait(g, v_tev[cc - 2][1])
                else:
                    s_t.wait(g, t_moe)
                    s_v.wait(g, vmoe)
                rl.inc(g.dma_start(out=olbuf[cc % 2], in_=z.d_oloc[cc]), 16)
                rl.inc(g.dma_start(out=gbuf[cc % 2], in_=z.d_G[cc]), 16)
                rl.inc(g.dma_start(out=mtbuf[cc % 2], in_=z.d_MT[cc]), 16)
            self.GE(grl)
            n_rl[cc] = rl.n

            def tc(t, cc=cc, rl=rl):
                rl.wait(t, n_rl[cc])
                if cc >= 1:
                    s_s.wait(t, s_Rev[cc - 1])
                s_v.wait(t, max(v_Rrd[cc - 1], pm01_free[0]))
                if cc >= 1:
                    s_v.wait(t, v_ofin[cc - 1])
                for h in range(H):
                    t.matmul(z.pm[h // 8][:, (h % 8) * 64:(h % 8) * 64 + 64],
                             gbuf[cc % 2][0:64, h * 128:(h + 1) * 128],
                             Rb[cc % 2][0:64, h * 64:(h + 1) * 64],
                             start=True, stop=True)
                for h in range(H):
                    t.matmul(z.pm[2 + h // 8][0:64, (h % 8) * 64:(h % 8) * 64 + 64],
                             mtbuf[cc % 2][0:64, h * 64:(h + 1) * 64],
                             Rb[cc % 2][0:64, h * 64:(h + 1) * 64],
                             start=True, stop=True)
                s_s.wait(t, s_g1)
                t.matmul(z.pm[4][:, 0:512], g1b[0:64, cc * 128:(cc + 1) * 128],
                         z.wg2b[0:64, 0:512], start=True, stop=True)
                t.matmul(z.pm[5][:, 0:512], g1b[0:64, cc * 128:(cc + 1) * 128],
                         z.wg2b[0:64, 512:1024], start=True, stop=True)
                s_t.inc(t.drain())
            self.TE(tc)
            t_mm = s_t.n

            def scg(sc, cc=cc, t_mm=t_mm):
                s_t.wait(sc, t_mm)
                sc.activation(gatebuf[cc % 2][:, 0:512], z.pm[4][:, 0:512], ACT.Sigmoid)
                sc.activation(gatebuf[cc % 2][:, 512:1024], z.pm[5][:, 0:512],
                              ACT.Sigmoid)
                sc.activation(Rb[(cc + 1) % 2][0:64, 0:512], z.pm[2][0:64, 0:512],
                              ACT.Copy)
                sc.activation(Rb[(cc + 1) % 2][0:64, 512:1024], z.pm[3][0:64, 0:512],
                              ACT.Copy)
                s_s.inc(sc.drain())
            self.SE(scg)
            s_gate[cc] = s_Rev[cc] = s_s.n
            v_Rrd[cc] = 0

            def vo(v, cc=cc, t_mm=t_mm):
                s_t.wait(v, t_mm)
                ob = olbuf[cc % 2]
                v.tensor_add(ob[:, 0:512], ob[:, 0:512], z.pm[0][:, 0:512])
                v.tensor_add(ob[:, 512:1024], ob[:, 512:1024], z.pm[1][:, 0:512])
                s_s.wait(v, s_gate[cc])
                v.drain()
                v.tensor_mul(ob, ob, gatebuf[cc % 2])
                v.drain()
                s_v.inc(v.tensor_reduce(smu, ob, AX.X, ALU.add))
                v.drain()
            self.VE(vo)
            v_gated = s_v.n
            pm01_free[0] = v_gated

            def sq_(sc, cc=cc, v_gated=v_gated):
                s_v.wait(sc, v_gated)
                sc.activation(sqb, olbuf[cc % 2], ACT.Square)
                s_s.inc(sc.drain())
            self.SE(sq_)
            s_sq = s_s.n

            def vst(v, cc=cc, s_sq=s_sq):
                s_s.wait(v, s_sq)
                v.tensor_reduce(ssq, sqb, AX.X, ALU.add)
                v.tensor_scalar(mu, smu, 1.0 / D, None, ALU.mult)
                v.drain()
                v.tensor_mul(mu2, mu, mu)
                v.tensor_scalar(vr, ssq, 1.0 / D, float(LN_EPS), ALU.mult, ALU.add)
                v.drain()
                s_v.inc(v.tensor_sub(vr, vr, mu2))
                v.drain()
            self.VE(vst)
            v_var = s_v.n

            def ssr(sc, v_var=v_var):
                s_v.wait(sc, v_var)
                s_s.inc(sc.activation(vr, vr, ACT.Sqrt))
            self.SE(ssr)
            s_sqrt = s_s.n

            def vnm(v, cc=cc, s_sqrt=s_sqrt):
                s_s.wait(v, s_sqrt)
                v.reciprocal(vr, vr)
                v.drain()
                ob = olbuf[cc % 2]
                s_v.inc(v.tensor_scalar(ob, ob, mu, vr, ALU.subtract, ALU.mult))
                v.drain()
            self.VE(vnm)
            v_normed = s_v.n

            def ttr(t, cc=cc, v_normed=v_normed):
                s_v.wait(t, v_normed)
                if cc >= 1:
                    s_v.wait(t, v_tev[cc - 1][0])
                for kc in range(NCH):
                    t.transpose(z.pt[0][:, kc * 128:(kc + 1) * 128],
                                olbuf[cc % 2][:, kc * 128:(kc + 1) * 128],
                                z.consts[:, 0:128])
                s_t.inc(t.drain())
            self.TE(ttr)
            t_tr = s_t.n

            def vtr(v, cc=cc, ci=ci, t_tr=t_tr):
                s_t.wait(v, t_tr)
                last = None
                for kc in range(NCH):
                    last = v.tensor_copy(on_fm[:, kc * 512 + ci * 128:kc * 512 + (ci + 1) * 128],
                                         z.pt[0][:, kc * 128:(kc + 1) * 128])
                s_v.inc(last)
                v.drain()
            self.VE(vtr)
            v_tev[cc] = (s_v.n, t_mm)
            v_ofin[cc] = s_v.n

            if ci == CPS - 1:
                v_onfm = s_v.n
                t_wo = {}
                v_xs = {-2: pm01_free[0], -1: pm01_free[0]}
                for mc in range(NCH):
                    def two(t, sl=sl, mc=mc, v_onfm=v_onfm):
                        s_v.wait(t, max(v_onfm, v_xs[mc - 2]))
                        bank = z.pm[mc % 2]
                        for kc in range(NCH):
                            last = t.matmul(bank[:, 0:512],
                                            z.wpool[:, kc * 1024 + mc * 128:kc * 1024 + (mc + 1) * 128],
                                            on_fm[:, kc * 512:(kc + 1) * 512],
                                            start=(kc == 0), stop=(kc == NCH - 1))
                        s_t.inc(last)
                    self.TE(two)
                    t_wo[mc] = s_t.n

                    def vwo(v, sl=sl, mc=mc):
                        s_t.wait(v, t_wo[mc])
                        xt_ = z.xs[:, mc * T + sl * SLAB:mc * T + (sl + 1) * SLAB]
                        s_v.inc(v.tensor_add(xt_, xt_, z.pm[mc % 2][:, 0:512]))
                        v.drain()
                    self.VE(vwo)
                    v_xs[mc] = s_v.n
                pm01_free[0] = s_v.n
        z.x1_done_v = s_v.n

    def ph_l1_prep(self):
        z = self
        s_v, s_s, s_t, s_d, s_g = z.s_v, z.s_s, z.s_t, z.s_d, z.s_g

        z.src_stride, z.src_off = T, 0
        z.dst_stride, z.dst_off = z.TE_, 32
        xn2_ready = self.emit_ln(z.xs, z.xn, z.x_loaded,
                                 extra_waits=[(s_v, z.x1_done_v)])
        z.xn2_ready = xn2_ready

        def vmx(v):
            s_v.wait(v, xn2_ready)
            for c in range(NCH):
                v.tensor_reduce(z.maxc[:, c:c + 1], z.xnsl(c, 0, T), AX.X, ALU.max,
                                apply_absolute_value=True)
            v.drain()
            s_v.inc(v.tensor_reduce(z.maxc[:, 8:9] if False else z.wsc[:, 1:2],
                                    z.maxc[:, 0:8], AX.X, ALU.max))
            v.drain()
        self.VE(vmx)
        v_mxc = s_v.n

        def tmx(t):
            s_v.wait(t, v_mxc)
            t.transpose(z.pm[0][0:1, 0:128], z.wsc[:, 1:2], z.identf[:])
            s_t.inc(t.drain())
        self.TE(tmx)
        t_mxr = s_t.n

        def vmx2(v):
            s_t.wait(v, t_mxr)
            v.memset(z.aux2[0:1, 2432:2440], 0.0)
            v.drain()
            v.tensor_reduce(z.aux2[0:1, 2432:2433], z.pm[0][0:1, 0:128], AX.X, ALU.max)
            s_v.inc(v.drain())
        self.VE(vmx2)
        v_mxrow = s_v.n

        def gmx(g):
            s_v.wait(g, v_mxrow)
            s_d.inc(g.dma_start(out=z.d_mx[0], in_=z.aux2[0:1, 2432:2440]), 16)
            s_d.wait(g)
            z.s_cm0.inc(g.collective_compute(
                "AllReduce", ALU.max, replica_groups=[list(range(8))],
                ins=[z.d_mx[0].opt()], outs=[z.d_mxo[0].opt()]))
        self.GE(gmx)

        hview = z.d_halo.rearrange("a (p c) -> (a p) c", p=128)
        hagA = z.d_haloag[0:1, :].rearrange("a (p c) -> (a p) c", p=128)
        hagB = z.d_haloag[1:2, :].rearrange("a (p c) -> (a p) c", p=128)

        def gh(g):
            s_v.wait(g, xn2_ready)
            for kc in range(NCH):
                s_d.inc(g.dma_start(out=hview[:, kc * 64:kc * 64 + 32],
                                    in_=z.xn[:, kc * z.TE_ + 32:kc * z.TE_ + 64]), 16)
                s_d.inc(g.dma_start(out=hview[:, kc * 64 + 32:kc * 64 + 64],
                                    in_=z.xn[:, kc * z.TE_ + 2048:kc * z.TE_ + 2080]), 16)
            s_d.wait(g)
            z.s_ch.inc(g.collective_compute(
                "AllGather", ALU.bypass,
                replica_groups=[[0, 1], [2, 3], [4, 5], [6, 7]],
                ins=[z.d_halo.opt()], outs=[z.d_haloag.opt()]))
            z.s_ch.wait(g, 1)
            for kc in range(NCH):
                s_d.inc(g.dma_start(out=z.dws[:, 1024 + kc * 32:1024 + (kc + 1) * 32],
                                    in_=hagA[:, kc * 64 + 32:kc * 64 + 64]), 16)
                s_d.inc(g.dma_start(out=z.dws[:, 1280 + kc * 32:1280 + (kc + 1) * 32],
                                    in_=hagB[:, kc * 64:kc * 64 + 32]), 16)
        self.GE(gh)
        d_hrl = s_d.n

        def vbl(v):
            s_d.wait(v, d_hrl)
            v.tensor_scalar(z.wsc[:, 0:1], z.flag2[:], -1.0, 1.0, ALU.mult, ALU.add)
            v.drain()
            last = None
            for kc in range(NCH):
                v.tensor_scalar(z.xn[:, kc * z.TE_:kc * z.TE_ + 32],
                                z.dws[:, 1024 + kc * 32:1024 + (kc + 1) * 32],
                                z.flag2[:], None, ALU.mult)
                last = v.tensor_scalar(z.xn[:, kc * z.TE_ + 2080:kc * z.TE_ + 2112],
                                       z.dws[:, 1280 + kc * 32:1280 + (kc + 1) * 32],
                                       z.wsc[:, 0:1], None, ALU.mult)
            s_v.inc(last)
            v.drain()
        self.VE(vbl)
        z.v_halo = s_v.n

        dtrow = z.aux2[0:1, 3072:4096]
        acol = z.aux2[0:64, 2048:2049]
        cbar = z.aux2[0:64, 2049:2050]
        arow = z.aux2[0:1, 2056:2120]
        bbuf = z.aux2[0:64, 1024:2048]
        Eslot = z.qkv[0:64, 8192:9216]
        Pslot = [z.qkv[0:64, i * 1024:(i + 1) * 1024] for i in range(4)]
        kernT = z.rep[:, 0:512]

        def gcb(g):
            s_v.wait(g, xn2_ready)
            z.s_lw.inc(g.dma_start(out=dtrow, in_=z.d_ldt), 16)
            z.s_lw.inc(g.dma_start(out=acol, in_=z.d_acol), 16)
            z.s_lw.inc(g.dma_start(out=bbuf, in_=z.d_bmat), 16)
            for kc in range(NCH):
                z.s_lw.inc(g.dma_start(out=z.dws[:, kc * 64:(kc + 1) * 64],
                                       in_=z.d_cmat[kc * 128:(kc + 1) * 128, :]), 16)
        self.GE(gcb)
        cb_ld = z.s_lw.n

        def scb(sc):
            z.s_lw.wait(sc, cb_ld)
            sc.activation(dtrow, dtrow, ACT.Exp)
            s_s.inc(sc.drain())
        self.SE(scb)
        s_dt = s_s.n

        def tcb(t, s_dt=s_dt):
            s_s.wait(t, s_dt)
            s_v.wait(t, z.v_halo)
            for kc in range(NCH):
                t.matmul(z.pm[2][0:64, 0:1], z.dws[:, kc * 64:(kc + 1) * 64],
                         z.onesb[:], start=(kc == 0), stop=(kc == NCH - 1))
            t.transpose(z.pm[3][0:1, 0:64], acol, z.identf[0:64, 0:64])
            s_t.inc(t.drain())
        self.TE(tcb)
        t_cb1 = s_t.n

        def vcb(v, t_cb1=t_cb1):
            s_t.wait(v, t_cb1)
            v.tensor_copy(cbar, z.pm[2][0:64, 0:1])
            v.tensor_copy(arow, z.pm[3][0:1, 0:64])
            v.drain()
            s_v.inc(v.tensor_scalar(Eslot, bbuf, cbar, None, ALU.mult))
            v.drain()
        self.VE(vcb)
        v_E = s_v.n

        def tcb2(t, v_E=v_E):
            s_v.wait(t, v_E)
            t.matmul(z.pm[0][0:64, 0:512], arow, dtrow[:, 0:512], start=True, stop=True)
            t.matmul(z.pm[1][0:64, 0:512], arow, dtrow[:, 512:1024], start=True, stop=True)
            s_t.inc(t.drain())
        self.TE(tcb2)
        t_adt = s_t.n

        srow = z.aux2[0:1, 3072:4096]
        tK1 = {}
        tK2 = {}
        sEv = {-1: 0}
        vK = {-1: v_E, -2: v_E, -3: v_E, -4: v_E}
        for l in range(64):
            j = l % 4

            def sp_(sc, l=l, j=j, gate=tK1.get(l - 4, t_adt)):
                s_t.wait(sc, gate)
                sc.activation(Pslot[j][:, 0:512], z.pm[0][0:64, 0:512], ACT.Exp,
                              scale=float(l))
                sc.activation(Pslot[j][:, 512:1024], z.pm[1][0:64, 0:512], ACT.Exp,
                              scale=float(l))
                s_s.inc(sc.drain())
            self.SE(sp_)
            sP = s_s.n

            def vp(v, l=l, j=j, sP=sP):
                s_s.wait(v, sP)
                s_v.inc(v.tensor_mul(Pslot[j], Pslot[j], Eslot))
                v.drain()
            self.VE(vp)
            vPm = s_v.n

            def tk1(t, l=l, j=j, vPm=vPm, gate=sEv[l - 1]):
                s_v.wait(t, vPm)
                s_s.wait(t, gate)
                t.matmul(z.pm[4][0:1, 0:512], z.onesb[0:64, :], Pslot[j][:, 0:512],
                         start=True, stop=True)
                t.matmul(z.pm[5][0:1, 0:512], z.onesb[0:64, :], Pslot[j][:, 512:1024],
                         start=True, stop=True)
                s_t.inc(t.drain())
            self.TE(tk1)
            tK1[l] = s_t.n

            def sev(sc, l=l, gate=tK2.get(l - 1, 0)):
                s_t.wait(sc, max(tK1[l], gate))
                sc.activation(srow[:, 0:512], z.pm[4][0:1, 0:512], ACT.Copy)
                sc.activation(srow[:, 512:1024], z.pm[5][0:1, 0:512], ACT.Copy)
                s_s.inc(sc.drain())
            self.SE(sev)
            sEv[l] = s_s.n

            def tk2(t, l=l, gate=vK[l - 2]):
                s_s.wait(t, sEv[l])
                s_v.wait(t, gate)
                for kc in range(NCH):
                    t.transpose(z.pm[2 + l % 2][:, kc:kc + 1],
                                srow[:, kc * 128:(kc + 1) * 128],
                                z.identf[0:1, 0:1])
                s_t.inc(t.drain())
            self.TE(tk2)
            tK2[l] = s_t.n

            def vk(v, l=l):
                s_t.wait(v, tK2[l])
                cols = z.pm[2 + l % 2][:, 0:8]
                if l == 31:
                    v.tensor_add(cols, cols, z.scol[:, 8:16])
                    v.drain()
                s_v.inc(v.tensor_copy(kernT[:, l * 8:(l + 1) * 8], cols))
                v.drain()
            self.VE(vk)
            vK[l] = s_v.n
        z.kern_done_v = s_v.n

    def ph_conv(self):
        z = self
        s_v, s_s, s_t = z.s_v, z.s_s, z.s_t
        kernT = z.rep[:, 0:512]
        tb = [z.bigb[:, 0:8192], z.qkv[:, 0:8192]]
        identb = z.consts[:, 0:128]

        vB = {}
        tM = {-1: 0}
        vA = {i: z.kern_done_v for i in range(-4, 0)}
        for kc in range(NCH):
            def vb(v, kc=kc, gate=tM[kc - 2] if kc >= 2 else 0):
                if kc == 0:
                    s_v.wait(v, z.kern_done_v)
                if gate:
                    s_t.wait(v, gate)
                last = None
                for l in range(64):
                    last = v.tensor_scalar(tb[kc % 2][:, l * 128:(l + 1) * 128],
                                           identb, kernT[:, l * 8 + kc:l * 8 + kc + 1],
                                           None, ALU.mult)
                s_v.inc(last)
                v.drain()
            self.VE(vb)
            vB[kc] = s_v.n

            for sl in range(NSLAB):
                bank = z.pm[2 + (kc * NSLAB + sl) % 4]

                def tm(t, kc=kc, sl=sl, bank=bank, gate=vA[kc * NSLAB + sl - 4]):
                    s_v.wait(t, max(vB[kc], z.v_halo, gate))
                    t0 = sl * SLAB
                    for l in range(64):
                        last = t.matmul(
                            bank[:, 0:512],
                            tb[kc % 2][:, l * 128:(l + 1) * 128],
                            z.xn[:, kc * z.TE_ + t0 + 63 - l:kc * z.TE_ + t0 + 63 - l + 512],
                            start=(l == 0), stop=(l == 63))
                    s_t.inc(last)
                self.TE(tm)
                tM[kc] = s_t.n
                tmn = s_t.n

                def va(v, kc=kc, sl=sl, bank=bank, tmn=tmn):
                    s_t.wait(v, tmn)
                    xt_ = z.xs[:, kc * T + sl * SLAB:kc * T + (sl + 1) * SLAB]
                    s_v.inc(v.tensor_add(xt_, xt_, bank[:, 0:512]))
                    v.drain()
                self.VE(va)
                vA[kc * NSLAB + sl] = s_v.n
        z.conv_done_v = s_v.n
        z.conv_done_t = s_t.n

    def ffn_scales(self, mxi, cols, nvals):
        z = self
        s_v, s_d = z.s_v, z.s_d
        cm = [z.s_cm0, z.s_cm1, z.s_cm2, z.s_cm3][mxi]
        mrow = z.aux2[0:1, 2440 + mxi * 8:2448 + mxi * 8]
        sc_ = z.aux2[0:1, 2472 + mxi * 8:2472 + mxi * 8 + 8]

        def gld(g):
            cm.wait(g, 1)
            s_d.inc(g.dma_start(out=mrow, in_=z.d_mxo[mxi]), 16)
        self.GE(gld)
        d_m = s_d.n

        def vsc(v):
            s_d.wait(v, d_m)
            for i in range(nvals):
                v.tensor_scalar(sc_[:, 2 * i + 1:2 * i + 2], mrow[:, i:i + 1],
                                1.0 / 32767.0, 1e-12, ALU.mult, ALU.max)
            v.drain()
            last = None
            for i in range(nvals):
                last = v.reciprocal(sc_[:, 2 * i:2 * i + 1],
                                    sc_[:, 2 * i + 1:2 * i + 2])
            s_v.inc(last)
            v.drain()
        self.VE(vsc)
        v_sc = s_v.n

        def gbc(g):
            s_v.wait(g, v_sc)
            for i in range(2 * nvals):
                s_d.inc(g.dma_start(
                    out=z.scol[:, cols[i]:cols[i] + 1],
                    in_=sc_[:, i:i + 1].unsqueeze(1).broadcast_to([1, 128, 1])), 16)
        self.GE(gbc)
        return s_d.n

    def fold_max(self, acc_cols, mxi, pm_off):
        z = self
        s_v, s_t, s_d = z.s_v, z.s_t, z.s_d
        n = len(acc_cols)
        vin = s_v.n

        def tf(t):
            s_v.wait(t, vin)
            for i, c in enumerate(acc_cols):
                t.transpose(z.pm[4][0:1, pm_off + i * 128:pm_off + (i + 1) * 128],
                            z.wsc[:, c:c + 1], z.identf[:])
            s_t.inc(t.drain())
        self.TE(tf)
        t_f = s_t.n
        pay = z.aux2[0:1, 2504 + mxi * 8:2512 + mxi * 8]

        def vf(v):
            s_t.wait(v, t_f)
            v.memset(pay, 0.0)
            v.drain()
            last = None
            for i in range(n):
                last = v.tensor_reduce(pay[:, i:i + 1],
                                       z.pm[4][0:1, pm_off + i * 128:pm_off + (i + 1) * 128],
                                       AX.X, ALU.max)
            s_v.inc(last)
            v.drain()
        self.VE(vf)
        v_p = s_v.n
        cm = [z.s_cm0, z.s_cm1, z.s_cm2, z.s_cm3][mxi]

        def gf(g):
            s_v.wait(g, v_p)
            s_d.inc(g.dma_start(out=z.d_mx[mxi], in_=pay), 16)
            s_d.wait(g)
            cm.inc(g.collective_compute(
                "AllReduce", ALU.max, replica_groups=[list(range(8))],
                ins=[z.d_mx[mxi].opt()], outs=[z.d_mxo[mxi].opt()]))
        self.GE(gf)

    def ph_ffn1(self, ntiles=48, do_fold=True):
        z = self
        s_v, s_s, s_t, s_d = z.s_v, z.s_s, z.s_t, z.s_d
        i16 = z.dws[:].bitcast(I16)
        d_sc0 = self.ffn_scales(0, [0, 1], 1)

        def vxq(v):
            s_t.wait(v, z.conv_done_t)
            s_d.wait(v, d_sc0)
            for c in range(NCH):
                v.tensor_scalar(i16[:, 0:2048], z.xnsl(c, 0, T), z.scol[:, 0:1],
                                None, ALU.mult)
                v.drain()
                v.tensor_scalar(z.xnsl(c, 0, T), i16[:, 0:2048], z.scol[:, 1:2],
                                None, ALU.mult)
                v.drain()
            v.memset(z.wsc[:, 2:4], 0.0)
            s_v.inc(v.drain())
        self.VE(vxq)
        v_xq = s_v.n

        wnat = [z.wpool[:, 16384 + i * 1024:16384 + (i + 1) * 1024] for i in range(2)]
        wi8 = z.qkv[:, 0:512].bitcast(I8)
        wdq = [z.qkv[:, 1024 + i * 1024:1024 + (i + 1) * 1024] for i in range(2)]
        lhs = [z.qkv[:, 3072 + i * 1024:3072 + (i + 1) * 1024] for i in range(2)]
        stg = [z.qkv[:, 5120 + i * 2048:5120 + (i + 1) * 2048] for i in range(2)]
        bm = z.aux2[:, 3208:3240]
        rb = z.aux2[:, 3240:3272]
        wsem = [z.s_w1, z.s_w2a]
        spsem = [z.s_rlA, z.s_rlB]
        vQ = {}
        tT = {}
        sE = {}
        tM = {}
        vMx = {}
        sevN = {}
        spN = {}
        for ti in range(ntiles):
            up_t = ti < 24
            t_ = ti if up_t else ti - 24
            src = z.d_wup if up_t else z.d_wgt
            ws = wsem[ti % 2]

            def gw(g, ti=ti, t_=t_, src=src, ws=ws, gate=vQ.get(ti - 2, 0)):
                if gate:
                    s_v.wait(g, gate)
                ws.inc(g.dma_start(out=wnat[ti % 2],
                                   in_=src[t_ * 128:(t_ + 1) * 128, :]), 16)
            self.GE(gw)
            w_ld = ws.n

            def vq(v, ti=ti, ws=ws, w_ld=w_ld, gate=tT.get(ti - 2, 0)):
                ws.wait(v, w_ld)
                if ti == 0:
                    s_v.wait(v, v_xq)
                if gate:
                    s_t.wait(v, gate)
                wn3 = wnat[ti % 2].rearrange("p (b f) -> p b f", f=32)
                v.tensor_reduce(bm, wn3, AX.X, ALU.max, apply_absolute_value=True)
                v.drain()
                v.tensor_scalar(bm, bm, 1.0 / 7.0, 1e-12, ALU.mult, ALU.max)
                v.drain()
                v.reciprocal(rb, bm)
                v.drain()
                v.tensor_tensor(wi8[:].rearrange("p (b f) -> p b f", f=32), wn3,
                                rb.unsqueeze(2).broadcast_to([128, 32, 32]), ALU.mult)
                v.drain()
                v.tensor_tensor(wdq[ti % 2].rearrange("p (b f) -> p b f", f=32),
                                wi8[:].rearrange("p (b f) -> p b f", f=32),
                                bm.unsqueeze(2).broadcast_to([128, 32, 32]), ALU.mult)
                s_v.inc(v.drain())
            self.VE(vq)
            vQ[ti] = s_v.n

            def tt(t, ti=ti, gate=sE.get(ti - 2, 0)):
                s_v.wait(t, vQ[ti])
                if gate:
                    s_s.wait(t, gate)
                for kc in range(NCH):
                    t.transpose(z.pt[ti % 2][:, kc * 128:(kc + 1) * 128],
                                wdq[ti % 2][:, kc * 128:(kc + 1) * 128],
                                z.consts[:, 0:128])
                s_t.inc(t.drain())
            self.TE(tt)
            tT[ti] = s_t.n

            def se_(sc, ti=ti, gate=tM.get(ti - 2, 0)):
                s_t.wait(sc, tT[ti])
                if gate:
                    s_t.wait(sc, gate)
                sc.activation(lhs[ti % 2], z.pt[ti % 2][:, 0:1024], ACT.Copy)
                s_s.inc(sc.drain())
            self.SE(se_)
            sE[ti] = s_s.n

            mmn = {}
            for sl in range(NSLAB):
                pk = (ti, sl - 2) if sl >= 2 else (ti - 1, sl + 2)

                def tm(t, ti=ti, sl=sl, gate=vMx.get(pk, 0),
                       gate2=sE[ti], gate3=sevN.get(pk, 0)):
                    s_s.wait(t, max(gate2, gate3))
                    if gate:
                        s_v.wait(t, gate)
                    for kc in range(NCH):
                        last = t.matmul(z.pm[sl % 2][:, 0:512],
                                        lhs[ti % 2][:, kc * 128:(kc + 1) * 128],
                                        z.xnsl(kc, sl * SLAB, SLAB),
                                        start=(kc == 0), stop=(kc == NCH - 1))
                    s_t.inc(last)
                self.TE(tm)
                tM[ti] = s_t.n
                mmn[sl] = s_t.n

                def vmx(v, ti=ti, sl=sl, up_t=up_t, mmn=mmn[sl]):
                    s_t.wait(v, mmn)
                    acc = z.wsc[:, 2:3] if up_t else z.wsc[:, 3:4]
                    v.tensor_reduce(z.wsc[:, 4:5], z.pm[sl % 2][:, 0:512], AX.X,
                                    ALU.max, apply_absolute_value=True)
                    v.drain()
                    s_v.inc(v.tensor_tensor(acc, acc, z.wsc[:, 4:5], ALU.max))
                    v.drain()
                self.VE(vmx)

                def sev(sc, ti=ti, sl=sl, mmn=mmn[sl],
                        gate=spN.get(ti - 2, 0)):
                    s_t.wait(sc, mmn)
                    if sl == 0 and gate:
                        spsem[ti % 2].wait(sc, gate)
                    sc.activation(stg[ti % 2][:, sl * SLAB:(sl + 1) * SLAB],
                                  z.pm[sl % 2][:, 0:512], ACT.Copy)
                    s_s.inc(sc.drain())
                self.SE(sev)
                sevN[(ti, sl)] = s_s.n
                vMx[(ti, sl)] = s_v.n
            s_stg = s_s.n

            def gsp(g, ti=ti, t_=t_, up_t=up_t, s_stg=s_stg):
                s_s.wait(g, s_stg)
                dst = z.d_up[t_] if up_t else z.d_gt[t_]
                spsem[ti % 2].inc(g.dma_start(out=dst, in_=stg[ti % 2]), 16)
            self.GE(gsp)
            spN[ti] = spsem[ti % 2].n

        if do_fold:
            self.fold_max([2, 3], 1, 0)

    def ph_ffn2(self):
        z = self
        s_v, s_s, s_t, s_d = z.s_v, z.s_s, z.s_t, z.s_d
        i16 = z.dws[:].bitcast(I16)
        u16 = z.bigb[:, 0:2048].bitcast(mybir.dt.uint16)
        gq = z.aux2[:, 0:2048]
        sig = z.rep[:]
        gt = [z.qkv[:, i * 2048:(i + 1) * 2048] for i in range(2)]
        g2 = [z.qkv[:, 4096 + i * 2048:4096 + (i + 1) * 2048] for i in range(2)]
        wsem = [z.s_w1, z.s_w2a]
        spsem = [z.s_rlA, z.s_rlB]
        d_sc1 = self.ffn_scales(1, [2, 3, 4, 5], 2)

        vini = s_v.n

        def vi(v):
            s_d.wait(v, d_sc1)
            s_v.inc(v.memset(z.wsc[:, 6:7], 0.0))
            v.drain()
        self.VE(vi)
        v_init = s_v.n

        vN = {}
        sN = {}
        vG = {}
        spN = {}
        for ti in range(24):
            ws = wsem[ti % 2]

            def gw(g, ti=ti, ws=ws, gate=vN.get(ti - 2, 0)):
                if gate:
                    s_v.wait(g, gate)
                ws.inc(g.dma_start(out=gt[ti % 2], in_=z.d_gt[ti]), 16)
            self.GE(gw)
            w_ld = ws.n

            def v1(v, ti=ti, ws=ws, w_ld=w_ld):
                ws.wait(v, w_ld)
                if ti == 0:
                    s_v.wait(v, v_init)
                if ti >= 1:
                    s_s.wait(v, sN[ti - 1])
                v.tensor_scalar(i16[:, 0:2048], gt[ti % 2], z.scol[:, 4:5],
                                None, ALU.mult)
                v.drain()
                s_v.inc(v.tensor_scalar(gq, i16[:, 0:2048], z.scol[:, 5:6],
                                        None, ALU.mult))
                v.drain()
            self.VE(v1)
            v_gq = s_v.n

            def s1(sc, ti=ti, v_gq=v_gq):
                s_v.wait(sc, v_gq)
                sc.activation(sig, gq, ACT.Sigmoid)
                s_s.inc(sc.drain())
            self.SE(s1)
            sN[ti] = s_s.n

            def v2(v, ti=ti, gate=spN.get(ti - 2, 0)):
                s_s.wait(v, sN[ti])
                if gate:
                    spsem[ti % 2].wait(v, gate)
                v.tensor_scalar(u16[:, 0:2048], sig, 65536.0, None, ALU.mult)
                v.drain()
                v.scalar_tensor_tensor(g2[ti % 2], u16[:, 0:2048], 1.0 / 65536.0,
                                       gq, ALU.mult, ALU.mult)
                v.drain()
                v.tensor_reduce(z.wsc[:, 4:5], g2[ti % 2], AX.X, ALU.max,
                                apply_absolute_value=True)
                v.drain()
                s_v.inc(v.tensor_tensor(z.wsc[:, 6:7], z.wsc[:, 6:7],
                                        z.wsc[:, 4:5], ALU.max))
                v.drain()
            self.VE(v2)
            vG[ti] = vN[ti] = s_v.n

            def gsp(g, ti=ti):
                s_v.wait(g, vG[ti])
                spsem[ti % 2].inc(g.dma_start(out=z.d_g2[ti], in_=g2[ti % 2]), 16)
            self.GE(gsp)
            spN[ti] = spsem[ti % 2].n

        self.fold_max([6], 2, 0)

    def ph_ffn3(self):
        z = self
        s_v, s_s, s_t, s_d = z.s_v, z.s_s, z.s_t, z.s_d
        i16 = z.dws[:].bitcast(I16)
        i16b = z.otm[:].bitcast(I16)
        gate2 = z.aux2[:, 0:2048]
        upq = z.rep[:]
        g2b = [z.qkv[:, i * 2048:(i + 1) * 2048] for i in range(2)]
        upb = [z.qkv[:, 4096 + i * 2048:4096 + (i + 1) * 2048] for i in range(2)]
        opre = [z.qkv[:, 8192 + i * 2048:8192 + (i + 1) * 2048] for i in range(2)]
        wsem = [z.s_w1, z.s_w2a]
        wsem2 = [z.s_w2b, z.s_lw]
        spsem = [z.s_rlA, z.s_rlB]
        d_sc2 = self.ffn_scales(2, [6, 7], 1)

        def vi(v):
            s_d.wait(v, d_sc2)
            s_v.inc(v.memset(z.wsc[:, 8:9], 0.0))
            v.drain()
        self.VE(vi)
        v_init = s_v.n

        vN = {}
        spN = {}
        for ti in range(24):
            ws = wsem[ti % 2]
            ws2 = wsem2[ti % 2]

            def gw(g, ti=ti, ws=ws, ws2=ws2, gate=vN.get(ti - 2, 0)):
                if gate:
                    s_v.wait(g, gate)
                ws.inc(g.dma_start(out=g2b[ti % 2], in_=z.d_g2[ti]), 16)
                ws2.inc(g.dma_start(out=upb[ti % 2], in_=z.d_up[ti]), 16)
            self.GE(gw)
            w_ld = ws.n
            w_ld2 = ws2.n

            def v1(v, ti=ti, ws=ws, ws2=ws2, w_ld=w_ld, w_ld2=w_ld2,
                   gate=spN.get(ti - 2, 0)):
                ws.wait(v, w_ld)
                ws2.wait(v, w_ld2)
                if ti == 0:
                    s_v.wait(v, v_init)
                if gate:
                    spsem[ti % 2].wait(v, gate)
                v.tensor_scalar(i16[:, 0:2048], g2b[ti % 2], z.scol[:, 6:7],
                                None, ALU.mult)
                v.tensor_scalar(i16b[:, 0:2048], upb[ti % 2], z.scol[:, 2:3],
                                None, ALU.mult)
                v.drain()
                v.tensor_scalar(gate2, i16[:, 0:2048], z.scol[:, 7:8], None, ALU.mult)
                v.tensor_scalar(upq, i16b[:, 0:2048], z.scol[:, 3:4], None, ALU.mult)
                v.drain()
                v.tensor_mul(opre[ti % 2], gate2, upq)
                v.drain()
                v.tensor_reduce(z.wsc[:, 4:5], opre[ti % 2], AX.X, ALU.max,
                                apply_absolute_value=True)
                v.drain()
                s_v.inc(v.tensor_tensor(z.wsc[:, 8:9], z.wsc[:, 8:9],
                                        z.wsc[:, 4:5], ALU.max))
                v.drain()
            self.VE(v1)
            vN[ti] = s_v.n

            def gsp(g, ti=ti):
                s_v.wait(g, vN[ti])
                spsem[ti % 2].inc(g.dma_start(out=z.d_opre[ti], in_=opre[ti % 2]), 16)
            self.GE(gsp)
            spN[ti] = spsem[ti % 2].n

        self.fold_max([8], 3, 0)

    def ph_ffn4(self):
        z = self
        s_v, s_s, s_t, s_d = z.s_v, z.s_s, z.s_t, z.s_d
        d_sc3 = self.ffn_scales(3, [8, 9], 1)
        wnat3 = z.bigb[:, 3072:6144]
        wdq3 = z.bigb[:, 0:3072]
        wi83 = z.dws[:].bitcast(I8)[:, 0:3072]
        bm96 = z.aux2[:, 2048:2144]
        rb96 = z.aux2[:, 2144:2240]
        i16 = z.dws[:].bitcast(I16)

        vW = {}
        tW = {}
        sW = {}
        for mc in range(NCH):
            def gw(g, mc=mc, gate=vW.get(mc - 1, 0)):
                if gate:
                    s_v.wait(g, gate)
                z.s_lb.inc(g.dma_start(out=wnat3,
                                       in_=z.d_wdn[mc * 128:(mc + 1) * 128, :]), 16)
            self.GE(gw)
            w_ld = z.s_lb.n

            def vq(v, mc=mc, w_ld=w_ld, gate=tW.get(mc - 1, 0)):
                z.s_lb.wait(v, w_ld)
                if gate:
                    s_t.wait(v, gate)
                wn3 = wnat3.rearrange("p (b f) -> p b f", f=32)
                v.tensor_reduce(bm96, wn3, AX.X, ALU.max, apply_absolute_value=True)
                v.drain()
                v.tensor_scalar(bm96, bm96, 1.0 / 7.0, 1e-12, ALU.mult, ALU.max)
                v.drain()
                v.reciprocal(rb96, bm96)
                v.drain()
                v.tensor_tensor(wi83[:].rearrange("p (b f) -> p b f", f=32), wn3,
                                rb96.unsqueeze(2).broadcast_to([128, 96, 32]), ALU.mult)
                v.drain()
                v.tensor_tensor(wdq3.rearrange("p (b f) -> p b f", f=32),
                                wi83[:].rearrange("p (b f) -> p b f", f=32),
                                bm96.unsqueeze(2).broadcast_to([128, 96, 32]), ALU.mult)
                s_v.inc(v.drain())
            self.VE(vq)
            vW[mc] = s_v.n

            for grp in range(3):
                def tt(t, mc=mc, grp=grp, gate=sW.get((mc, grp - 1), sW.get((mc - 1, 2), 0))):
                    s_v.wait(t, vW[mc])
                    if gate:
                        s_s.wait(t, gate)
                    for j in range(8):
                        tcol = grp * 8 + j
                        t.transpose(z.pt[0][:, j * 128:(j + 1) * 128],
                                    wdq3[:, tcol * 128:(tcol + 1) * 128],
                                    z.consts[:, 0:128])
                    s_t.inc(t.drain())
                self.TE(tt)
                tW[mc] = s_t.n
                t_g = s_t.n

                def se_(sc, mc=mc, grp=grp, t_g=t_g):
                    s_t.wait(sc, t_g)
                    sc.activation(z.wpool[:, (mc * 24 + grp * 8) * 128:
                                          (mc * 24 + grp * 8 + 8) * 128],
                                  z.pt[0][:, 0:1024], ACT.Copy)
                    s_s.inc(sc.drain())
                self.SE(se_)
                sW[(mc, grp)] = s_s.n
        s_wdn = s_s.n

        ost = z.qkv[:]
        oldN = {}
        vqd = {}
        tDm = {}
        vAd = {-1: 0, -2: 0}
        for sl in range(NSLAB):
            osem = [z.s_w1, z.s_w2a][sl % 2]

            def go(g, sl=sl, osem=osem, gate=tDm.get(sl - 1, 0)):
                if gate:
                    s_t.wait(g, gate)
                if sl == 0:
                    s_d.wait(g, d_sc3)
                for t_ in range(NRB):
                    osem.inc(g.dma_start(
                        out=ost[:, t_ * 512:(t_ + 1) * 512],
                        in_=z.d_opre[t_][:, sl * SLAB:(sl + 1) * SLAB]), 16)
            self.GE(go)
            oldN[sl] = osem.n

            def vo(v, sl=sl, osem=osem, gate=vW.get(7)):
                osem.wait(v, oldN[sl])
                s_v.wait(v, gate)
                if sl == 0:
                    s_d.wait(v, d_sc3)
                for t_ in range(NRB):
                    v.tensor_scalar(i16[:, 0:512],
                                    ost[:, t_ * 512:(t_ + 1) * 512],
                                    z.scol[:, 8:9], None, ALU.mult)
                    v.drain()
                    v.tensor_scalar(ost[:, t_ * 512:(t_ + 1) * 512],
                                    i16[:, 0:512], z.scol[:, 9:10], None, ALU.mult)
                    v.drain()
                s_v.inc(v.drain())
            self.VE(vo)
            vqd[sl] = s_v.n

            for mc in range(NCH):
                def tdm(t, sl=sl, mc=mc, gate=vAd[(sl * NCH + mc) - 2]):
                    s_v.wait(t, max(vqd[sl], gate))
                    if sl == 0 and mc == 0:
                        s_s.wait(t, s_wdn)
                    for t_ in range(NRB):
                        last = t.matmul(z.pm[mc % 2][:, 0:512],
                                        z.wpool[:, (mc * 24 + t_) * 128:
                                                (mc * 24 + t_ + 1) * 128],
                                        ost[:, t_ * 512:(t_ + 1) * 512],
                                        start=(t_ == 0), stop=(t_ == NRB - 1))
                    s_t.inc(last)
                self.TE(tdm)
                tDm[sl] = s_t.n
                t_mm = s_t.n

                def va(v, sl=sl, mc=mc, t_mm=t_mm):
                    s_t.wait(v, t_mm)
                    xt_ = z.xs[:, mc * T + sl * SLAB:mc * T + (sl + 1) * SLAB]
                    s_v.inc(v.tensor_add(xt_, xt_, z.pm[mc % 2][:, 0:512]))
                    v.drain()
                self.VE(va)
                vAd[sl * NCH + mc] = s_v.n
        z.ffn_done_v = s_v.n

    def ph_out(self):
        z = self

        def g(gg):
            for ss in (z.s_v, z.s_s, z.s_t, z.s_g, z.s_ld, z.s_lw, z.s_lb, z.s_d,
                       z.s_cc, z.s_w1, z.s_w2a, z.s_w2b, z.s_rlA, z.s_rlB, z.s_ch,
                       z.s_cm0, z.s_cm1, z.s_cm2, z.s_cm3):
                ss.wait(gg)
            for c in range(NCH):
                z.s_st.inc(gg.dma_start(out=z.d_y[c * 128:(c + 1) * 128, :],
                                        in_=z.xs[:, c * T:(c + 1) * T]), 16)
            z.s_st.wait(gg)
        self.GE(g)


def build(dbg=None):
    b = Builder(dbg)
    b.declare()
    b.ph_consts_loads()
    if b.dump_fm("x0", b.xs):
        b.es.close(); return b.nc
    b.src_stride, b.src_off = T, 0
    b.dst_stride, b.dst_off = b.TE_, 32
    xn_ready = b.emit_ln(b.xs, b.xn, b.x_loaded)
    if b.dump_fm("xn", b.xn, stride=b.TE_, off=32):
        b.es.close(); return b.nc
    b.ph_l0(xn_ready)
    if b.dbg not in ("khat", "oloc"):
        b.ph_moe()
        if b.dump_fm("xmoe", b.xs):
            b.es.close(); return b.nc
        b.ph_attn_fin()
        if b.dump_fm("x1", b.xs):
            b.es.close(); return b.nc
        b.ph_l1_prep()
        if b.dump_fm("xn2", b.xn, stride=b.TE_, off=32):
            b.es.close(); return b.nc
        if b.dump_raw("kern", [b.rep[:, 0:512]]):
            b.es.close(); return b.nc
        b.ph_conv()
        if b.dump_fm("xconv", b.xs):
            b.es.close(); return b.nc
        if b.dbg in ("f1a", "f1b"):
            b.ph_ffn1(ntiles=(0 if b.dbg == "f1a" else 2), do_fold=False)
            b.dump_fm(b.dbg, b.xs)
            b.es.close(); return b.nc
        b.ph_ffn1()
        if b.dump_fm("f1", b.xs):
            b.es.close(); return b.nc
        b.ph_ffn2()
        if b.dump_fm("f2", b.xs):
            b.es.close(); return b.nc
        b.ph_ffn3()
        if b.dump_fm("f3", b.xs):
            b.es.close(); return b.nc
        b.ph_ffn4()
        for tag, dsrc in (("fup", b.d_up), ("fgt", b.d_gt), ("fg2", b.d_g2),
                          ("fop", b.d_opre)):
            if b.dbg == tag and not b.dbg_done:
                b.dbg_done = True
                z = b
                def gdd(gg, dsrc=dsrc):
                    for ss in (z.s_v, z.s_s, z.s_t, z.s_g, z.s_ld, z.s_lw, z.s_lb,
                               z.s_d, z.s_cc, z.s_w1, z.s_w2a, z.s_w2b, z.s_rlA,
                               z.s_rlB, z.s_ch, z.s_cm0, z.s_cm1, z.s_cm2, z.s_cm3):
                        ss.wait(gg)
                    for i in range(NCH):
                        z.s_st.inc(gg.dma_start(out=z.d_dbg[i * 128:(i + 1) * 128, :],
                                                in_=dsrc[i]), 16)
                    for c in range(NCH):
                        z.s_st.inc(gg.dma_start(out=z.d_y[c * 128:(c + 1) * 128, :],
                                                in_=z.xs[:, c * T:(c + 1) * T]), 16)
                    z.s_st.wait(gg)
                b.GE(gdd)
                b.es.close(); return b.nc
        if b.dump_raw("fo", [b.qkv[:, i * 2048:(i + 1) * 2048] for i in range(6)]):
            b.es.close(); return b.nc
        if b.dump_raw("fsc", [b.scol[:], b.maxc[:], b.wsc[:], b.aux2[:, 3072:4096]]):
            b.es.close(); return b.nc
        if b.dump_raw("fw", [b.wpool[:, i * 2048:(i + 1) * 2048] for i in range(8)]):
            b.es.close(); return b.nc
        if b.dump_fm("x2", b.xs):
            b.es.close(); return b.nc
        b.ph_out()
    if b.dbg in ("khat", "oloc"):
        z = b
        def gdump(gg):
            for ss in (z.s_v, z.s_s, z.s_t, z.s_g, z.s_ld, z.s_lw, z.s_lb, z.s_d, z.s_cc,
                       z.s_w1, z.s_w2a, z.s_w2b, z.s_rlA, z.s_rlB, z.s_ch,
                       z.s_cm0, z.s_cm1, z.s_cm2, z.s_cm3):
                ss.wait(gg)
            flat = z.d_dbg.rearrange("a b -> (a b)")
            if b.dbg == "khat":
                for blk in range(6):
                    z.s_st.inc(gg.dma_start(
                        out=z.d_dbg[blk * 128:(blk + 1) * 128, :],
                        in_=z.qkv[:, blk * 2048:(blk + 1) * 2048]), 16)
            else:
                for cc in range(NCHUNK):
                    z.s_st.inc(gg.dma_start(
                        out=flat[cc * 131072:(cc + 1) * 131072],
                        in_=z.d_oloc[cc].rearrange("p f -> (p f)")), 16)
            for c in range(NCH):
                z.s_st.inc(gg.dma_start(out=z.d_y[c * 128:(c + 1) * 128, :],
                                        in_=z.xs[:, c * T:(c + 1) * T]), 16)
            z.s_st.wait(gg)
        b.GE(gdump)
        b.dbg_done = True
        b.es.close(); return b.nc
    b.es.close()
    return b.nc


def shard_inputs(inp):
    x = np.asarray(inp["L0_attn_x"])
    w1 = np.ascontiguousarray(
        np.asarray(inp["L0_ffn_W1"]).reshape(E, D, 16, 128).transpose(0, 2, 1, 3))
    w2 = np.ascontiguousarray(
        np.asarray(inp["L0_ffn_W2"]).reshape(E, HMOE, 8, 128).transpose(0, 2, 1, 3))
    b1 = np.ascontiguousarray(
        np.asarray(inp["L0_ffn_b1"]).reshape(E, 16, 128).transpose(2, 0, 1).reshape(128, E * 16))
    common = {
        "wq": np.asarray(inp["L0_attn_Wq"]), "wk": np.asarray(inp["L0_attn_Wk"]),
        "wv": np.asarray(inp["L0_attn_Wv"]), "wo": np.asarray(inp["L0_attn_Wo"]),
        "wgam": np.asarray(inp["L0_attn_Wgamma"]), "wf": np.asarray(inp["L0_attn_Wf"]),
        "wg1": np.asarray(inp["L0_attn_Wg1"]), "wg2": np.asarray(inp["L0_attn_Wg2"]),
        "normw": np.ascontiguousarray(np.asarray(inp["L0_attn_norm_w"]).reshape(NCH, 128).T),
        "mwg": np.asarray(inp["L0_ffn_Wg"]), "mw1": w1, "mb1": b1,
        "mw2": w2, "mb2": np.asarray(inp["L0_ffn_b2"]),
        "acol": np.asarray(inp["L1_attn_A"]).reshape(64, 1),
        "bmat": np.asarray(inp["L1_attn_B"]),
        "cmat": np.asarray(inp["L1_attn_C"]),
        "dvec": np.ascontiguousarray(np.asarray(inp["L1_attn_D"]).reshape(NCH, 128).T),
        "ldt": np.asarray(inp["L1_attn_log_dt"]).reshape(1, D),
        "wup": np.asarray(inp["L1_ffn_w_up"]), "wgt": np.asarray(inp["L1_ffn_w_gate"]),
        "wdn": np.asarray(inp["L1_ffn_w_down"]),
    }
    maps = []
    for k in range(8):
        bb, h = k // 2, k % 2
        m = dict(common)
        m["x"] = np.ascontiguousarray(x[bb, h * T:(h + 1) * T, :].T)
        m["flag2"] = np.full((128, 1), float(h), np.float32)
        maps.append(m)
    return maps


def run(dbg=None, inputs=None):
    nc = build(dbg=dbg)
    maps = shard_inputs(inputs)
    return run_bass_kernel_spmd(nc, maps, core_ids=list(range(8)))


def kernel(**inputs):
    nc = build(dbg=None)
    maps = shard_inputs(inputs)
    res = run_bass_kernel_spmd(nc, maps, core_ids=list(range(8)))
    out = np.zeros((4, 4096, D), np.float32)
    for k in range(8):
        bb, h = k // 2, k % 2
        out[bb, h * T:(h + 1) * T, :] = res.results[k]["y"].T
    return out


# version 02 (final)
# speedup vs baseline: 1.0429x; 1.0429x over previous
import numpy as np
from contextlib import ExitStack
import concourse.bass as bass
import concourse.mybir as mybir
from concourse.bass_utils import run_bass_kernel_spmd

F32 = mybir.dt.float32
BF16 = mybir.dt.bfloat16
I16 = mybir.dt.int16
I8 = mybir.dt.int8
U16 = mybir.dt.uint16
ALU = mybir.AluOpType
ACT = mybir.ActivationFunctionType
AX = mybir.AxisListType

D, T, NCH, H, HD = 1024, 2048, 8, 16, 64
C, NCHUNK, NSLAB, SLAB, CPS = 128, 16, 4, 512, 4
HMOE, E, I3, NRB = 2048, 8, 3072, 24
LN_EPS = 1e-5
ATT_EPS = 1e-5

R_LOGF, R_LGAM, R_S, R_W, R_LAM, R_KSC, R_BET, R_BLAM = range(8)
CS, CW, CLAM, CKSC, CBET, CBLAM = 0, 16, 32, 48, 64, 80


class Builder:
    def __init__(self, dbg=None):
        self.dbg = dbg
        self.nc = bass.Bass("TRN2", target_bir_lowering=False)
        self.es = ExitStack()
        self.dbg_done = False

    def din(self, name, shape, dt=F32):
        return self.nc.dram_tensor(name, list(shape), dt, kind="ExternalInput").ap()

    def dtmp(self, name, shape, dt=F32):
        return self.nc.dram_tensor(name, list(shape), dt).ap()

    def S(self, name, shape, dt=F32):
        return self.es.enter_context(self.nc.sbuf_tensor(name, list(shape), dt))

    def declare(self):
        nc, din, dtmp, S = self.nc, self.din, self.dtmp, self.S
        self.d_x = din("x", [D, T])
        self.d_wq = din("wq", [D, D]); self.d_wk = din("wk", [D, D])
        self.d_wv = din("wv", [D, D]); self.d_wo = din("wo", [D, D])
        self.d_wgam = din("wgam", [D, H]); self.d_wf = din("wf", [D, H])
        self.d_wg1 = din("wg1", [D, HD]); self.d_wg2 = din("wg2", [HD, D])
        self.d_normw = din("normw", [128, NCH])
        self.d_mwg = din("mwg", [D, E])
        self.d_mw1 = din("mw1", [E, 16, 128, 1024])
        self.d_mb1 = din("mb1", [128, E * 16])
        self.d_mw2 = din("mw2", [E, 8, 128, 2048])
        self.d_mb2 = din("mb2", [E, D])
        self.d_acol = din("acol", [64, 1])
        self.d_bmat = din("bmat", [64, D])
        self.d_cmat = din("cmat", [D, 64])
        self.d_dvec = din("dvec", [128, NCH])
        self.d_ldt = din("ldt", [1, D])
        self.d_wup = din("wup", [I3, D]); self.d_wgt = din("wgt", [I3, D])
        self.d_wdn = din("wdn", [D, I3])
        self.d_flag2 = din("flag2", [128, 1])
        self.d_y = self.nc.dram_tensor("y", [D, T], F32, kind="ExternalOutput").ap()

        self.d_oloc = dtmp("d_oloc", [NCHUNK, 128, D], BF16)
        self.d_G = dtmp("d_G", [NCHUNK, 64, H * 128], BF16)
        self.d_MT = dtmp("d_MT", [NCHUNK, 64, 1024], BF16)
        self.d_sout = dtmp("d_sout", [1, 64 * 1024])
        self.d_sag = dtmp("d_sag", [2, 64 * 1024])
        self.d_halo = dtmp("d_halo", [1, 128 * 512], BF16)
        self.d_haloag = dtmp("d_haloag", [2, 128 * 512], BF16)
        self.d_gt = dtmp("d_gt", [NRB, 128, T], BF16)
        self.d_g2 = dtmp("d_g2", [NRB, 128, T], BF16)
        self.d_mx = [dtmp(f"d_mx{i}", [1, 8]) for i in range(4)]
        self.d_mxo = [dtmp(f"d_mxo{i}", [1, 8]) for i in range(4)]
        self.d_up = dtmp("d_up", [NRB, 128, T], BF16)
        self.d_opre = dtmp("d_opre", [NRB, 128, T], BF16)
        if self.dbg:
            self.d_dbg = self.nc.dram_tensor("dbgout", [D, T], F32,
                                             kind="ExternalOutput").ap()

        TE_ = T + 64
        self.TE_ = TE_
        self.xs = S("xs", [128, NCH * T], BF16)
        self.xn = S("xn", [128, NCH * TE_], BF16)
        self.wpool = S("wpool", [128, 3 * NCH * 1024], BF16)
        self.qkv = S("qkv", [128, 3 * NCH * SLAB], BF16)
        self.bigb = S("bigb", [128, 8192], BF16)
        self.otm = S("otm", [128, 2 * D], BF16)
        self.rep = S("rep", [128, 4 * SLAB])
        self.cols = S("cols", [128, 96])
        self.pairc64 = S("pairc64", [64, 16])
        self.onesf_t = S("onesf_t", [1, 128])
        self.identf = S("identf", [128, 128])
        self.mrow_t = S("mrow_t", [1, 32])
        self.nrm_t = S("nrm_t", [16, SLAB])
        self.aux2 = S("aux2", [128, 2 * T])
        self.wsmall = S("wsmall", [128, 1024], BF16)
        self.wg2b = S("wg2b", [64, D], BF16)
        self.mb2s = S("mb2s", [E, D], BF16)
        self.mb1s = S("mb1s", [128, E * 16])
        self.consts = S("consts", [128, 704], BF16)
        self.onesb = S("onesb", [128, 1], BF16)
        self.dws = S("dws", [128, 2048], BF16)
        self.state = S("state", [64, 1024])
        self.stateb = S("stateb", [128, 512], BF16)
        self.gcur = S("gcur", [64, H * 128], BF16)
        self.mtc = S("mtc", [64, 1024], BF16)
        self.maxc = S("maxc", [128, 8])
        self.iota8 = S("iota8", [128, 8])
        self.scol = S("scol", [128, 16])
        self.wsc = S("wsc", [128, 32])
        self.flag2 = S("flag2s", [128, 1])
        qf = self.qkv[:].bitcast(F32)
        self.stateR = qf[0:64, 0:1024]
        self.stateRb = self.qkv[0:64, 2048:3072]
        self.wnat = self.qkv[:, 1536:2560]
        self.wi8 = self.qkv[:, 2560:3072].bitcast(I8)
        self.q16st = self.qkv[:, 3328:3840].bitcast(I16)[:, 0:512]
        self.halo = self.rep[:].bitcast(F32)[:, 0:512] if False else None
        self.tsc = None
        self.dwf = None
        self.statrow = self.aux2[0:1, :]
        self.onesf = self.onesf_t[:]
        self.mrow = self.mrow_t[:]
        self.nrm = self.nrm_t[:]
        self.scors = self.otm[0:8, :]
        self.segA = self.aux2[0:16, 0:512]
        self.segB = self.aux2[0:16, 512:1024]
        self.seglogf = self.aux2[0:16, 1024:1536]
        self.seglgam = self.aux2[0:16, 1536:2048]
        self.strip = self.aux2[0:16, 2304:3072]
        self.kern = self.aux2[0:64, 0:1024]
        self.abuf = self.aux2[64:128, 0:1024]
        self.ebuf = self.aux2[0:64, 1024:2048]
        self.srow0 = self.aux2[0:1, 2048:2176]
        self.nsrow0 = self.aux2[0:1, 2176:2304]
        self.wrow0 = self.aux2[0:1, 2304:2432]
        self.tsc = self.aux2[:, 3072:3120]
        self.halo = self.aux2[:, 2560:3072]

        self.pm = [self.es.enter_context(self.nc.psum_tensor(f"pm{i}", [128, 512], F32))
                   for i in range(6)]
        self.pt = [self.es.enter_context(self.nc.psum_tensor(f"pt{i}", [128, 1024], BF16))
                   for i in range(2)]

        class Sem:
            def __init__(s, name):
                s.h = self.es.enter_context(self.nc.semaphore(name))
                s.n = 0
            def inc(s, inst, k=1):
                inst.then_inc(s.h, k); s.n += k; return inst
            def wait(s, eng, n=None):
                eng.wait_ge(s.h, s.n if n is None else n)

        for nm in ("s_ld", "s_lw", "s_lb", "s_w1", "s_w2a", "s_w2b", "s_v", "s_s", "s_t", "s_g", "s_d", "s_cc", "s_st", "s_rlA", "s_rlB", "s_ch", "s_cm0", "s_cm1", "s_cm2", "s_cm3"):
            setattr(self, nm, Sem(nm))

        self.block = self.es.enter_context(self.nc.Block())
        b = self.block
        self.TE, self.VE, self.SE, self.GE, self.SY = b.tensor, b.vector, b.scalar, b.gpsimd, b.sync

    def dump_fm(self, tag, src, stride=T, off=0):
        if self.dbg != tag or self.dbg_done:
            return False
        self.dbg_done = True
        z = self
        def g(gg):
            for ss in (z.s_v, z.s_s, z.s_t, z.s_g, z.s_ld, z.s_lw, z.s_lb, z.s_d, z.s_cc,
                       z.s_w1, z.s_w2a, z.s_w2b, z.s_rlA, z.s_rlB, z.s_ch,
                       z.s_cm0, z.s_cm1, z.s_cm2, z.s_cm3):
                ss.wait(gg)
            for c in range(NCH):
                z.s_st.inc(gg.dma_start(out=z.d_dbg[c * 128:(c + 1) * 128, :],
                                        in_=src[:, c * stride + off:c * stride + off + T]), 16)
            for c in range(NCH):
                z.s_st.inc(gg.dma_start(out=z.d_y[c * 128:(c + 1) * 128, :],
                                        in_=z.xs[:, c * T:(c + 1) * T]), 16)
            z.s_st.wait(gg)
        self.GE(g)
        return True

    def dump_raw(self, tag, src_ap_list):
        if self.dbg != tag or self.dbg_done:
            return False
        self.dbg_done = True
        z = self
        def g(gg):
            for ss in (z.s_v, z.s_s, z.s_t, z.s_g, z.s_ld, z.s_lw, z.s_lb, z.s_d, z.s_cc,
                       z.s_w1, z.s_w2a, z.s_w2b, z.s_rlA, z.s_rlB, z.s_ch,
                       z.s_cm0, z.s_cm1, z.s_cm2, z.s_cm3):
                ss.wait(gg)
            for c, ap in enumerate(src_ap_list[:NCH]):
                z.s_st.inc(gg.dma_start(out=z.d_dbg[c * 128:(c + 1) * 128, 0:ap.shape[-1]],
                                        in_=ap), 16)
            for c in range(NCH):
                z.s_st.inc(gg.dma_start(out=z.d_y[c * 128:(c + 1) * 128, :],
                                        in_=z.xs[:, c * T:(c + 1) * T]), 16)
            z.s_st.wait(gg)
        self.GE(g)
        return True

    def ph_consts_loads(self):
        z = self
        def g0(g):
            g.iota(z.iota8[:], pattern=[[1, 8]], base=0, channel_multiplier=0,
                   allow_small_or_imprecise_dtypes=True)
            z.s_g.inc(g.memset(z.consts[:], 0.0))
            z.s_g.wait(g)
            g.affine_select(out=z.consts[:, 0:128], in_=z.consts[:, 0:128],
                            compare_op=ALU.not_equal, fill=1.0, base=0,
                            pattern=[[-1, 128]], channel_multiplier=1)
            g.memset(z.onesb[:], 1.0)
            g.memset(z.onesf[:], 1.0)
            z.s_g.inc(g.memset(z.consts[:, 128:256], 1.0))
            z.s_g.wait(g)
            g.affine_select(out=z.consts[:, 128:256], in_=z.consts[:, 128:256],
                            compare_op=ALU.is_ge, fill=0.0, base=-1,
                            pattern=[[-1, 128]], channel_multiplier=1)
            z.s_g.inc(g.memset(z.consts[:, 256:384], -1.0))
            z.s_g.wait(g)
            g.affine_select(out=z.consts[:, 256:384], in_=z.consts[:, 256:384],
                            compare_op=ALU.is_ge, fill=0.0, base=-1,
                            pattern=[[1, 128]], channel_multiplier=-1)
            z.s_g.inc(g.memset(z.consts[:, 384:512], 1.0))
            z.s_g.wait(g)
            g.affine_select(out=z.consts[:, 384:512], in_=z.consts[:, 384:512],
                            compare_op=ALU.is_ge, fill=0.0, base=0,
                            pattern=[[-1, 128]], channel_multiplier=1)
            z.s_g.inc(g.memset(z.consts[:, 512:576], 0.0))
            z.s_g.wait(g)
            g.affine_select(out=z.consts[0:64, 512:576], in_=z.consts[0:64, 512:576],
                            compare_op=ALU.not_equal, fill=1.0, base=0,
                            pattern=[[-1, 64]], channel_multiplier=1)
            g.affine_select(out=z.consts[64:128, 512:576],
                            in_=z.consts[64:128, 512:576],
                            compare_op=ALU.not_equal, fill=1.0, base=-64,
                            pattern=[[-1, 64]], channel_multiplier=1)
            z.s_g.inc(g.memset(z.identf[:], 0.0))
            z.s_g.wait(g)
            g.affine_select(out=z.identf[:], in_=z.identf[:], compare_op=ALU.not_equal,
                            fill=1.0, base=0, pattern=[[-1, 128]], channel_multiplier=1)
            for mc in range(NCH):
                off = 576 + mc * 16
                g.affine_select(out=z.consts[0:64, off:off + 16],
                                in_=z.consts[0:64, off:off + 16],
                                compare_op=ALU.not_equal, fill=1.0, base=-(2 * mc),
                                pattern=[[1, 16]], channel_multiplier=0)
                g.affine_select(out=z.consts[64:128, off:off + 16],
                                in_=z.consts[64:128, off:off + 16],
                                compare_op=ALU.not_equal, fill=1.0, base=-(2 * mc + 1),
                                pattern=[[1, 16]], channel_multiplier=0)
            z.s_g.inc(g.drain())
            for c in range(NCH):
                z.s_ld.inc(g.dma_start(out=z.xs[:, c * T:(c + 1) * T],
                                       in_=z.d_x[c * 128:(c + 1) * 128, :]), 16)
            z.x_loaded = z.s_ld.n
            for c in range(NCH):
                z.s_lw.inc(g.dma_start(out=z.wsmall[:, c * 16:(c + 1) * 16],
                                       in_=z.d_wgam[c * 128:(c + 1) * 128, :]), 16)
                z.s_lw.inc(g.dma_start(out=z.wsmall[:, 128 + c * 16:128 + (c + 1) * 16],
                                       in_=z.d_wf[c * 128:(c + 1) * 128, :]), 16)
                z.s_lw.inc(g.dma_start(out=z.wsmall[:, 256 + c * 64:256 + (c + 1) * 64],
                                       in_=z.d_wg1[c * 128:(c + 1) * 128, :]), 16)
            z.s_lw.inc(g.dma_start(out=z.wg2b[:], in_=z.d_wg2), 16)
            z.s_lw.inc(g.dma_start(out=z.flag2[:], in_=z.d_flag2), 16)
            for c in range(NCH):
                z.s_lw.inc(g.dma_start(out=z.wsmall[:, 768 + c * 8:768 + (c + 1) * 8],
                                       in_=z.d_mwg[c * 128:(c + 1) * 128, :]), 16)
            z.s_lw.inc(g.dma_start(out=z.mb2s[:], in_=z.d_mb2), 16)
            z.s_lw.inc(g.dma_start(out=z.scol[:, 0:8], in_=z.d_normw), 16)
            z.s_lw.inc(g.dma_start(out=z.scol[:, 8:16], in_=z.d_dvec), 16)
            z.s_lw.inc(g.dma_start(out=z.mb1s[:], in_=z.d_mb1), 16)
            z.small_loaded = z.s_lw.n
            for wi_, dw in enumerate([z.d_wq, z.d_wk, z.d_wv]):
                for kc in range(NCH):
                    z.s_lb.inc(g.dma_start(
                        out=z.wpool[:, (wi_ * NCH + kc) * 1024:(wi_ * NCH + kc + 1) * 1024],
                        in_=dw[kc * 128:(kc + 1) * 128, :]), 16)
            z.qkvw_loaded = z.s_lb.n
        self.GE(g0)

    def emit_ln(self, src, dst, need_ld, extra_waits=()):
        z = self
        s_v, s_s, s_t, s_d = z.s_v, z.s_s, z.s_t, z.s_d
        sst, sof, dstr, dof = z.src_stride, z.src_off, z.dst_stride, z.dst_off

        def ssl(c, sl):
            return src[:, c * sst + sof + sl * SLAB:c * sst + sof + (sl + 1) * SLAB]

        def dsl(c, sl):
            return dst[:, c * dstr + dof + sl * SLAB:c * dstr + dof + (sl + 1) * SLAB]

        sq_done = []
        s0 = s_s.n

        def se(sc):
            z.s_ld.wait(sc, need_ld)
            for (ss, cnt) in extra_waits:
                ss.wait(sc, cnt)
            for sl in range(NSLAB):
                if sl >= 2:
                    s_t.wait(sc, t_msq_target[sl - 2])
                last = None
                for c in range(NCH):
                    last = sc.activation(z.bigb[:, ((sl % 2) * NCH + c) * SLAB:
                                                 ((sl % 2) * NCH + c + 1) * SLAB],
                                         ssl(c, sl), ACT.Square)
                s_s.inc(last)
                sq_done.append(s_s.n)
        t_msq_target = [None] * NSLAB

        t0 = s_t.n
        for sl in range(NSLAB):
            t_msq_target[sl] = t0 + sl + 1
        self.SE(se)

        v_pair_done = [s_v.n + 1, s_v.n + 2]

        def te(t):
            z.s_ld.wait(t, need_ld)
            for (ss, cnt) in extra_waits:
                ss.wait(t, cnt)
            for sl in range(NSLAB):
                if sl >= 2:
                    s_v.wait(t, v_pair_done[0])
                s_s.wait(t, sq_done[sl])
                for c in range(NCH):
                    t.matmul(z.pm[2 + sl % 2][0:1, :], z.onesb[:], ssl(c, sl),
                             start=(c == 0), stop=(c == NCH - 1))
                last = None
                for c in range(NCH):
                    last = t.matmul(z.pm[4 + sl % 2][0:1, :], z.onesb[:],
                                    z.bigb[:, ((sl % 2) * NCH + c) * SLAB:
                                           ((sl % 2) * NCH + c + 1) * SLAB],
                                    start=(c == 0), stop=(c == NCH - 1))
                s_t.inc(last)
        self.TE(te)
        assert s_t.n == t0 + NSLAB

        def ve(v):
            for pair in range(2):
                sls = [2 * pair, 2 * pair + 1]
                for sl in sls:
                    s_t.wait(v, t_msq_target[sl])
                for sl in sls:
                    mu = z.statrow[:, sl * 1024:sl * 1024 + SLAB]
                    v.tensor_scalar(mu, z.pm[2 + sl % 2][0:1, :], 1.0 / D, None, ALU.mult)
                v.drain()
                bxf = z.rep
                for si, sl in enumerate(sls):
                    x = bxf[0:1, si * 512:(si + 1) * 512]
                    mu = z.statrow[:, sl * 1024:sl * 1024 + SLAB]
                    v.tensor_mul(x, mu, mu)
                for si, sl in enumerate(sls):
                    q = z.statrow[:, sl * 1024 + 512:sl * 1024 + 512 + SLAB]
                    v.tensor_scalar(q, z.pm[4 + sl % 2][0:1, :], 1.0 / D, float(LN_EPS),
                                    ALU.mult, ALU.add)
                v.drain()
                last = None
                for si, sl in enumerate(sls):
                    q = z.statrow[:, sl * 1024 + 512:sl * 1024 + 512 + SLAB]
                    x = bxf[0:1, si * 512:(si + 1) * 512]
                    last = v.tensor_sub(q, q, x)
                s_v.inc(last)
        self.VE(ve)
        assert s_v.n == v_pair_done[1]
        v_var = s_v.n

        def se2(sc):
            s_v.wait(sc, v_var)
            last = None
            for sl in range(NSLAB):
                q = z.statrow[:, sl * 1024 + 512:sl * 1024 + 512 + SLAB]
                last = sc.activation(q, q, ACT.Sqrt)
            s_s.inc(last)
        self.SE(se2)
        s_sqrt = s_s.n

        def ve2(v):
            s_s.wait(v, s_sqrt)
            last = None
            for sl in range(NSLAB):
                q = z.statrow[:, sl * 1024 + 512:sl * 1024 + 512 + SLAB]
                last = v.reciprocal(q, q)
            s_v.inc(last)
        self.VE(ve2)
        v_rstd = s_v.n

        rep_done = []

        def ge(g):
            s_v.wait(g, v_rstd)
            for sl in range(NSLAB):
                if sl >= 1:
                    s_v.wait(g, v_rstd + sl)
                s_d.inc(g.dma_start(
                    out=z.rep[:, 0:SLAB],
                    in_=z.statrow[:, sl * 1024:sl * 1024 + SLAB]
                        .unsqueeze(1).broadcast_to([1, 128, SLAB])), 16)
                s_d.inc(g.dma_start(
                    out=z.rep[:, SLAB:2 * SLAB],
                    in_=z.statrow[:, sl * 1024 + 512:sl * 1024 + 512 + SLAB]
                        .unsqueeze(1).broadcast_to([1, 128, SLAB])), 16)
                rep_done.append(s_d.n)
        self.GE(ge)

        def ve3(v):
            for sl in range(NSLAB):
                s_d.wait(v, rep_done[sl])
                for c in range(NCH):
                    v.tensor_sub(dsl(c, sl), ssl(c, sl), z.rep[:, 0:SLAB])
                v.drain()
                last = None
                for c in range(NCH):
                    last = v.tensor_mul(dsl(c, sl), dsl(c, sl), z.rep[:, SLAB:2 * SLAB])
                s_v.inc(last)
            assert s_v.n == v_rstd + NSLAB
        self.VE(ve3)
        return s_v.n


    def xnsl(self, kc, t0, n):
        return self.xn[:, kc * self.TE_ + 32 + t0:kc * self.TE_ + 32 + t0 + n]

    def wtile(self, w, kc, mc):
        return self.wpool[:, (w * NCH + kc) * 1024 + mc * 128:(w * NCH + kc) * 1024 + (mc + 1) * 128]

    def ph_l0(self, xn_ready):
        z = self
        s_v, s_s, s_t, s_g, s_d = z.s_v, z.s_s, z.s_t, z.s_g, z.s_d
        TEc = z.TE_

        def v0(v):
            v.memset(z.state[:], 0.0)
            s_v.inc(v.memset(z.stateb[:], 0.0))
            v.drain()
        self.VE(v0)
        z.prev_unit_v = s_v.n

        for sl in range(NSLAB):
            self.l0_slab_proj(sl, xn_ready)
            if self.dbg == "khat":
                return
            for ci in range(CPS):
                cc = sl * CPS + ci
                self.l0_chunk(cc, sl)
        dsp = s_d.n

        def gse(g):
            s_v.wait(g)
            s_d.inc(g.dma_start(out=z.d_sout[0:1, :],
                                in_=z.state[:]), 16)
            s_d.wait(g, dsp + 16)
            z.s_cc.inc(g.collective_compute(
                "AllGather", ALU.bypass,
                replica_groups=[[0, 1], [2, 3], [4, 5], [6, 7]],
                ins=[z.d_sout.opt()], outs=[z.d_sag.opt()]))
        self.GE(gse)

    def l0_slab_proj(self, sl, xn_ready):
        z = self
        s_v, s_s, s_t, s_g, s_d = z.s_v, z.s_s, z.s_t, z.s_g, z.s_d

        t0 = s_t.n
        n_tiles = 3 * NCH
        ev_base = s_s.n
        nev_groups = n_tiles // 4

        def te(t):
            z.s_ld.wait(t, xn_ready and z.x_loaded)
            s_v.wait(t, xn_ready)
            z.s_lb.wait(t)
            z.s_lw.wait(t)
            idx = 0
            for w in range(3):
                for mc in range(NCH):
                    bank = z.pm[idx % 4]
                    if idx >= 4:
                        s_s.wait(t, ev_base + (idx - 4) // 4 + 1)
                    for kc in range(NCH):
                        last = t.matmul(bank[:, :], z.wtile(w, kc, mc),
                                        z.xnsl(kc, sl * SLAB, SLAB),
                                        start=(kc == 0), stop=(kc == NCH - 1))
                    if idx % 4 == 3:
                        s_t.inc(last)
                    idx += 1
            for kc in range(NCH):
                t.matmul(z.pm[4][0:16, :], z.wsmall[:, 128 + kc * 16:128 + (kc + 1) * 16],
                         z.xnsl(kc, sl * SLAB, SLAB), start=(kc == 0), stop=(kc == NCH - 1))
            for kc in range(NCH):
                last = t.matmul(z.pm[5][0:16, :], z.wsmall[:, kc * 16:(kc + 1) * 16],
                                z.xnsl(kc, sl * SLAB, SLAB), start=(kc == 0), stop=(kc == NCH - 1))
            s_t.inc(last)
        self.TE(te)
        t_qkv = [t0 + 1 + i for i in range(nev_groups)]
        t_fg = t0 + nev_groups + 1

        def se(sc):
            idx = 0
            for w in range(3):
                for mc in range(NCH):
                    if idx % 4 == 0:
                        s_t.wait(sc, t_qkv[idx // 4])
                    bank = z.pm[idx % 4]
                    last = sc.activation(z.qkv[:, w * 4096 + mc * SLAB:w * 4096 + (mc + 1) * SLAB],
                                         bank[:, :], ACT.Silu)
                    if idx % 4 == 3:
                        s_s.inc(sc.drain())
                    idx += 1
        self.SE(se)
        s_qkv_done = s_s.n

        def se2(sc):
            s_t.wait(sc, t_fg)
            sc.activation(z.segA[:], z.pm[4][0:16, :], ACT.Abs)
            sc.activation(z.segB[:], z.pm[5][0:16, :], ACT.Abs)
            sc.drain()
            sc.activation(z.segA[:], z.segA[:], ACT.Exp, scale=-1.0)
            sc.activation(z.segB[:], z.segB[:], ACT.Exp, scale=-1.0)
            sc.drain()
            sc.activation(z.segA[:], z.segA[:], ACT.Ln, bias=1.0)
            last = sc.activation(z.segB[:], z.segB[:], ACT.Ln, bias=1.0)
            s_s.inc(sc.drain())
        self.SE(se2)
        s_lnchain = s_s.n

        def ve2(v):
            s_t.wait(v, t_fg)
            v.tensor_scalar(z.seglogf[:], z.pm[4][0:16, :], 0.0, None, ALU.min)
            v.tensor_scalar(z.seglgam[:], z.pm[5][0:16, :], 0.0, None, ALU.min)
            s_s.wait(v, s_lnchain)
            v.drain()
            v.tensor_sub(z.seglogf[:], z.seglogf[:], z.segA[:])
            last = v.tensor_sub(z.seglgam[:], z.seglgam[:], z.segB[:])
            s_v.inc(v.drain())
        self.VE(ve2)
        z.rows_done = s_v.n

        def se3(sc):
            s_s.wait(sc, s_qkv_done)
            last = None
            for mc in range(NCH):
                last = sc.activation(z.bigb[:, mc * SLAB:(mc + 1) * SLAB],
                                     z.qkv[:, 4096 + mc * SLAB:4096 + (mc + 1) * SLAB],
                                     ACT.Square)
            s_s.inc(sc.drain())
        self.SE(se3)
        s_ksq = s_s.n

        def te2(t):
            s_s.wait(t, s_ksq)
            s_v.wait(t, z.rows_done)
            for mc in range(NCH):
                last = t.matmul(z.pm[4][0:16, :], z.consts[:, 576 + mc * 16:576 + (mc + 1) * 16],
                                z.bigb[:, mc * SLAB:(mc + 1) * SLAB],
                                start=(mc == 0), stop=(mc == NCH - 1))
            s_t.inc(last)
        self.TE(te2)
        t_n2 = s_t.n

        def se4(sc):
            s_t.wait(sc, t_n2)
            s_s.inc(sc.activation(z.nrm[:], z.pm[4][0:16, :], ACT.Sqrt))
        self.SE(se4)
        s_nrm = s_s.n

        def ve3(v):
            s_s.wait(v, s_nrm)
            v.tensor_scalar(z.nrm[:], z.nrm[:], 1e-12, None, ALU.max)
            v.drain()
            last = v.reciprocal(z.nrm[:], z.nrm[:])
            s_v.inc(v.drain())
        self.VE(ve3)
        v_nrm = s_v.n

        repd = []

        def ge(g):
            s_v.wait(g, v_nrm)
            for mc in range(NCH):
                if mc >= 1:
                    s_v.wait(g, v_nrm + mc)
                s_d.inc(g.dma_start(
                    out=z.rep[:, 0:SLAB],
                    in_=z.nrm[2 * mc:2 * mc + 2, :].unsqueeze(1).broadcast_to([2, 64, SLAB])), 16)
                repd.append(s_d.n)
        self.GE(ge)

        def ve4(v):
            for mc in range(NCH):
                s_d.wait(v, repd[mc])
                kt = z.qkv[:, 4096 + mc * SLAB:4096 + (mc + 1) * SLAB]
                s_v.inc(v.tensor_mul(kt, kt, z.rep[:, 0:SLAB]))
                v.drain()
        self.VE(ve4)
        z.khat_done = s_v.n

    def l0_chunk(self, cc, sl):
        z = self
        s_v, s_s, s_t, s_g, s_d = z.s_v, z.s_s, z.s_t, z.s_g, z.s_d
        ci = cc - sl * CPS
        win = (ci * 128, ci * 128 + 128)

        def seg(ap):
            return ap[:, win[0]:win[1]]

        def sp(i):
            return z.strip[:, i * 128:(i + 1) * 128]

        def ve(v):
            s_v.wait(v, max(z.rows_done, z.khat_done))
            v.tensor_tensor_scan(sp(0), seg(z.seglogf), seg(z.seglogf), 0.0,
                                 ALU.add, ALU.bypass)
            v.tensor_add(sp(1), seg(z.seglogf), seg(z.seglgam))
            v.drain()
            v.tensor_add(sp(2), sp(1), sp(0))
            v.drain()
            v.tensor_scalar(sp(5), sp(0), sp(0)[:, 127:128], None, ALU.subtract)
            s_v.inc(v.drain())
        self.VE(ve)
        v_r2 = s_v.n

        def se(sc):
            s_v.wait(sc, v_r2)
            sc.activation(sp(4), sp(0), ACT.Exp)
            sc.activation(sp(3), sp(1), ACT.Exp)
            sc.activation(sp(5), sp(5), ACT.Exp, scale=-1.0)
            s_s.inc(sc.drain())
        self.SE(se)
        s_r = s_s.n

        def ve2(v):
            s_s.wait(v, s_r)
            v.tensor_scalar(sp(3), sp(3), -1.0, None, ALU.mult)
            v.drain()
            v.tensor_mul(sp(1), sp(3), sp(4))
            s_v.inc(v.drain())
        self.VE(ve2)
        v_r3 = s_v.n

        v_rows = s_v.n

        def tc0(t):
            s_v.wait(t, v_rows)
            t.transpose(z.pm[4][0:1, 480:496], sp(4)[:, 127:128], z.identf[0:16, 0:16])
            s_t.inc(t.drain())
        self.TE(tc0)
        t_lamc = s_t.n

        def vc0(v):
            s_t.wait(v, t_lamc)
            s_v.inc(v.tensor_copy(z.mrow[:, 0:16], z.pm[4][0:1, 480:496]))
            v.drain()
        self.VE(vc0)
        v_lamc = s_v.n

        def gc0(g):
            s_v.wait(g, v_lamc)
            s_d.inc(g.dma_start(out=z.pairc64[:],
                                in_=z.mrow[:, 0:16].unsqueeze(1).broadcast_to([1, 64, 16])), 16)
        self.GE(gc0)
        d_pairc = s_d.n

        def te(t):
            s_v.wait(t, max(v_rows, z.prev_unit_v))
            for slot, piece in enumerate([0, 2, 4, 5, 3, 1]):
                t.transpose(z.pm[5][:, slot * 16:(slot + 1) * 16], sp(piece),
                            z.identf[0:16, 0:16])
            s_t.inc(t.drain())
        self.TE(te)
        t_cols = s_t.n

        def vec(v):
            s_t.wait(v, t_cols)
            s_d.wait(v, d_pairc)
            s_v.inc(v.tensor_copy(z.cols[:], z.pm[5][:, 0:96]))
            v.drain()
        self.VE(vec)
        v_cols = s_v.n

        for h in range(H):
            self.l0_unit(cc, sl, h, v_cols)

        def gsp(g):
            s_v.wait(g)
            s_d.inc(g.dma_start(out=z.d_oloc[cc], in_=z.otm[:, (cc % 2) * D:(cc % 2 + 1) * D]), 16)
            s_d.inc(g.dma_start(out=z.d_G[cc], in_=z.gcur[:]), 16)
            s_d.inc(g.dma_start(out=z.d_MT[cc], in_=z.mtc[:]), 16)
        self.GE(gsp)
        z.l0_spill_d = s_d.n

    def l0_unit(self, cc, sl, h, v_cols):
        z = self
        s_v, s_s, s_t, s_g, s_d = z.s_v, z.s_s, z.s_t, z.s_g, z.s_d
        ci = cc - sl * CPS
        pb = 64 * (h % 2)
        mc = h // 2
        base = mc * SLAB + ci * 128
        q_fm = z.qkv[pb:pb + 64, 0 * 4096 + base:0 * 4096 + base + 128]
        k_fm = z.qkv[pb:pb + 64, 1 * 4096 + base:1 * 4096 + base + 128]
        v_fm = z.qkv[pb:pb + 64, 2 * 4096 + base:2 * 4096 + base + 128]
        c_s = z.cols[:, 0 * 16 + h:0 * 16 + h + 1]
        c_w = z.cols[:, 1 * 16 + h:1 * 16 + h + 1]
        c_lam = z.cols[:, 2 * 16 + h:2 * 16 + h + 1]
        c_ksc = z.cols[:, 3 * 16 + h:3 * 16 + h + 1]
        c_bet = z.cols[:, 4 * 16 + h:4 * 16 + h + 1]
        c_blam = z.cols[:, 5 * 16 + h:5 * 16 + h + 1]
        lamC_col = z.pairc64[:, h:h + 1]
        S0f = z.state[0:64, h * 64:(h + 1) * 64]
        S0b = z.stateb[pb:pb + 64, mc * 64:(mc + 1) * 64]
        A = z.dws[:, 0:128]; AT = z.dws[:, 128:256]; X = z.dws[:, 256:384]
        QKmT = z.dws[:, 384:512]; DS = z.dws[:, 512:640]
        VTM = z.dws[:, 640:704]; KTK = z.dws[:, 704:768]
        LAMQ = z.dws[:, 768:832]
        msl = z.consts[:, 128:256]
        msuN = z.consts[:, 256:384]
        mli = z.consts[:, 384:512]

        DS2 = z.dws[:, 896:1024]
        QS = z.dws[:, 832:896]

        def t1(t):
            s_v.wait(t, max(v_cols, z.prev_unit_v))
            t.transpose(z.pm[5][0:1, 96:224], c_s, z.identf[:])
            t.transpose(z.pm[5][0:1, 224:352], c_w, z.identf[:])
            s_t.inc(t.drain())
        self.TE(t1)
        t_rows = s_t.n

        def v1(v):
            s_t.wait(v, t_rows)
            v.tensor_copy(z.srow0[:], z.pm[5][0:1, 96:224])
            v.tensor_scalar(z.nsrow0[:], z.pm[5][0:1, 96:224], -1.0, None, ALU.mult)
            v.tensor_copy(z.wrow0[:], z.pm[5][0:1, 224:352])
            s_v.inc(v.drain())
        self.VE(v1)
        v_rows0 = s_v.n

        def t2(t):
            s_v.wait(t, v_rows0)
            t.matmul(z.pm[0][:, 0:128], k_fm, k_fm, start=True, stop=True)
            t.matmul(z.pm[1][:, 0:128], q_fm, k_fm, start=True, stop=True)
            t.matmul(z.pm[2][:, 0:128], z.srow0[:], z.onesf[:], start=True, stop=False)
            t.matmul(z.pm[2][:, 0:128], z.onesf[:], z.nsrow0[:], start=False, stop=True)
            t.matmul(z.pm[3][:, 0:128], z.nsrow0[:], z.onesf[:], start=True, stop=False)
            t.matmul(z.pm[3][:, 0:128], z.onesf[:], z.wrow0[:], start=False, stop=True)
            ident64 = z.consts[pb:pb + 64, pb:pb + 64]
            t.transpose(z.pt[0][:, 0:64], k_fm, ident64)
            t.transpose(z.pt[0][:, 64:128], v_fm, ident64)
            t.transpose(z.pt[0][:, 128:192], q_fm, ident64)
            s_t.inc(t.drain())
        self.TE(t2)
        t_mm1 = s_t.n

        def v2(v):
            s_t.wait(v, t_mm1)
            v.tensor_scalar(z.pm[2][:, 0:128], z.pm[2][:, 0:128], 0.0, None, ALU.min)
            v.tensor_scalar(z.pm[3][:, 0:128], z.pm[3][:, 0:128], 0.0, None, ALU.min)
            s_v.inc(v.drain())
        self.VE(v2)
        v_clamp = s_v.n

        def s1(sc):
            s_v.wait(sc, v_clamp)
            sc.activation(DS, z.pm[2][:, 0:128], ACT.Exp)
            s_s.inc(sc.drain())
        self.SE(s1)
        s_expA = s_s.n

        def v3(v):
            s_s.wait(v, s_expA)
            v.scalar_tensor_tensor(A, DS, c_bet, msl, ALU.mult, ALU.mult)
            v.tensor_tensor(DS2, DS, mli, ALU.mult)
            v.drain()
            v.tensor_tensor(A, A, z.pm[0][:, 0:128], ALU.mult)
            v.tensor_tensor(DS2, DS2, z.pm[1][:, 0:128], ALU.mult)
            s_v.inc(v.drain())
        self.VE(v3)
        v_A = s_v.n

        def s2(sc):
            s_v.wait(sc, v_A)
            sc.activation(DS, z.pm[3][:, 0:128], ACT.Exp)
            s_s.inc(sc.drain())
        self.SE(s2)
        s_expB = s_s.n

        def v4(v):
            s_s.wait(v, s_expB)
            v.tensor_tensor(AT, DS, msuN, ALU.mult)
            v.drain()
            s_v.inc(v.tensor_tensor(AT, AT, z.pm[0][:, 0:128], ALU.mult))
            v.drain()
        self.VE(v4)
        v_AT = s_v.n

        def t3(t):
            s_v.wait(t, v_A)
            s_s.wait(t, s_expB)
            t.transpose(z.pt[1][:, 0:128], DS2, z.consts[:, 0:128])
            t.matmul(z.pm[2][:, 256:320], k_fm, S0b, start=True, stop=True)
            t.matmul(z.pm[3][:, 256:320], q_fm, S0b, start=True, stop=True)
            s_t.inc(t.drain())
        self.TE(t3)
        t_mm2 = s_t.n

        def v5(v):
            s_t.wait(v, t_mm2)
            v.tensor_copy(VTM, z.pt[0][:, 64:128])
            v.tensor_scalar(KTK, z.pt[0][:, 0:64], c_ksc, None, ALU.mult)
            v.tensor_scalar(X[:, 64:128], z.pt[0][:, 0:64], c_blam, None, ALU.mult)
            v.tensor_scalar(LAMQ, z.pt[0][:, 128:192], c_lam, None, ALU.mult)
            v.tensor_copy(QS, z.pm[3][:, 256:320])
            v.drain()
            v.scalar_tensor_tensor(X[:, 0:64], z.pm[2][:, 256:320], c_blam, VTM,
                                   ALU.mult, ALU.add)
            s_v.inc(v.drain())
        self.VE(v5)
        v_X = s_v.n

        def s3(sc):
            s_v.wait(sc, v_AT)
            s_t.wait(sc, t_mm2)
            sc.activation(QKmT, z.pt[1][:, 0:128], ACT.Copy)
            s_s.inc(sc.drain())
        self.SE(s3)
        s_qkmt = s_s.n

        tprev = [s_t.n]
        vprev = [max(v_X, v_AT)]
        sprev = [s_qkmt]
        for j in range(7):
            lastlevel = (j == 6)

            def tl(t, last=lastlevel, _v=vprev[0], _s=sprev[0]):
                s_v.wait(t, _v)
                s_s.wait(t, _s)
                t.matmul(z.pm[0][:, 128:256], AT, X, start=True, stop=True)
                if not last:
                    t.matmul(z.pm[1][:, 128:256], AT, A, start=True, stop=True)
                    t.matmul(z.pm[2][:, 128:256], A, AT, start=True, stop=True)
                s_t.inc(t.drain())
            self.TE(tl)
            tprev[0] = s_t.n

            def vl(v, last=lastlevel, _t=tprev[0]):
                s_t.wait(v, _t)
                v.tensor_tensor(X, X, z.pm[0][:, 128:256], ALU.add)
                if not last:
                    v.tensor_copy(A, z.pm[1][:, 128:256])
                s_v.inc(v.drain())
            self.VE(vl)
            vprev[0] = s_v.n

            if not lastlevel:
                def sl_(sc, _t=tprev[0]):
                    s_t.wait(sc, _t)
                    sc.activation(AT, z.pm[2][:, 128:256], ACT.Copy)
                    s_s.inc(sc.drain())
                self.SE(sl_)
                sprev[0] = s_s.n
        v_solve = vprev[0]
        s_solve = sprev[0]

        def t4(t):
            s_v.wait(t, v_solve)
            s_s.wait(t, s_solve)
            t.matmul(z.pm[0][:, 256:320], QKmT, X[:, 0:64], start=True, stop=True)
            t.matmul(z.pm[1][0:64, 256:320], KTK, X[:, 0:64], start=True, stop=True)
            t.matmul(z.pm[4][0:64, 256:320], X[:, 64:128], KTK, start=True, stop=True)
            t.matmul(z.pm[5][0:64, 352:480], X[:, 64:128], QKmT, start=True, stop=False)
            t.matmul(z.pm[5][0:64, 352:480], LAMQ, z.consts[:, 0:128], start=False, stop=True)
            s_t.inc(t.drain())
        self.TE(t4)
        t_out = s_t.n

        def v6(v):
            s_t.wait(v, t_out)
            otile = z.otm[:, (cc % 2) * D + h * 64:(cc % 2) * D + (h + 1) * 64]
            v.scalar_tensor_tensor(otile, QS, c_lam, z.pm[0][:, 256:320],
                                   ALU.mult, ALU.add)
            v.scalar_tensor_tensor(S0f, S0f, lamC_col, z.pm[1][0:64, 256:320],
                                   ALU.mult, ALU.add)
            v.scalar_tensor_tensor(z.mtc[0:64, h * 64:(h + 1) * 64],
                                   z.consts[0:64, 512:576],
                                   lamC_col, z.pm[4][0:64, 256:320],
                                   ALU.mult, ALU.add)
            v.tensor_copy(z.gcur[0:64, h * 128:(h + 1) * 128], z.pm[5][0:64, 352:480])
            v.drain()
            s_v.inc(v.tensor_copy(S0b, S0f))
            v.drain()
        self.VE(v6)
        z.prev_unit_v = s_v.n


    def ph_moe(self):
        z = self
        s_v, s_s, s_t, s_g, s_d = z.s_v, z.s_s, z.s_t, z.s_g, z.s_d
        repb = z.rep[:].bitcast(BF16)
        stg = lambda n: repb[:, n * SLAB:(n + 1) * SLAB]
        stg2 = lambda m, ni: repb[:, (m % 2) * 1024 + ni * SLAB:
                                  (m % 2) * 1024 + (ni + 1) * SLAB]
        srep = lambda sl_: repb[:, 2048 + sl_ * SLAB:2048 + (sl_ + 1) * SLAB]

        v1n = {}
        v3n = {}
        v_start = z.s_v.n
        s_start = z.s_s.n
        spill_gate = z.s_d.n
        for tt in range(16):
            def tr(t, tt=tt, gate=(v1n.get(tt - 2, v_start))):
                s_v.wait(t, gate)
                s_s.wait(t, s_start)
                bank = z.pm[4 + tt % 2]
                for kc in range(NCH):
                    last = t.matmul(bank[:, 0:8],
                                    z.xn[:, kc * z.TE_ + 32 + tt * 128:
                                         kc * z.TE_ + 32 + (tt + 1) * 128],
                                    z.wsmall[:, 768 + kc * 8:768 + (kc + 1) * 8],
                                    start=(kc == 0), stop=(kc == NCH - 1))
                s_t.inc(last)
            self.TE(tr)
            t_log = s_t.n

            def v1(v, tt=tt, t_log=t_log):
                s_t.wait(v, t_log)
                L = z.tsc[:, 0:8]; M1 = z.tsc[:, 8:9]; I1 = z.tsc[:, 9:10]
                T8 = z.tsc[:, 32:40]; OH1 = z.tsc[:, 16:24]
                v.tensor_copy(L, z.pm[4 + tt % 2][:, 0:8])
                v.drain()
                v.tensor_reduce(M1, L, AX.X, ALU.max)
                v.drain()
                v.tensor_scalar(OH1, L, M1, None, ALU.is_ge)
                v.drain()
                v.tensor_scalar(T8, OH1, -1000.0, 1000.0, ALU.mult, ALU.add)
                v.drain()
                v.tensor_add(T8, T8, z.iota8[:])
                v.drain()
                v.tensor_reduce(I1, T8, AX.X, ALU.min)
                v.drain()
                v.tensor_scalar(OH1, z.iota8[:], I1, None, ALU.is_equal)
                v.drain()
                v.scalar_tensor_tensor(T8, OH1, -1e30, L, ALU.mult, ALU.add)
                v.drain()
                v.tensor_reduce(z.tsc[:, 10:11], T8, AX.X, ALU.max)
                v.drain()
                v.tensor_scalar(z.tsc[:, 24:32], T8, z.tsc[:, 10:11], None, ALU.is_ge)
                v.drain()
                v.tensor_scalar(T8, z.tsc[:, 24:32], -1000.0, 1000.0, ALU.mult, ALU.add)
                v.drain()
                v.tensor_add(T8, T8, z.iota8[:])
                v.drain()
                v.tensor_reduce(z.tsc[:, 11:12], T8, AX.X, ALU.min)
                v.drain()
                v.tensor_scalar(z.tsc[:, 24:32], z.iota8[:], z.tsc[:, 11:12], None, ALU.is_equal)
                v.tensor_sub(z.tsc[:, 12:13], z.tsc[:, 10:11], M1)
                s_v.inc(v.drain())
            self.VE(v1)
            v_pre = s_v.n
            v1n[tt] = v_pre

            def s1(sc, v_pre=v_pre):
                s_v.wait(sc, v_pre)
                sc.activation(z.tsc[:, 12:13], z.tsc[:, 12:13], ACT.Exp)
                s_s.inc(sc.drain())
            self.SE(s1)
            s_exp = s_s.n

            def v2(v, s_exp=s_exp):
                s_s.wait(v, s_exp)
                S1 = z.tsc[:, 13:14]; S2 = z.tsc[:, 14:15]; E2 = z.tsc[:, 12:13]
                v.tensor_scalar(S1, E2, 1.0, None, ALU.add)
                v.drain()
                v.reciprocal(S1, S1)
                v.drain()
                v.tensor_mul(S2, S1, E2)
                v.drain()
                v.tensor_scalar(z.tsc[:, 40:48], z.tsc[:, 24:32], S2, None, ALU.mult)
                v.drain()
                v.scalar_tensor_tensor(z.tsc[:, 40:48], z.tsc[:, 16:24], S1,
                                       z.tsc[:, 40:48], ALU.mult, ALU.add)
                s_v.inc(v.drain())
            self.VE(v2)
            v_sct = s_v.n

            def t2(t, v_sct=v_sct, pv3=v3n.get(tt - 1, v_start)):
                s_v.wait(t, max(v_sct, pv3))
                t.transpose(z.pm[0][0:8, 384:512], z.tsc[:, 40:48], z.identf[:])
                s_t.inc(t.drain())
            self.TE(t2)
            t_tr = s_t.n

            def v3(v, tt=tt, t_tr=t_tr):
                s_t.wait(v, t_tr)
                if tt == 0:
                    s_d.wait(v, spill_gate)
                s_v.inc(v.tensor_copy(z.scors[:, tt * 128:(tt + 1) * 128],
                                      z.pm[0][0:8, 384:512]))
                v.drain()
            self.VE(v3)
            v3n[tt] = z.s_v.n
        v_scores = z.s_v.n

        b2v = {}
        for mc in range(NCH):
            for n in range(NSLAB):
                idx = mc * NSLAB + n
                bank = z.pm[idx % 4]

                def tb(t, mc=mc, n=n, bank=bank, prev_v=b2v.get(idx - 4, 0)):
                    s_v.wait(t, prev_v)
                    t.matmul(bank[:, 0:512], z.mb2s[:, mc * 128:(mc + 1) * 128],
                             z.scors[:, n * SLAB:(n + 1) * SLAB], start=True, stop=True)
                    s_t.inc(t.drain())
                self.TE(tb)
                tb_n = z.s_t.n

                def vb(v, mc=mc, n=n, bank=bank, tb_n=tb_n):
                    s_t.wait(v, tb_n)
                    xt_ = z.xs[:, mc * T + n * SLAB:mc * T + (n + 1) * SLAB]
                    s_v.inc(v.tensor_add(xt_, xt_, bank[:, 0:512]))
                    v.drain()
                self.VE(vb)
                b2v[idx] = z.s_v.n
        v_b2 = z.s_v.n

        def Hc(m):
            if m < 12:
                return z.qkv[:, m * 1024:(m + 1) * 1024]
            return z.bigb[:, (m - 12) * 1024:(m - 11) * 1024]

        w1buf = z.wpool[:, 0:16384]
        w2buf = [z.wpool[:, 16384 + i_ * 2048:16384 + (i_ + 1) * 2048] for i_ in range(2)]

        h_done_t = 0
        ty_prev = [0, 0]
        for e in range(E):
            prev_hv = z.s_v.n

            def gsr(g, e=e, prev_hv=prev_hv):
                s_v.wait(g, prev_hv)
                if e == 0:
                    s_d.wait(g, spill_gate)
                for sl_ in range(NSLAB):
                    s_d.inc(g.dma_start(
                        out=srep(sl_),
                        in_=z.scors[e:e + 1, sl_ * SLAB:(sl_ + 1) * SLAB]
                            .unsqueeze(1).broadcast_to([1, 128, SLAB])), 16)
            self.GE(gsr)
            d_srep = z.s_d.n

            def gw1(g, e=e, gate=h_done_t):
                if gate:
                    z.s_t.wait(g, gate)
                for m in range(16):
                    z.s_w1.inc(g.dma_start(
                        out=w1buf[:, m * 1024:(m + 1) * 1024],
                        in_=z.d_mw1[e, m]), 16)
            self.GE(gw1)
            w1_ld = z.s_w1.n

            for half in range(2):
                ns = [2 * half, 2 * half + 1]
                sh_hist = {}
                vh_hist = {}
                for m in range(16):
                    def th(t, e=e, m=m, w1_ld=w1_ld, sh_prev=sh_hist.get(m - 2, 0),
                           vw=z.s_v.n, vg=vh_hist.get(m - 2, 0), ns=ns):
                        z.s_w1.wait(t, w1_ld)
                        if sh_prev:
                            s_s.wait(t, sh_prev)
                        if m <= 1:
                            s_v.wait(t, vw)
                        if vg:
                            s_v.wait(t, vg)
                        for ni, n in enumerate(ns):
                            for kc in range(NCH):
                                last = t.matmul(
                                    z.pm[(m % 2) * 2 + ni][:, 0:512],
                                    w1buf[:, m * 1024 + kc * 128:m * 1024 + (kc + 1) * 128],
                                    z.xn[:, kc * z.TE_ + 32 + n * SLAB:
                                         kc * z.TE_ + 32 + (n + 1) * SLAB],
                                    start=(kc == 0), stop=(kc == NCH - 1))
                        s_t.inc(last)
                    self.TE(th)
                    th_n = z.s_t.n

                    def sh(sc, e=e, m=m, th_n=th_n, ns=ns):
                        s_t.wait(sc, th_n)
                        for ni, n in enumerate(ns):
                            sc.activation(stg2(m, ni),
                                          z.pm[(m % 2) * 2 + ni][:, 0:512],
                                          ACT.Relu,
                                          bias=z.mb1s[:, e * 16 + m:e * 16 + m + 1])
                        s_s.inc(sc.drain())
                    self.SE(sh)
                    sh_hist[m] = z.s_s.n
                    sh_n = sh_hist[m]

                    def vh(v, e=e, m=m, sh_n=sh_n, d_srep=d_srep, ns=ns):
                        s_s.wait(v, sh_n)
                        s_d.wait(v, d_srep)
                        for ni, n in enumerate(ns):
                            v.tensor_mul(Hc(m)[:, ni * SLAB:(ni + 1) * SLAB],
                                         stg2(m, ni), srep(n))
                        s_v.inc(v.drain())
                    self.VE(vh)
                    vh_hist[m] = z.s_v.n
                h_done_t = z.s_t.n
                h_v_done = z.s_v.n

                y_mm_n = {}
                yv_hist = {}
                for ym in range(8):
                    wsem = z.s_w2a if ym % 2 == 0 else z.s_w2b

                    def gw2(g, e=e, ym=ym, wsem=wsem, gate=ty_prev[ym % 2]):
                        if gate:
                            z.s_t.wait(g, gate)
                        wsem.inc(g.dma_start(
                            out=w2buf[ym % 2], in_=z.d_mw2[e, ym]), 16)
                    self.GE(gw2)
                    w2_ld = wsem.n

                    def ty(t, e=e, ym=ym, wsem=wsem, w2_ld=w2_ld, hv=h_v_done,
                           yv=yv_hist.get(ym - 2, 0), ns=ns):
                        wsem.wait(t, w2_ld)
                        s_v.wait(t, max(hv, yv))
                        for ni, n in enumerate(ns):
                            for k in range(16):
                                last = t.matmul(z.pm[2 + (ym % 2) * 2 + ni][:, 0:512],
                                                w2buf[ym % 2][:, k * 128:(k + 1) * 128],
                                                Hc(k)[:, ni * SLAB:(ni + 1) * SLAB],
                                                start=(k == 0), stop=(k == 15))
                            s_t.inc(last)
                            y_mm_n[(ym, ni)] = s_t.n
                    self.TE(ty)
                    ty_prev[ym % 2] = z.s_t.n

                    for ni, n in enumerate(ns):
                        def vy(v, e=e, ym=ym, ni=ni, n=n, key=(ym, ni)):
                            s_t.wait(v, y_mm_n[key])
                            xt_ = z.xs[:, ym * T + n * SLAB:ym * T + (n + 1) * SLAB]
                            s_v.inc(v.tensor_add(
                                xt_, xt_, z.pm[2 + (ym % 2) * 2 + ni][:, 0:512]))
                            v.drain()
                        self.VE(vy)
                    yv_hist[ym] = z.s_v.n
        z.moe_done_v = z.s_v.n
        z.moe_done_t = z.s_t.n

    def ph_attn_fin(self):
        z = self
        s_v, s_s, s_t, s_d = z.s_v, z.s_s, z.s_t, z.s_d
        qb = z.qkv
        qf = z.qkv[:].bitcast(F32)
        g1b = qb[0:64, 0:2048]
        Rb = [qb[0:64, 2048:3072], qb[0:64, 3072:4096]]
        gbuf = [qb[0:64, 4096:6144], qb[0:64, 6144:8192]]
        mtbuf = [qb[0:64, 8192:9216], qb[0:64, 9216:10240]]
        stateR = qf[0:64, 5120:6144]
        olbuf = [z.otm[:, 0:1024], z.otm[:, 1024:2048]]
        gatebuf = [z.bigb[:, 0:1024], z.bigb[:, 1024:2048]]
        on_fm = z.bigb[:, 4096:8192]
        sqb = z.rep[:, 0:1024]
        smu = z.aux2[:, 3200:3201]; ssq = z.aux2[:, 3201:3202]
        mu = z.aux2[:, 3202:3203]; mu2 = z.aux2[:, 3203:3204]
        vr = z.aux2[:, 3204:3205]
        rls = [z.s_rlA, z.s_rlB]

        t_moe = z.moe_done_t
        vmoe = z.moe_done_v

        def ga(g):
            z.s_t.wait(g, t_moe)
            s_v.wait(g, vmoe)
            for kc in range(NCH):
                z.s_lb.inc(g.dma_start(out=z.wpool[:, kc * 1024:(kc + 1) * 1024],
                                       in_=z.d_wo[kc * 128:(kc + 1) * 128, :]), 16)
            z.s_cc.wait(g, 1)
            s_d.inc(g.dma_start(
                out=stateR, in_=z.d_sag[0:1, :].rearrange("a (p f) -> (a p) f", p=64)), 16)
        self.GE(ga)
        wo_ld = z.s_lb.n
        d_sin = s_d.n

        def va(v):
            z.s_lb.wait(v, wo_ld)
            for kc in range(NCH):
                wslc = z.wpool[:, kc * 1024:(kc + 1) * 1024]
                v.tensor_scalar(wslc, wslc, z.scol[:, kc:kc + 1], None, ALU.mult)
            s_d.wait(v, d_sin)
            v.tensor_scalar(stateR, stateR, z.flag2[0:64, 0:1], None, ALU.mult)
            v.drain()
            v.tensor_copy(Rb[0], stateR)
            s_v.inc(v.drain())
        self.VE(va)
        v_wo = s_v.n

        g1_ev = []
        for sl in range(NSLAB):
            def tg(t, sl=sl, prev=(g1_ev[-1] if g1_ev else None)):
                if sl == 0:
                    s_v.wait(t, v_wo)
                else:
                    s_s.wait(t, prev)
                for kc in range(NCH):
                    last = t.matmul(z.pm[4][0:64, :],
                                    z.wsmall[:, 256 + kc * 64:256 + (kc + 1) * 64],
                                    z.xnsl(kc, sl * SLAB, SLAB),
                                    start=(kc == 0), stop=(kc == NCH - 1))
                s_t.inc(last)
            self.TE(tg)
            tg_n = s_t.n

            def sg(sc, sl=sl, tg_n=tg_n):
                s_t.wait(sc, tg_n)
                sc.activation(g1b[:, sl * SLAB:(sl + 1) * SLAB], z.pm[4][0:64, :],
                              ACT.Copy)
                s_s.inc(sc.drain())
            self.SE(sg)
            g1_ev.append(s_s.n)
        s_g1 = s_s.n

        n_rl = {}
        v_ofin = {}
        v_Rrd = {-1: v_wo}
        s_gate = {}
        s_Rev = {-1: 0}
        v_tev = {}
        pm01_free = [v_wo]
        for cc in range(NCHUNK):
            sl, ci = cc // CPS, cc % CPS
            rl = rls[cc % 2]

            def grl(g, cc=cc, rl=rl):
                if cc >= 2:
                    s_v.wait(g, v_ofin[cc - 2])
                    s_t.wait(g, v_tev[cc - 2][1])
                else:
                    s_t.wait(g, t_moe)
                    s_v.wait(g, vmoe)
                rl.inc(g.dma_start(out=olbuf[cc % 2], in_=z.d_oloc[cc]), 16)
                rl.inc(g.dma_start(out=gbuf[cc % 2], in_=z.d_G[cc]), 16)
                rl.inc(g.dma_start(out=mtbuf[cc % 2], in_=z.d_MT[cc]), 16)
            self.GE(grl)
            n_rl[cc] = rl.n

            def tc(t, cc=cc, rl=rl):
                rl.wait(t, n_rl[cc])
                if cc >= 1:
                    s_s.wait(t, s_Rev[cc - 1])
                s_v.wait(t, max(v_Rrd[cc - 1], pm01_free[0]))
                if cc >= 1:
                    s_v.wait(t, v_ofin[cc - 1])
                for h in range(H):
                    t.matmul(z.pm[h // 8][:, (h % 8) * 64:(h % 8) * 64 + 64],
                             gbuf[cc % 2][0:64, h * 128:(h + 1) * 128],
                             Rb[cc % 2][0:64, h * 64:(h + 1) * 64],
                             start=True, stop=True)
                for h in range(H):
                    t.matmul(z.pm[2 + h // 8][0:64, (h % 8) * 64:(h % 8) * 64 + 64],
                             mtbuf[cc % 2][0:64, h * 64:(h + 1) * 64],
                             Rb[cc % 2][0:64, h * 64:(h + 1) * 64],
                             start=True, stop=True)
                s_s.wait(t, s_g1)
                t.matmul(z.pm[4][:, 0:512], g1b[0:64, cc * 128:(cc + 1) * 128],
                         z.wg2b[0:64, 0:512], start=True, stop=True)
                t.matmul(z.pm[5][:, 0:512], g1b[0:64, cc * 128:(cc + 1) * 128],
                         z.wg2b[0:64, 512:1024], start=True, stop=True)
                s_t.inc(t.drain())
            self.TE(tc)
            t_mm = s_t.n

            def scg(sc, cc=cc, t_mm=t_mm):
                s_t.wait(sc, t_mm)
                sc.activation(gatebuf[cc % 2][:, 0:512], z.pm[4][:, 0:512], ACT.Sigmoid)
                sc.activation(gatebuf[cc % 2][:, 512:1024], z.pm[5][:, 0:512],
                              ACT.Sigmoid)
                sc.activation(Rb[(cc + 1) % 2][0:64, 0:512], z.pm[2][0:64, 0:512],
                              ACT.Copy)
                sc.activation(Rb[(cc + 1) % 2][0:64, 512:1024], z.pm[3][0:64, 0:512],
                              ACT.Copy)
                s_s.inc(sc.drain())
            self.SE(scg)
            s_gate[cc] = s_Rev[cc] = s_s.n
            v_Rrd[cc] = 0

            def vo(v, cc=cc, t_mm=t_mm):
                s_t.wait(v, t_mm)
                ob = olbuf[cc % 2]
                v.tensor_add(ob[:, 0:512], ob[:, 0:512], z.pm[0][:, 0:512])
                v.tensor_add(ob[:, 512:1024], ob[:, 512:1024], z.pm[1][:, 0:512])
                s_s.wait(v, s_gate[cc])
                v.drain()
                v.tensor_mul(ob, ob, gatebuf[cc % 2])
                v.drain()
                s_v.inc(v.tensor_reduce(smu, ob, AX.X, ALU.add))
                v.drain()
            self.VE(vo)
            v_gated = s_v.n
            pm01_free[0] = v_gated

            def sq_(sc, cc=cc, v_gated=v_gated):
                s_v.wait(sc, v_gated)
                sc.activation(sqb, olbuf[cc % 2], ACT.Square)
                s_s.inc(sc.drain())
            self.SE(sq_)
            s_sq = s_s.n

            def vst(v, cc=cc, s_sq=s_sq):
                s_s.wait(v, s_sq)
                v.tensor_reduce(ssq, sqb, AX.X, ALU.add)
                v.tensor_scalar(mu, smu, 1.0 / D, None, ALU.mult)
                v.drain()
                v.tensor_mul(mu2, mu, mu)
                v.tensor_scalar(vr, ssq, 1.0 / D, float(LN_EPS), ALU.mult, ALU.add)
                v.drain()
                s_v.inc(v.tensor_sub(vr, vr, mu2))
                v.drain()
            self.VE(vst)
            v_var = s_v.n

            def ssr(sc, v_var=v_var):
                s_v.wait(sc, v_var)
                s_s.inc(sc.activation(vr, vr, ACT.Sqrt))
            self.SE(ssr)
            s_sqrt = s_s.n

            def vnm(v, cc=cc, s_sqrt=s_sqrt):
                s_s.wait(v, s_sqrt)
                v.reciprocal(vr, vr)
                v.drain()
                ob = olbuf[cc % 2]
                s_v.inc(v.tensor_scalar(ob, ob, mu, vr, ALU.subtract, ALU.mult))
                v.drain()
            self.VE(vnm)
            v_normed = s_v.n

            def ttr(t, cc=cc, v_normed=v_normed):
                s_v.wait(t, v_normed)
                if cc >= 1:
                    s_v.wait(t, v_tev[cc - 1][0])
                for kc in range(NCH):
                    t.transpose(z.pt[0][:, kc * 128:(kc + 1) * 128],
                                olbuf[cc % 2][:, kc * 128:(kc + 1) * 128],
                                z.consts[:, 0:128])
                s_t.inc(t.drain())
            self.TE(ttr)
            t_tr = s_t.n

            def vtr(v, cc=cc, ci=ci, t_tr=t_tr):
                s_t.wait(v, t_tr)
                last = None
                for kc in range(NCH):
                    last = v.tensor_copy(on_fm[:, kc * 512 + ci * 128:kc * 512 + (ci + 1) * 128],
                                         z.pt[0][:, kc * 128:(kc + 1) * 128])
                s_v.inc(last)
                v.drain()
            self.VE(vtr)
            v_tev[cc] = (s_v.n, t_mm)
            v_ofin[cc] = s_v.n

            if ci == CPS - 1:
                v_onfm = s_v.n
                t_wo = {}
                v_xs = {-2: pm01_free[0], -1: pm01_free[0]}
                for mc in range(NCH):
                    def two(t, sl=sl, mc=mc, v_onfm=v_onfm):
                        s_v.wait(t, max(v_onfm, v_xs[mc - 2]))
                        bank = z.pm[mc % 2]
                        for kc in range(NCH):
                            last = t.matmul(bank[:, 0:512],
                                            z.wpool[:, kc * 1024 + mc * 128:kc * 1024 + (mc + 1) * 128],
                                            on_fm[:, kc * 512:(kc + 1) * 512],
                                            start=(kc == 0), stop=(kc == NCH - 1))
                        s_t.inc(last)
                    self.TE(two)
                    t_wo[mc] = s_t.n

                    def vwo(v, sl=sl, mc=mc):
                        s_t.wait(v, t_wo[mc])
                        xt_ = z.xs[:, mc * T + sl * SLAB:mc * T + (sl + 1) * SLAB]
                        s_v.inc(v.tensor_add(xt_, xt_, z.pm[mc % 2][:, 0:512]))
                        v.drain()
                    self.VE(vwo)
                    v_xs[mc] = s_v.n
                pm01_free[0] = s_v.n
        z.x1_done_v = s_v.n

    def ph_l1_prep(self):
        z = self
        s_v, s_s, s_t, s_d, s_g = z.s_v, z.s_s, z.s_t, z.s_d, z.s_g

        z.src_stride, z.src_off = T, 0
        z.dst_stride, z.dst_off = z.TE_, 32
        xn2_ready = self.emit_ln(z.xs, z.xn, z.x_loaded,
                                 extra_waits=[(s_v, z.x1_done_v)])
        z.xn2_ready = xn2_ready

        def vmx(v):
            s_v.wait(v, xn2_ready)
            for c in range(NCH):
                v.tensor_reduce(z.maxc[:, c:c + 1], z.xnsl(c, 0, T), AX.X, ALU.max,
                                apply_absolute_value=True)
            v.drain()
            s_v.inc(v.tensor_reduce(z.maxc[:, 8:9] if False else z.wsc[:, 1:2],
                                    z.maxc[:, 0:8], AX.X, ALU.max))
            v.drain()
        self.VE(vmx)
        v_mxc = s_v.n

        def tmx(t):
            s_v.wait(t, v_mxc)
            t.transpose(z.pm[0][0:1, 0:128], z.wsc[:, 1:2], z.identf[:])
            s_t.inc(t.drain())
        self.TE(tmx)
        t_mxr = s_t.n

        def vmx2(v):
            s_t.wait(v, t_mxr)
            v.memset(z.aux2[0:1, 2432:2440], 0.0)
            v.drain()
            v.tensor_reduce(z.aux2[0:1, 2432:2433], z.pm[0][0:1, 0:128], AX.X, ALU.max)
            s_v.inc(v.drain())
        self.VE(vmx2)
        v_mxrow = s_v.n

        def gmx(g):
            s_v.wait(g, v_mxrow)
            s_d.inc(g.dma_start(out=z.d_mx[0], in_=z.aux2[0:1, 2432:2440]), 16)
            s_d.wait(g)
            z.s_cm0.inc(g.collective_compute(
                "AllReduce", ALU.max, replica_groups=[list(range(8))],
                ins=[z.d_mx[0].opt()], outs=[z.d_mxo[0].opt()]))
        self.GE(gmx)

        hview = z.d_halo.rearrange("a (p c) -> (a p) c", p=128)
        hagA = z.d_haloag[0:1, :].rearrange("a (p c) -> (a p) c", p=128)
        hagB = z.d_haloag[1:2, :].rearrange("a (p c) -> (a p) c", p=128)

        def gh(g):
            s_v.wait(g, xn2_ready)
            for kc in range(NCH):
                s_d.inc(g.dma_start(out=hview[:, kc * 64:kc * 64 + 32],
                                    in_=z.xn[:, kc * z.TE_ + 32:kc * z.TE_ + 64]), 16)
                s_d.inc(g.dma_start(out=hview[:, kc * 64 + 32:kc * 64 + 64],
                                    in_=z.xn[:, kc * z.TE_ + 2048:kc * z.TE_ + 2080]), 16)
            s_d.wait(g)
            z.s_ch.inc(g.collective_compute(
                "AllGather", ALU.bypass,
                replica_groups=[[0, 1], [2, 3], [4, 5], [6, 7]],
                ins=[z.d_halo.opt()], outs=[z.d_haloag.opt()]))
            z.s_ch.wait(g, 1)
            for kc in range(NCH):
                s_d.inc(g.dma_start(out=z.dws[:, 1024 + kc * 32:1024 + (kc + 1) * 32],
                                    in_=hagA[:, kc * 64 + 32:kc * 64 + 64]), 16)
                s_d.inc(g.dma_start(out=z.dws[:, 1280 + kc * 32:1280 + (kc + 1) * 32],
                                    in_=hagB[:, kc * 64:kc * 64 + 32]), 16)
        self.GE(gh)
        d_hrl = s_d.n

        def vbl(v):
            s_d.wait(v, d_hrl)
            v.tensor_scalar(z.wsc[:, 0:1], z.flag2[:], -1.0, 1.0, ALU.mult, ALU.add)
            v.drain()
            last = None
            for kc in range(NCH):
                v.tensor_scalar(z.xn[:, kc * z.TE_:kc * z.TE_ + 32],
                                z.dws[:, 1024 + kc * 32:1024 + (kc + 1) * 32],
                                z.flag2[:], None, ALU.mult)
                last = v.tensor_scalar(z.xn[:, kc * z.TE_ + 2080:kc * z.TE_ + 2112],
                                       z.dws[:, 1280 + kc * 32:1280 + (kc + 1) * 32],
                                       z.wsc[:, 0:1], None, ALU.mult)
            s_v.inc(last)
            v.drain()
        self.VE(vbl)
        z.v_halo = s_v.n

        dtrow = z.aux2[0:1, 3072:4096]
        acol = z.aux2[0:64, 2048:2049]
        cbar = z.aux2[0:64, 2049:2050]
        arow = z.aux2[0:1, 2056:2120]
        bbuf = z.aux2[0:64, 1024:2048]
        Eslot = z.qkv[0:64, 8192:9216]
        Pslot = [z.qkv[0:64, i * 1024:(i + 1) * 1024] for i in range(4)]
        kernT = z.rep[:, 0:512]

        def gcb(g):
            s_v.wait(g, xn2_ready)
            z.s_lw.inc(g.dma_start(out=dtrow, in_=z.d_ldt), 16)
            z.s_lw.inc(g.dma_start(out=acol, in_=z.d_acol), 16)
            z.s_lw.inc(g.dma_start(out=bbuf, in_=z.d_bmat), 16)
            for kc in range(NCH):
                z.s_lw.inc(g.dma_start(out=z.dws[:, kc * 64:(kc + 1) * 64],
                                       in_=z.d_cmat[kc * 128:(kc + 1) * 128, :]), 16)
        self.GE(gcb)
        cb_ld = z.s_lw.n

        def scb(sc):
            z.s_lw.wait(sc, cb_ld)
            sc.activation(dtrow, dtrow, ACT.Exp)
            s_s.inc(sc.drain())
        self.SE(scb)
        s_dt = s_s.n

        def tcb(t, s_dt=s_dt):
            s_s.wait(t, s_dt)
            s_v.wait(t, z.v_halo)
            for kc in range(NCH):
                t.matmul(z.pm[2][0:64, 0:1], z.dws[:, kc * 64:(kc + 1) * 64],
                         z.onesb[:], start=(kc == 0), stop=(kc == NCH - 1))
            t.transpose(z.pm[3][0:1, 0:64], acol, z.identf[0:64, 0:64])
            s_t.inc(t.drain())
        self.TE(tcb)
        t_cb1 = s_t.n

        def vcb(v, t_cb1=t_cb1):
            s_t.wait(v, t_cb1)
            v.tensor_copy(cbar, z.pm[2][0:64, 0:1])
            v.tensor_copy(arow, z.pm[3][0:1, 0:64])
            v.drain()
            s_v.inc(v.tensor_scalar(Eslot, bbuf, cbar, None, ALU.mult))
            v.drain()
        self.VE(vcb)
        v_E = s_v.n

        def tcb2(t, v_E=v_E):
            s_v.wait(t, v_E)
            t.matmul(z.pm[0][0:64, 0:512], arow, dtrow[:, 0:512], start=True, stop=True)
            t.matmul(z.pm[1][0:64, 0:512], arow, dtrow[:, 512:1024], start=True, stop=True)
            s_t.inc(t.drain())
        self.TE(tcb2)
        t_adt = s_t.n

        srow = z.aux2[0:1, 3072:4096]
        tK1 = {}
        tK2 = {}
        sEv = {-1: 0}
        vK = {-1: v_E, -2: v_E, -3: v_E, -4: v_E}
        for l in range(64):
            j = l % 4

            def sp_(sc, l=l, j=j, gate=tK1.get(l - 4, t_adt)):
                s_t.wait(sc, gate)
                sc.activation(Pslot[j][:, 0:512], z.pm[0][0:64, 0:512], ACT.Exp,
                              scale=float(l))
                sc.activation(Pslot[j][:, 512:1024], z.pm[1][0:64, 0:512], ACT.Exp,
                              scale=float(l))
                s_s.inc(sc.drain())
            self.SE(sp_)
            sP = s_s.n

            def vp(v, l=l, j=j, sP=sP):
                s_s.wait(v, sP)
                s_v.inc(v.tensor_mul(Pslot[j], Pslot[j], Eslot))
                v.drain()
            self.VE(vp)
            vPm = s_v.n

            def tk1(t, l=l, j=j, vPm=vPm, gate=sEv[l - 1]):
                s_v.wait(t, vPm)
                s_s.wait(t, gate)
                t.matmul(z.pm[4][0:1, 0:512], z.onesb[0:64, :], Pslot[j][:, 0:512],
                         start=True, stop=True)
                t.matmul(z.pm[5][0:1, 0:512], z.onesb[0:64, :], Pslot[j][:, 512:1024],
                         start=True, stop=True)
                s_t.inc(t.drain())
            self.TE(tk1)
            tK1[l] = s_t.n

            def sev(sc, l=l, gate=tK2.get(l - 1, 0)):
                s_t.wait(sc, max(tK1[l], gate))
                sc.activation(srow[:, 0:512], z.pm[4][0:1, 0:512], ACT.Copy)
                sc.activation(srow[:, 512:1024], z.pm[5][0:1, 0:512], ACT.Copy)
                s_s.inc(sc.drain())
            self.SE(sev)
            sEv[l] = s_s.n

            def tk2(t, l=l, gate=vK[l - 2]):
                s_s.wait(t, sEv[l])
                s_v.wait(t, gate)
                for kc in range(NCH):
                    t.transpose(z.pm[2 + l % 2][:, kc:kc + 1],
                                srow[:, kc * 128:(kc + 1) * 128],
                                z.identf[0:1, 0:1])
                s_t.inc(t.drain())
            self.TE(tk2)
            tK2[l] = s_t.n

            def vk(v, l=l):
                s_t.wait(v, tK2[l])
                cols = z.pm[2 + l % 2][:, 0:8]
                if l == 31:
                    v.tensor_add(cols, cols, z.scol[:, 8:16])
                    v.drain()
                s_v.inc(v.tensor_copy(kernT[:, l * 8:(l + 1) * 8], cols))
                v.drain()
            self.VE(vk)
            vK[l] = s_v.n
        z.kern_done_v = s_v.n

    def ph_conv(self):
        z = self
        s_v, s_s, s_t = z.s_v, z.s_s, z.s_t
        kernT = z.rep[:, 0:512]
        tb = [z.bigb[:, 0:8192], z.qkv[:, 0:8192]]
        identb = z.consts[:, 0:128]

        vB = {}
        tM = {-1: 0}
        vA = {i: z.kern_done_v for i in range(-4, 0)}
        for kc in range(NCH):
            def vb(v, kc=kc, gate=tM[kc - 2] if kc >= 2 else 0):
                if kc == 0:
                    s_v.wait(v, z.kern_done_v)
                if gate:
                    s_t.wait(v, gate)
                last = None
                for l in range(64):
                    last = v.tensor_scalar(tb[kc % 2][:, l * 128:(l + 1) * 128],
                                           identb, kernT[:, l * 8 + kc:l * 8 + kc + 1],
                                           None, ALU.mult)
                s_v.inc(last)
                v.drain()
            self.VE(vb)
            vB[kc] = s_v.n

            for sl in range(NSLAB):
                bank = z.pm[2 + (kc * NSLAB + sl) % 4]

                def tm(t, kc=kc, sl=sl, bank=bank, gate=vA[kc * NSLAB + sl - 4]):
                    s_v.wait(t, max(vB[kc], z.v_halo, gate))
                    t0 = sl * SLAB
                    for l in range(64):
                        last = t.matmul(
                            bank[:, 0:512],
                            tb[kc % 2][:, l * 128:(l + 1) * 128],
                            z.xn[:, kc * z.TE_ + t0 + 63 - l:kc * z.TE_ + t0 + 63 - l + 512],
                            start=(l == 0), stop=(l == 63))
                    s_t.inc(last)
                self.TE(tm)
                tM[kc] = s_t.n
                tmn = s_t.n

                def va(v, kc=kc, sl=sl, bank=bank, tmn=tmn):
                    s_t.wait(v, tmn)
                    xt_ = z.xs[:, kc * T + sl * SLAB:kc * T + (sl + 1) * SLAB]
                    s_v.inc(v.tensor_add(xt_, xt_, bank[:, 0:512]))
                    v.drain()
                self.VE(va)
                vA[kc * NSLAB + sl] = s_v.n
        z.conv_done_v = s_v.n
        z.conv_done_t = s_t.n

    def ffn_scales(self, mxi, cols, nvals):
        z = self
        s_v, s_d = z.s_v, z.s_d
        cm = [z.s_cm0, z.s_cm1, z.s_cm2, z.s_cm3][mxi]
        mrow = z.aux2[0:1, 2440 + mxi * 8:2448 + mxi * 8]
        sc_ = z.aux2[0:1, 2472 + mxi * 8:2472 + mxi * 8 + 8]

        def gld(g):
            cm.wait(g, 1)
            s_d.inc(g.dma_start(out=mrow, in_=z.d_mxo[mxi]), 16)
        self.GE(gld)
        d_m = s_d.n

        def vsc(v):
            s_d.wait(v, d_m)
            for i in range(nvals):
                v.tensor_scalar(sc_[:, 2 * i + 1:2 * i + 2], mrow[:, i:i + 1],
                                1.0 / 32767.0, 1e-12, ALU.mult, ALU.max)
            v.drain()
            last = None
            for i in range(nvals):
                last = v.reciprocal(sc_[:, 2 * i:2 * i + 1],
                                    sc_[:, 2 * i + 1:2 * i + 2])
            s_v.inc(last)
            v.drain()
        self.VE(vsc)
        v_sc = s_v.n

        def gbc(g):
            s_v.wait(g, v_sc)
            for i in range(2 * nvals):
                s_d.inc(g.dma_start(
                    out=z.scol[:, cols[i]:cols[i] + 1],
                    in_=sc_[:, i:i + 1].unsqueeze(1).broadcast_to([1, 128, 1])), 16)
        self.GE(gbc)
        return s_d.n

    def fold_max(self, acc_cols, mxi, pm_off):
        z = self
        s_v, s_t, s_d = z.s_v, z.s_t, z.s_d
        n = len(acc_cols)
        vin = s_v.n

        def tf(t):
            s_v.wait(t, vin)
            for i, c in enumerate(acc_cols):
                t.transpose(z.pm[4][0:1, pm_off + i * 128:pm_off + (i + 1) * 128],
                            z.wsc[:, c:c + 1], z.identf[:])
            s_t.inc(t.drain())
        self.TE(tf)
        t_f = s_t.n
        pay = z.aux2[0:1, 2504 + mxi * 8:2512 + mxi * 8]

        def vf(v):
            s_t.wait(v, t_f)
            v.memset(pay, 0.0)
            v.drain()
            last = None
            for i in range(n):
                last = v.tensor_reduce(pay[:, i:i + 1],
                                       z.pm[4][0:1, pm_off + i * 128:pm_off + (i + 1) * 128],
                                       AX.X, ALU.max)
            s_v.inc(last)
            v.drain()
        self.VE(vf)
        v_p = s_v.n
        cm = [z.s_cm0, z.s_cm1, z.s_cm2, z.s_cm3][mxi]

        def gf(g):
            s_v.wait(g, v_p)
            s_d.inc(g.dma_start(out=z.d_mx[mxi], in_=pay), 16)
            s_d.wait(g)
            cm.inc(g.collective_compute(
                "AllReduce", ALU.max, replica_groups=[list(range(8))],
                ins=[z.d_mx[mxi].opt()], outs=[z.d_mxo[mxi].opt()]))
        self.GE(gf)

    def ph_ffn1(self, ntiles=48, do_fold=True):
        z = self
        s_v, s_s, s_t, s_d = z.s_v, z.s_s, z.s_t, z.s_d
        i16 = z.dws[:].bitcast(I16)
        d_sc0 = self.ffn_scales(0, [0, 1], 1)

        def vxq(v):
            s_t.wait(v, z.conv_done_t)
            s_d.wait(v, d_sc0)
            for c in range(NCH):
                v.tensor_scalar(i16[:, 0:2048], z.xnsl(c, 0, T), z.scol[:, 0:1],
                                None, ALU.mult)
                v.drain()
                v.tensor_scalar(z.xnsl(c, 0, T), i16[:, 0:2048], z.scol[:, 1:2],
                                None, ALU.mult)
                v.drain()
            v.memset(z.wsc[:, 2:4], 0.0)
            s_v.inc(v.drain())
        self.VE(vxq)
        v_xq = s_v.n

        wnat = [z.wpool[:, 16384 + i * 1024:16384 + (i + 1) * 1024] for i in range(2)]
        wi8 = z.qkv[:, 0:512].bitcast(I8)
        wdq = [z.qkv[:, 1024 + i * 1024:1024 + (i + 1) * 1024] for i in range(2)]
        lhs = [z.qkv[:, 3072 + i * 1024:3072 + (i + 1) * 1024] for i in range(2)]
        stg = [z.qkv[:, 5120 + i * 2048:5120 + (i + 1) * 2048] for i in range(2)]
        bm = z.aux2[:, 3208:3240]
        rb = z.aux2[:, 3240:3272]
        wsem = [z.s_w1, z.s_w2a]
        spsem = [z.s_rlA, z.s_rlB]
        vQ = {}
        tT = {}
        sE = {}
        tM = {}
        vMx = {}
        sevN = {}
        spN = {}
        for ti in range(ntiles):
            up_t = ti < 24
            t_ = ti if up_t else ti - 24
            src = z.d_wup if up_t else z.d_wgt
            ws = wsem[ti % 2]

            def gw(g, ti=ti, t_=t_, src=src, ws=ws, gate=vQ.get(ti - 2, 0)):
                if gate:
                    s_v.wait(g, gate)
                ws.inc(g.dma_start(out=wnat[ti % 2],
                                   in_=src[t_ * 128:(t_ + 1) * 128, :]), 16)
            self.GE(gw)
            w_ld = ws.n

            def vq(v, ti=ti, ws=ws, w_ld=w_ld, gate=tT.get(ti - 2, 0)):
                ws.wait(v, w_ld)
                if ti == 0:
                    s_v.wait(v, v_xq)
                if gate:
                    s_t.wait(v, gate)
                wn3 = wnat[ti % 2].rearrange("p (b f) -> p b f", f=32)
                v.tensor_reduce(bm, wn3, AX.X, ALU.max, apply_absolute_value=True)
                v.drain()
                v.tensor_scalar(bm, bm, 1.0 / 7.0, 1e-12, ALU.mult, ALU.max)
                v.drain()
                v.reciprocal(rb, bm)
                v.drain()
                v.tensor_tensor(wi8[:].rearrange("p (b f) -> p b f", f=32), wn3,
                                rb.unsqueeze(2).broadcast_to([128, 32, 32]), ALU.mult)
                v.drain()
                v.tensor_tensor(wdq[ti % 2].rearrange("p (b f) -> p b f", f=32),
                                wi8[:].rearrange("p (b f) -> p b f", f=32),
                                bm.unsqueeze(2).broadcast_to([128, 32, 32]), ALU.mult)
                s_v.inc(v.drain())
            self.VE(vq)
            vQ[ti] = s_v.n

            def tt(t, ti=ti, gate=sE.get(ti - 2, 0)):
                s_v.wait(t, vQ[ti])
                if gate:
                    s_s.wait(t, gate)
                for kc in range(NCH):
                    t.transpose(z.pt[ti % 2][:, kc * 128:(kc + 1) * 128],
                                wdq[ti % 2][:, kc * 128:(kc + 1) * 128],
                                z.consts[:, 0:128])
                s_t.inc(t.drain())
            self.TE(tt)
            tT[ti] = s_t.n

            def se_(sc, ti=ti, gate=tM.get(ti - 2, 0)):
                s_t.wait(sc, tT[ti])
                if gate:
                    s_t.wait(sc, gate)
                sc.activation(lhs[ti % 2], z.pt[ti % 2][:, 0:1024], ACT.Copy)
                s_s.inc(sc.drain())
            self.SE(se_)
            sE[ti] = s_s.n

            mmn = {}
            for sl in range(NSLAB):
                pk = (ti - 1, sl)

                def tm(t, ti=ti, sl=sl, gate=vMx.get(pk, 0),
                       gate2=sE[ti], gate3=sevN.get(pk, 0)):
                    s_s.wait(t, max(gate2, gate3))
                    if gate:
                        s_v.wait(t, gate)
                    for kc in range(NCH):
                        last = t.matmul(z.pm[sl][:, 0:512],
                                        lhs[ti % 2][:, kc * 128:(kc + 1) * 128],
                                        z.xnsl(kc, sl * SLAB, SLAB),
                                        start=(kc == 0), stop=(kc == NCH - 1))
                    s_t.inc(last)
                self.TE(tm)
                tM[ti] = s_t.n
                mmn[sl] = s_t.n

                def vmx(v, ti=ti, sl=sl, up_t=up_t, mmn=mmn[sl]):
                    s_t.wait(v, mmn)
                    acc = z.wsc[:, 2:3] if up_t else z.wsc[:, 3:4]
                    v.tensor_reduce(z.wsc[:, 4:5], z.pm[sl][:, 0:512], AX.X,
                                    ALU.max, apply_absolute_value=True)
                    v.drain()
                    s_v.inc(v.tensor_tensor(acc, acc, z.wsc[:, 4:5], ALU.max))
                    v.drain()
                self.VE(vmx)

                def sev(sc, ti=ti, sl=sl, mmn=mmn[sl],
                        gate=spN.get(ti - 2, 0)):
                    s_t.wait(sc, mmn)
                    if sl == 0 and gate:
                        spsem[ti % 2].wait(sc, gate)
                    sc.activation(stg[ti % 2][:, sl * SLAB:(sl + 1) * SLAB],
                                  z.pm[sl][:, 0:512], ACT.Copy)
                    s_s.inc(sc.drain())
                self.SE(sev)
                sevN[(ti, sl)] = s_s.n
                vMx[(ti, sl)] = s_v.n
            s_stg = s_s.n

            def gsp(g, ti=ti, t_=t_, up_t=up_t, s_stg=s_stg):
                s_s.wait(g, s_stg)
                dst = z.d_up[t_] if up_t else z.d_gt[t_]
                spsem[ti % 2].inc(g.dma_start(out=dst, in_=stg[ti % 2]), 16)
            self.GE(gsp)
            spN[ti] = spsem[ti % 2].n

        if do_fold:
            self.fold_max([2, 3], 1, 0)

    def ph_ffn2(self):
        z = self
        s_v, s_s, s_t, s_d = z.s_v, z.s_s, z.s_t, z.s_d
        i16 = z.dws[:].bitcast(I16)
        u16 = z.bigb[:, 0:2048].bitcast(mybir.dt.uint16)
        gq = z.aux2[:, 0:2048]
        sig = z.rep[:]
        gt = [z.qkv[:, i * 2048:(i + 1) * 2048] for i in range(2)]
        g2 = [z.qkv[:, 4096 + i * 2048:4096 + (i + 1) * 2048] for i in range(2)]
        wsem = [z.s_w1, z.s_w2a]
        spsem = [z.s_rlA, z.s_rlB]
        d_sc1 = self.ffn_scales(1, [2, 3, 4, 5], 2)

        vini = s_v.n

        def vi(v):
            s_d.wait(v, d_sc1)
            s_v.inc(v.memset(z.wsc[:, 6:7], 0.0))
            v.drain()
        self.VE(vi)
        v_init = s_v.n

        vN = {}
        sN = {}
        vG = {}
        spN = {}
        for ti in range(24):
            ws = wsem[ti % 2]

            def gw(g, ti=ti, ws=ws, gate=vN.get(ti - 2, 0)):
                if gate:
                    s_v.wait(g, gate)
                ws.inc(g.dma_start(out=gt[ti % 2], in_=z.d_gt[ti]), 16)
            self.GE(gw)
            w_ld = ws.n

            def v1(v, ti=ti, ws=ws, w_ld=w_ld):
                ws.wait(v, w_ld)
                if ti == 0:
                    s_v.wait(v, v_init)
                if ti >= 1:
                    s_s.wait(v, sN[ti - 1])
                v.tensor_scalar(i16[:, 0:2048], gt[ti % 2], z.scol[:, 4:5],
                                None, ALU.mult)
                v.drain()
                s_v.inc(v.tensor_scalar(gq, i16[:, 0:2048], z.scol[:, 5:6],
                                        None, ALU.mult))
                v.drain()
            self.VE(v1)
            v_gq = s_v.n

            def s1(sc, ti=ti, v_gq=v_gq):
                s_v.wait(sc, v_gq)
                sc.activation(sig, gq, ACT.Sigmoid)
                s_s.inc(sc.drain())
            self.SE(s1)
            sN[ti] = s_s.n

            def v2(v, ti=ti, gate=spN.get(ti - 2, 0)):
                s_s.wait(v, sN[ti])
                if gate:
                    spsem[ti % 2].wait(v, gate)
                v.tensor_scalar(u16[:, 0:2048], sig, 65536.0, None, ALU.mult)
                v.drain()
                v.scalar_tensor_tensor(g2[ti % 2], u16[:, 0:2048], 1.0 / 65536.0,
                                       gq, ALU.mult, ALU.mult)
                v.drain()
                v.tensor_reduce(z.wsc[:, 4:5], g2[ti % 2], AX.X, ALU.max,
                                apply_absolute_value=True)
                v.drain()
                s_v.inc(v.tensor_tensor(z.wsc[:, 6:7], z.wsc[:, 6:7],
                                        z.wsc[:, 4:5], ALU.max))
                v.drain()
            self.VE(v2)
            vG[ti] = vN[ti] = s_v.n

            def gsp(g, ti=ti):
                s_v.wait(g, vG[ti])
                spsem[ti % 2].inc(g.dma_start(out=z.d_g2[ti], in_=g2[ti % 2]), 16)
            self.GE(gsp)
            spN[ti] = spsem[ti % 2].n

        self.fold_max([6], 2, 0)

    def ph_ffn3(self):
        z = self
        s_v, s_s, s_t, s_d = z.s_v, z.s_s, z.s_t, z.s_d
        i16 = z.dws[:].bitcast(I16)
        i16b = z.otm[:].bitcast(I16)
        gate2 = z.aux2[:, 0:2048]
        upq = z.rep[:]
        g2b = [z.qkv[:, i * 2048:(i + 1) * 2048] for i in range(2)]
        upb = [z.qkv[:, 4096 + i * 2048:4096 + (i + 1) * 2048] for i in range(2)]
        opre = [z.qkv[:, 8192 + i * 2048:8192 + (i + 1) * 2048] for i in range(2)]
        wsem = [z.s_w1, z.s_w2a]
        wsem2 = [z.s_w2b, z.s_lw]
        spsem = [z.s_rlA, z.s_rlB]
        d_sc2 = self.ffn_scales(2, [6, 7], 1)

        def vi(v):
            s_d.wait(v, d_sc2)
            s_v.inc(v.memset(z.wsc[:, 8:9], 0.0))
            v.drain()
        self.VE(vi)
        v_init = s_v.n

        vN = {}
        spN = {}
        for ti in range(24):
            ws = wsem[ti % 2]
            ws2 = wsem2[ti % 2]

            def gw(g, ti=ti, ws=ws, ws2=ws2, gate=vN.get(ti - 2, 0)):
                if gate:
                    s_v.wait(g, gate)
                ws.inc(g.dma_start(out=g2b[ti % 2], in_=z.d_g2[ti]), 16)
                ws2.inc(g.dma_start(out=upb[ti % 2], in_=z.d_up[ti]), 16)
            self.GE(gw)
            w_ld = ws.n
            w_ld2 = ws2.n

            def v1(v, ti=ti, ws=ws, ws2=ws2, w_ld=w_ld, w_ld2=w_ld2,
                   gate=spN.get(ti - 2, 0)):
                ws.wait(v, w_ld)
                ws2.wait(v, w_ld2)
                if ti == 0:
                    s_v.wait(v, v_init)
                if gate:
                    spsem[ti % 2].wait(v, gate)
                v.tensor_scalar(i16[:, 0:2048], g2b[ti % 2], z.scol[:, 6:7],
                                None, ALU.mult)
                v.tensor_scalar(i16b[:, 0:2048], upb[ti % 2], z.scol[:, 2:3],
                                None, ALU.mult)
                v.drain()
                v.tensor_scalar(gate2, i16[:, 0:2048], z.scol[:, 7:8], None, ALU.mult)
                v.tensor_scalar(upq, i16b[:, 0:2048], z.scol[:, 3:4], None, ALU.mult)
                v.drain()
                v.tensor_mul(opre[ti % 2], gate2, upq)
                v.drain()
                v.tensor_reduce(z.wsc[:, 4:5], opre[ti % 2], AX.X, ALU.max,
                                apply_absolute_value=True)
                v.drain()
                s_v.inc(v.tensor_tensor(z.wsc[:, 8:9], z.wsc[:, 8:9],
                                        z.wsc[:, 4:5], ALU.max))
                v.drain()
            self.VE(v1)
            vN[ti] = s_v.n

            def gsp(g, ti=ti):
                s_v.wait(g, vN[ti])
                spsem[ti % 2].inc(g.dma_start(out=z.d_opre[ti], in_=opre[ti % 2]), 16)
            self.GE(gsp)
            spN[ti] = spsem[ti % 2].n

        self.fold_max([8], 3, 0)

    def ph_ffn4(self):
        z = self
        s_v, s_s, s_t, s_d = z.s_v, z.s_s, z.s_t, z.s_d
        d_sc3 = self.ffn_scales(3, [8, 9], 1)
        wnat3 = z.bigb[:, 3072:6144]
        wdq3 = z.bigb[:, 0:3072]
        wi83 = z.dws[:].bitcast(I8)[:, 0:3072]
        bm96 = z.aux2[:, 2048:2144]
        rb96 = z.aux2[:, 2144:2240]
        i16 = z.dws[:].bitcast(I16)

        vW = {}
        tW = {}
        sW = {}
        for mc in range(NCH):
            def gw(g, mc=mc, gate=vW.get(mc - 1, 0)):
                if gate:
                    s_v.wait(g, gate)
                z.s_lb.inc(g.dma_start(out=wnat3,
                                       in_=z.d_wdn[mc * 128:(mc + 1) * 128, :]), 16)
            self.GE(gw)
            w_ld = z.s_lb.n

            def vq(v, mc=mc, w_ld=w_ld, gate=tW.get(mc - 1, 0)):
                z.s_lb.wait(v, w_ld)
                if gate:
                    s_t.wait(v, gate)
                wn3 = wnat3.rearrange("p (b f) -> p b f", f=32)
                v.tensor_reduce(bm96, wn3, AX.X, ALU.max, apply_absolute_value=True)
                v.drain()
                v.tensor_scalar(bm96, bm96, 1.0 / 7.0, 1e-12, ALU.mult, ALU.max)
                v.drain()
                v.reciprocal(rb96, bm96)
                v.drain()
                v.tensor_tensor(wi83[:].rearrange("p (b f) -> p b f", f=32), wn3,
                                rb96.unsqueeze(2).broadcast_to([128, 96, 32]), ALU.mult)
                v.drain()
                v.tensor_tensor(wdq3.rearrange("p (b f) -> p b f", f=32),
                                wi83[:].rearrange("p (b f) -> p b f", f=32),
                                bm96.unsqueeze(2).broadcast_to([128, 96, 32]), ALU.mult)
                s_v.inc(v.drain())
            self.VE(vq)
            vW[mc] = s_v.n

            for grp in range(3):
                def tt(t, mc=mc, grp=grp, gate=sW.get((mc, grp - 1), sW.get((mc - 1, 2), 0))):
                    s_v.wait(t, vW[mc])
                    if gate:
                        s_s.wait(t, gate)
                    for j in range(8):
                        tcol = grp * 8 + j
                        t.transpose(z.pt[0][:, j * 128:(j + 1) * 128],
                                    wdq3[:, tcol * 128:(tcol + 1) * 128],
                                    z.consts[:, 0:128])
                    s_t.inc(t.drain())
                self.TE(tt)
                tW[mc] = s_t.n
                t_g = s_t.n

                def se_(sc, mc=mc, grp=grp, t_g=t_g):
                    s_t.wait(sc, t_g)
                    sc.activation(z.wpool[:, (mc * 24 + grp * 8) * 128:
                                          (mc * 24 + grp * 8 + 8) * 128],
                                  z.pt[0][:, 0:1024], ACT.Copy)
                    s_s.inc(sc.drain())
                self.SE(se_)
                sW[(mc, grp)] = s_s.n
        s_wdn = s_s.n

        ost = z.qkv[:]
        oldN = {}
        vqd = {}
        tDm = {}
        vAd = {i: 0 for i in range(-4, 0)}
        for sl in range(NSLAB):
            osem = [z.s_w1, z.s_w2a][sl % 2]

            def go(g, sl=sl, osem=osem, gate=tDm.get(sl - 1, 0)):
                if gate:
                    s_t.wait(g, gate)
                if sl == 0:
                    s_d.wait(g, d_sc3)
                for t_ in range(NRB):
                    osem.inc(g.dma_start(
                        out=ost[:, t_ * 512:(t_ + 1) * 512],
                        in_=z.d_opre[t_][:, sl * SLAB:(sl + 1) * SLAB]), 16)
            self.GE(go)
            oldN[sl] = osem.n

            def vo(v, sl=sl, osem=osem, gate=vW.get(7)):
                osem.wait(v, oldN[sl])
                s_v.wait(v, gate)
                if sl == 0:
                    s_d.wait(v, d_sc3)
                for t_ in range(NRB):
                    v.tensor_scalar(i16[:, 0:512],
                                    ost[:, t_ * 512:(t_ + 1) * 512],
                                    z.scol[:, 8:9], None, ALU.mult)
                    v.drain()
                    v.tensor_scalar(ost[:, t_ * 512:(t_ + 1) * 512],
                                    i16[:, 0:512], z.scol[:, 9:10], None, ALU.mult)
                    v.drain()
                s_v.inc(v.drain())
            self.VE(vo)
            vqd[sl] = s_v.n

            for mc in range(NCH):
                def tdm(t, sl=sl, mc=mc, gate=vAd[(sl * NCH + mc) - 4]):
                    s_v.wait(t, max(vqd[sl], gate))
                    if sl == 0 and mc == 0:
                        s_s.wait(t, s_wdn)
                    for t_ in range(NRB):
                        last = t.matmul(z.pm[mc % 4][:, 0:512],
                                        z.wpool[:, (mc * 24 + t_) * 128:
                                                (mc * 24 + t_ + 1) * 128],
                                        ost[:, t_ * 512:(t_ + 1) * 512],
                                        start=(t_ == 0), stop=(t_ == NRB - 1))
                    s_t.inc(last)
                self.TE(tdm)
                tDm[sl] = s_t.n
                t_mm = s_t.n

                def va(v, sl=sl, mc=mc, t_mm=t_mm):
                    s_t.wait(v, t_mm)
                    xt_ = z.xs[:, mc * T + sl * SLAB:mc * T + (sl + 1) * SLAB]
                    s_v.inc(v.tensor_add(xt_, xt_, z.pm[mc % 4][:, 0:512]))
                    v.drain()
                self.VE(va)
                vAd[sl * NCH + mc] = s_v.n
        z.ffn_done_v = s_v.n

    def ph_out(self):
        z = self

        def g(gg):
            for ss in (z.s_v, z.s_s, z.s_t, z.s_g, z.s_ld, z.s_lw, z.s_lb, z.s_d,
                       z.s_cc, z.s_w1, z.s_w2a, z.s_w2b, z.s_rlA, z.s_rlB, z.s_ch,
                       z.s_cm0, z.s_cm1, z.s_cm2, z.s_cm3):
                ss.wait(gg)
            for c in range(NCH):
                z.s_st.inc(gg.dma_start(out=z.d_y[c * 128:(c + 1) * 128, :],
                                        in_=z.xs[:, c * T:(c + 1) * T]), 16)
            z.s_st.wait(gg)
        self.GE(g)


def build(dbg=None):
    b = Builder(dbg)
    b.declare()
    b.ph_consts_loads()
    if b.dump_fm("x0", b.xs):
        b.es.close(); return b.nc
    b.src_stride, b.src_off = T, 0
    b.dst_stride, b.dst_off = b.TE_, 32
    xn_ready = b.emit_ln(b.xs, b.xn, b.x_loaded)
    if b.dump_fm("xn", b.xn, stride=b.TE_, off=32):
        b.es.close(); return b.nc
    b.ph_l0(xn_ready)
    if b.dbg not in ("khat", "oloc"):
        b.ph_moe()
        if b.dump_fm("xmoe", b.xs):
            b.es.close(); return b.nc
        b.ph_attn_fin()
        if b.dump_fm("x1", b.xs):
            b.es.close(); return b.nc
        b.ph_l1_prep()
        if b.dump_fm("xn2", b.xn, stride=b.TE_, off=32):
            b.es.close(); return b.nc
        if b.dump_raw("kern", [b.rep[:, 0:512]]):
            b.es.close(); return b.nc
        b.ph_conv()
        if b.dump_fm("xconv", b.xs):
            b.es.close(); return b.nc
        if b.dbg in ("f1a", "f1b"):
            b.ph_ffn1(ntiles=(0 if b.dbg == "f1a" else 2), do_fold=False)
            b.dump_fm(b.dbg, b.xs)
            b.es.close(); return b.nc
        b.ph_ffn1()
        if b.dump_fm("f1", b.xs):
            b.es.close(); return b.nc
        b.ph_ffn2()
        if b.dump_fm("f2", b.xs):
            b.es.close(); return b.nc
        b.ph_ffn3()
        if b.dump_fm("f3", b.xs):
            b.es.close(); return b.nc
        b.ph_ffn4()
        for tag, dsrc in (("fup", b.d_up), ("fgt", b.d_gt), ("fg2", b.d_g2),
                          ("fop", b.d_opre)):
            if b.dbg == tag and not b.dbg_done:
                b.dbg_done = True
                z = b
                def gdd(gg, dsrc=dsrc):
                    for ss in (z.s_v, z.s_s, z.s_t, z.s_g, z.s_ld, z.s_lw, z.s_lb,
                               z.s_d, z.s_cc, z.s_w1, z.s_w2a, z.s_w2b, z.s_rlA,
                               z.s_rlB, z.s_ch, z.s_cm0, z.s_cm1, z.s_cm2, z.s_cm3):
                        ss.wait(gg)
                    for i in range(NCH):
                        z.s_st.inc(gg.dma_start(out=z.d_dbg[i * 128:(i + 1) * 128, :],
                                                in_=dsrc[i]), 16)
                    for c in range(NCH):
                        z.s_st.inc(gg.dma_start(out=z.d_y[c * 128:(c + 1) * 128, :],
                                                in_=z.xs[:, c * T:(c + 1) * T]), 16)
                    z.s_st.wait(gg)
                b.GE(gdd)
                b.es.close(); return b.nc
        if b.dump_raw("fo", [b.qkv[:, i * 2048:(i + 1) * 2048] for i in range(6)]):
            b.es.close(); return b.nc
        if b.dump_raw("fsc", [b.scol[:], b.maxc[:], b.wsc[:], b.aux2[:, 3072:4096]]):
            b.es.close(); return b.nc
        if b.dump_raw("fw", [b.wpool[:, i * 2048:(i + 1) * 2048] for i in range(8)]):
            b.es.close(); return b.nc
        if b.dump_fm("x2", b.xs):
            b.es.close(); return b.nc
        b.ph_out()
    if b.dbg in ("khat", "oloc"):
        z = b
        def gdump(gg):
            for ss in (z.s_v, z.s_s, z.s_t, z.s_g, z.s_ld, z.s_lw, z.s_lb, z.s_d, z.s_cc,
                       z.s_w1, z.s_w2a, z.s_w2b, z.s_rlA, z.s_rlB, z.s_ch,
                       z.s_cm0, z.s_cm1, z.s_cm2, z.s_cm3):
                ss.wait(gg)
            flat = z.d_dbg.rearrange("a b -> (a b)")
            if b.dbg == "khat":
                for blk in range(6):
                    z.s_st.inc(gg.dma_start(
                        out=z.d_dbg[blk * 128:(blk + 1) * 128, :],
                        in_=z.qkv[:, blk * 2048:(blk + 1) * 2048]), 16)
            else:
                for cc in range(NCHUNK):
                    z.s_st.inc(gg.dma_start(
                        out=flat[cc * 131072:(cc + 1) * 131072],
                        in_=z.d_oloc[cc].rearrange("p f -> (p f)")), 16)
            for c in range(NCH):
                z.s_st.inc(gg.dma_start(out=z.d_y[c * 128:(c + 1) * 128, :],
                                        in_=z.xs[:, c * T:(c + 1) * T]), 16)
            z.s_st.wait(gg)
        b.GE(gdump)
        b.dbg_done = True
        b.es.close(); return b.nc
    b.es.close()
    return b.nc


def shard_inputs(inp):
    x = np.asarray(inp["L0_attn_x"])
    w1 = np.ascontiguousarray(
        np.asarray(inp["L0_ffn_W1"]).reshape(E, 8, 128, 16, 128)
        .transpose(0, 3, 2, 1, 4).reshape(E, 16, 128, 1024))
    w2 = np.ascontiguousarray(
        np.asarray(inp["L0_ffn_W2"]).reshape(E, 16, 128, 8, 128)
        .transpose(0, 3, 2, 1, 4).reshape(E, 8, 128, 2048))
    b1 = np.ascontiguousarray(
        np.asarray(inp["L0_ffn_b1"]).reshape(E, 16, 128).transpose(2, 0, 1).reshape(128, E * 16))
    common = {
        "wq": np.asarray(inp["L0_attn_Wq"]), "wk": np.asarray(inp["L0_attn_Wk"]),
        "wv": np.asarray(inp["L0_attn_Wv"]), "wo": np.asarray(inp["L0_attn_Wo"]),
        "wgam": np.asarray(inp["L0_attn_Wgamma"]), "wf": np.asarray(inp["L0_attn_Wf"]),
        "wg1": np.asarray(inp["L0_attn_Wg1"]), "wg2": np.asarray(inp["L0_attn_Wg2"]),
        "normw": np.ascontiguousarray(np.asarray(inp["L0_attn_norm_w"]).reshape(NCH, 128).T),
        "mwg": np.asarray(inp["L0_ffn_Wg"]), "mw1": w1, "mb1": b1,
        "mw2": w2, "mb2": np.asarray(inp["L0_ffn_b2"]),
        "acol": np.asarray(inp["L1_attn_A"]).reshape(64, 1),
        "bmat": np.asarray(inp["L1_attn_B"]),
        "cmat": np.asarray(inp["L1_attn_C"]),
        "dvec": np.ascontiguousarray(np.asarray(inp["L1_attn_D"]).reshape(NCH, 128).T),
        "ldt": np.asarray(inp["L1_attn_log_dt"]).reshape(1, D),
        "wup": np.asarray(inp["L1_ffn_w_up"]), "wgt": np.asarray(inp["L1_ffn_w_gate"]),
        "wdn": np.asarray(inp["L1_ffn_w_down"]),
    }
    maps = []
    for k in range(8):
        bb, h = k // 2, k % 2
        m = dict(common)
        m["x"] = np.ascontiguousarray(x[bb, h * T:(h + 1) * T, :].T)
        m["flag2"] = np.full((128, 1), float(h), np.float32)
        maps.append(m)
    return maps


def run(dbg=None, inputs=None):
    nc = build(dbg=dbg)
    maps = shard_inputs(inputs)
    return run_bass_kernel_spmd(nc, maps, core_ids=list(range(8)))


def kernel(**inputs):
    nc = build(dbg=None)
    maps = shard_inputs(inputs)
    res = run_bass_kernel_spmd(nc, maps, core_ids=list(range(8)))
    out = np.zeros((4, 4096, D), np.float32)
    for k in range(8):
        bb, h = k // 2, k % 2
        out[bb, h * T:(h + 1) * T, :] = res.results[k]["y"].T
    return out
